# version 09
# baseline (speedup 1.0000x reference)
import contextlib
import numpy as np
import ml_dtypes
import concourse.bass as bass
import concourse.mybir as mybir
from concourse.bass_utils import run_bass_kernel_spmd

F32 = mybir.dt.float32
BF16 = mybir.dt.bfloat16
AF = mybir.ActivationFunctionType
ALU = mybir.AluOpType

B, T, D = 8, 4096, 1024
NQ = 4
TQ = T // NQ
FC = D // 128
EPS = 1e-5
SCALE = 1.0 / 512.0
RINV = 1.0 / 1024.0

V_LNW, V_LNB = 0, 1
V_EQB, V_MEBV, V_MEBO, V_C2B = 2, 3, 4, 5
(V_SYNB, V_CEBV, V_CEBO, V_N1G, V_N1B, V_F2B, V_N2G, V_N2B, V_GENB,
 V_MLB2) = 6, 7, 8, 9, 10, 11, 12, 13, 14, 15
V_BQ, V_BK, V_BV = 16, 17, 18
V_C1B0 = 19
V_F1B0 = 21
V_MLB1 = 23

DC_CAPL, DC_CBPL, DC_GA, DC_GB, DC_GW, DC_GB2 = 0, 8, 16, 24, 32, 40

ENGS = ("tensor", "vector", "scalar", "gpsimd", "sync")
LAST_MARKS = []
LAST_LOG = []


class Plan:

    def __init__(self, nc):
        self.nc = nc
        self.ops = {e: [] for e in ENGS}
        self.cnt = {}
        self.seen = {e: {} for e in ENGS}
        self.handles = {}
        self.marks = []
        self.mm_count = [0]
        self.log = []

    def mark(self, label):
        def go(e, _l=label, _p=self):
            _p.marks.append((_p.mm_count[0], _l))
        self.ops["tensor"].append(go)

    def sem(self, name):
        if name not in self.cnt:
            self.cnt[name] = 0
        return name

    def handle(self, name):
        return self.handles[name]

    def _bump(self, name, k):
        self.sem(name)
        self.cnt[name] += k
        return (name, self.cnt[name])

    def tok(self, eng):
        return (eng + "prog", self.cnt.get(eng + "prog", 0))

    def wait(self, eng, toks):
        best = {}
        for t in toks:
            if t is None:
                continue
            n, v = t
            if v > 0 and v > best.get(n, 0):
                best[n] = v
        todo = []
        for n, v in sorted(best.items()):
            if self.seen[eng].get(n, 0) < v:
                self.sem(n)
                todo.append((n, v))
                self.seen[eng][n] = v
        if todo:
            self.log.append(("wait", eng, tuple(todo)))

            def go(e, _todo=tuple(todo), _p=self):
                for n, v in _todo:
                    e.wait_ge(_p.handle(n), v)
            self.ops[eng].append(go)

    def op(self, eng, fn_name, *args, waits=(), inc=None, inc_k=1, **kw):
        self.wait(eng, list(waits))
        sem_name = inc if inc is not None else eng + "prog"
        tok = self._bump(sem_name, inc_k)
        aps = [a for a in args if hasattr(a, "ap") and hasattr(a, "tensor")]
        wr, rd = [], []
        if "out" in kw:
            wr.append(kw["out"])
        if "in_" in kw:
            rd.append(kw["in_"])
        if aps:
            wr.append(aps[0])
            rd.extend(aps[1:])
        for k2 in ("bias", "scale"):
            if k2 in kw and hasattr(kw[k2], "ap"):
                rd.append(kw[k2])
        self.log.append(("op", eng, tuple(rd), tuple(wr), ((sem_name, tok[1]),),
                         fn_name))

        def go(e, _fn=fn_name, _args=args, _kw=kw, _sem=sem_name, _k=inc_k, _p=self):
            ins = getattr(e, _fn)(*_args, **_kw)
            ins.then_inc(_p.handle(_sem), _k)
        self.ops[eng].append(go)
        return tok

    def rec(self, eng, reads=(), writes=(), incs=(), tag=""):
        self.log.append(("op", eng, tuple(reads), tuple(writes), tuple(incs), tag))

    def barrier(self, engs=("tensor", "vector", "scalar")):
        cur = {e: (e + "prog", self.cnt.get(e + "prog", 0)) for e in engs}
        for e in engs:
            self.wait(e, [cur[o] for o in engs if o != e])

    def emit(self, block):
        class _CountingTensor:
            def __init__(s, eng, p):
                s._e, s._p = eng, p

            def matmul(s, *a, **k):
                s._p.mm_count[0] += 1
                return s._e.matmul(*a, **k)

            def __getattr__(s, n):
                return getattr(s._e, n)

        for eng in ENGS:
            lst = self.ops[eng]
            if not lst:
                continue

            def run(engine, _lst=lst, _eng=eng):
                if _eng == "tensor":
                    engine = _CountingTensor(engine, self)
                for f in _lst:
                    f(engine)
            getattr(block, eng)(run)


class Bank:
    def __init__(self, ap, name):
        self.ap = ap
        self.name = name
        self.free_tok = None


def build_nc(debug=(), pace=()):
    nc = bass.Bass("TRN2", target_bir_lowering=False)
    plan = Plan(nc)
    din = {}

    def dt_in(name, shape, dtype=F32):
        din[name] = nc.dram_tensor(name, shape, dtype, kind="ExternalInput").ap()
        return din[name]

    x_in = dt_in("x_fm", [D, T])
    c_in = dt_in("c_fm", [128, FC])
    w_qkv = dt_in("w_qkv", [D, 3 * D])
    w_mod = dt_in("w_mod", [D, 6 * D])
    w_router = dt_in("w_router", [D, 4])
    rb_in = dt_in("rb", [1, 4])
    lb_in = dt_in("lb", [1, 4])
    sw_w1 = dt_in("sw_w1", [D, 4096])
    sw_w3 = dt_in("sw_w3", [D, 4096])
    sw_w2 = dt_in("sw_w2", [4096, D])
    me_eq = dt_in("me_eq", [D, D])
    me_wv = dt_in("me_wv", [D, D])
    me_wo = dt_in("me_wo", [D, D])
    me_c1 = dt_in("me_c1", [D, 2048])
    me_c2 = dt_in("me_c2", [2048, D])
    ce_syn = dt_in("ce_syn", [D, D])
    ce_wv = dt_in("ce_wv", [D, D])
    ce_wo = dt_in("ce_wo", [D, D])
    ce_f1 = dt_in("ce_f1", [D, 2048])
    ce_f2 = dt_in("ce_f2", [2048, D])
    ce_gen = dt_in("ce_gen", [D, D])
    ml_w1 = dt_in("ml_w1", [D, 4096])
    ml_w2 = dt_in("ml_w2", [4096, D])
    vecs_in = dt_in("vecs_fm", [128, 27, 8])
    bvrow_in = dt_in("bvrow", [1, D])
    ones_in = dt_in("ones_bf", [128, 128], BF16)
    ident_in = dt_in("ident", [128, 128])
    maskA_in = dt_in("maskA", [128, 128])
    maskC_in = dt_in("maskC", [128, 128])
    hot_in = dt_in("hotrows", [32, 4096], BF16)

    y_out = nc.dram_tensor("y_fm", [D, T], F32, kind="ExternalOutput").ap()
    dbg = {}
    for k, shape, dtp in debug:
        dbg[k] = nc.dram_tensor("dbg_" + k, list(shape), dtp, kind="ExternalOutput").ap()

    es = contextlib.ExitStack()

    def sbt(name, shape, dtype=F32):
        return es.enter_context(nc.sbuf_tensor(name, shape, dtype))

    with es:
        xres = sbt("xres", [128, FC, TQ], F32)
        hbuf = sbt("hbuf", [128, FC, TQ], BF16)
        work = sbt("work", [128, 32768], BF16)
        wring = sbt("wring", [128, 20480], BF16)
        mstat = sbt("mstat", [128, 3, 1024], F32)
        statbf = sbt("statbf", [128, 2, 1024], BF16)
        xbsq = sbt("xbsq", [128, 4, 1024], BF16)
        wbg = sbt("wbg", [128, 1024], F32)
        gtmp = sbt("gtmp", [128, 512], F32)
        halo = sbt("halo", [128, 4096], BF16)
        stmp = sbt("stmp", [128, 3, 512], BF16)
        vecs = sbt("vecs_s", [128, 27, 8], F32)
        modc = sbt("modc", [128, 48], F32)
        dcols = sbt("dcols", [128, 64], F32)
        ones = sbt("ones_s", [128, 128], BF16)
        ident = sbt("ident_s", [128, 128], F32)
        maskA = sbt("maskA_s", [128, 128], F32)
        maskC = sbt("maskC_s", [128, 128], F32)
        hot = sbt("hot_s", [32, 4096], BF16)
        w32 = sbt("w32", [32, 128], BF16)
        wtm = sbt("wtm", [128, 32], F32)
        wq = sbt("wq", [128, 32], F32)
        rmath = sbt("rmath", [128, 12, 8], F32)
        rblr = sbt("rblr", [128, 32], F32)
        bvrows = sbt("bvrows", [128, D], BF16)
        epsc = sbt("epsc", [128, 1], F32)
        cfm = sbt("cfm", [128, FC], F32)
        silc = sbt("silc", [128, FC], BF16)
        rrow = mstat[0:1, 1, 0:8]
        rrbf = statbf[0:1, 1, 0:8]
        bvrow = mstat[0:1, 0, 0:1024]
        bvrbf = statbf[0:1, 0, 0:1024]

        pA = [es.enter_context(nc.psum_tensor(f"pA{i}", [128, 512], F32)) for i in range(4)]
        pO = [es.enter_context(nc.psum_tensor(f"pO{i}", [128, 1024], F32)) for i in range(2)]
        bankA = [Bank(pA[i][:], f"A{i}") for i in range(4)]
        bankO = [Bank(pO[i][:], f"O{i}") for i in range(2)]

        rrA = [0]

        def takeA():
            b = bankA[rrA[0] % 4]
            rrA[0] += 1
            return b

        rrO = [0]

        def takeO():
            b = bankO[rrO[0] % 2]
            rrO[0] += 1
            return b

        def mm_acc(bank_ap, pairs, waits=(), flags=None):
            plan.wait("tensor", list(waits))
            tok = plan._bump("tensorprog", 1)
            plan.rec("tensor", reads=[a for pr in pairs for a in pr],
                     writes=[bank_ap], incs=[("tensorprog", tok[1])], tag="mm")
            n = len(pairs)
            if flags is None:
                flags = [(i == 0, i == n - 1) for i in range(n)]

            def go(t, _ap=bank_ap, _pairs=tuple(pairs), _flags=tuple(flags), _p=plan):
                ins = None
                for (lh, rh), (st, sp) in zip(_pairs, _flags):
                    ins = t.matmul(_ap, lh, rh, start=st, stop=sp)
                ins.then_inc(_p.handle("tensorprog"), 1)
            plan.ops["tensor"].append(go)
            return tok

        RING_UNITS, UNIT = 10, 2048

        class WRing:
            def __init__(self):
                self.head = 0
                self.loads = []
                self.ld_tok = []
                self.unit_free = [None] * RING_UNITS

            def plan_load(self, src_ap, kc, inner):
                elems = kc * inner
                n_units = (elems + UNIT - 1) // UNIT
                if (self.head % RING_UNITS) + n_units > RING_UNITS:
                    while self.head % RING_UNITS != 0:
                        self.head += 1
                off = self.head % RING_UNITS
                idx = len(self.loads)
                self.loads.append((off, n_units, src_ap, kc, inner))
                self.head += n_units
                waits = [self.unit_free[u] for u in range(off, off + n_units)]
                plan.wait("gpsimd", waits)
                sem = f"wl{idx % 16}"
                tok = plan._bump(sem, 16)
                plan.rec("gpsimd", reads=[],
                         writes=[wring[:, off * UNIT:off * UNIT + kc * inner]],
                         incs=[(sem, tok[1])], tag="wload")

                def go(g, _off=off, _src=src_ap, _kc=kc, _in=inner, _p=plan,
                       _sem=sem):
                    dst = wring[:, _off * UNIT:_off * UNIT + _kc * _in].rearrange(
                        "p (kc n) -> p kc n", kc=_kc)
                    g.dma_start(out=dst, in_=_src).then_inc(_p.handle(_sem), 16)
                plan.ops["gpsimd"].append(go)
                self.ld_tok.append((sem, tok[1]))
                return idx

            def view(self, idx):
                off, n, _src, kc, inner = self.loads[idx]
                return wring[:, off * UNIT:off * UNIT + kc * inner].rearrange(
                    "p (kc n) -> p kc n", kc=kc)

            def use_tok(self, idx):
                return self.ld_tok[idx]

            def release(self, idx, tok):
                off, n, _src, _kc, _in = self.loads[idx]
                for u in range(off, off + n):
                    self.unit_free[u] = tok

        ring = WRing()
        plan.sem("wload")

        def w_src(w_ap, r0, r1, c0, c1):
            return w_ap[r0:r1, c0:c1].rearrange("(kc p) n -> p kc n", p=128)

        def load_cols(w_ap, c0, c1):
            return ring.plan_load(w_src(w_ap, 0, D, c0, c1), 8, c1 - c0)

        def load_rows(w_ap, r0, r1):
            return ring.plan_load(w_src(w_ap, r0, r1, 0, D), (r1 - r0) // 128, D)

        nconst = [0]

        def const_dma(dst, src):
            plan.op("sync", "dma_start", out=dst, in_=src, inc="cload", inc_k=16)
            nconst[0] += 1

        const_dma(vecs[:], vecs_in[:])
        const_dma(ones[:], ones_in[:])
        const_dma(ident[:], ident_in[:])
        const_dma(maskA[:], maskA_in[:])
        const_dma(maskC[:], maskC_in[:])
        const_dma(hot[:], hot_in[:])
        const_dma(cfm[:], c_in[:])
        const_dma(rrow[:, 0:4], rb_in[:])
        const_dma(rrow[:, 4:8], lb_in[:])
        const_dma(bvrow[:, :], bvrow_in[:])
        CONST = ("cload", 16 * nconst[0])

        def tap(name, ap, waits):
            if name in dbg:
                plan.op("sync", "dma_start", out=dbg[name][:], in_=ap,
                        waits=list(waits) + [CONST], inc="tapst", inc_k=16)
            elif name.split("_q")[0] in pace:
                plan.wait("sync", list(waits))

        eps_tok = plan.op("vector", "memset", epsc[:], EPS)
        sil_tok = plan.op("scalar", "activation", silc[:], cfm[:], AF.Silu,
                          waits=[CONST])

        wmod_ids = []
        ev_toks = []
        for mc in range(48):
            j, off = mc // 4, (mc % 4) * 128
            if mc % 4 == 0:
                wmod_ids.append(load_cols(w_mod, j * 512, (j + 1) * 512))
            lid = wmod_ids[j]
            wv = ring.view(lid)
            bank = takeA()
            pairs = [(wv[:, kc, off:off + 128], silc[:, kc:kc + 1]) for kc in range(8)]
            tok = mm_acc(bank.ap[:, 0:1], pairs,
                         waits=[ring.use_tok(lid), sil_tok, bank.free_tok])
            if mc % 4 == 3:
                ring.release(lid, tok)
            et = plan.op("vector", "tensor_copy", modc[:, mc:mc + 1], bank.ap[:, 0:1],
                         waits=[tok])
            bank.free_tok = et
            ev_toks.append(et)

        plan.op("vector", "tensor_scalar_add", dcols[:, DC_CAPL:DC_CAPL + 8],
                modc[:, 8:16], 1.0, waits=[ev_toks[15]])
        plan.op("vector", "tensor_scalar_add", dcols[:, DC_CBPL:DC_CBPL + 8],
                modc[:, 32:40], 1.0, waits=[ev_toks[39]])
        clip_prev = None
        for gi, (src0, dst) in enumerate(((16, DC_GA), (40, DC_GB))):
            sgn, mag = rmath[:, 2 * gi, :], rmath[:, 2 * gi + 1, :]
            t1 = plan.op("vector", "tensor_single_scalar", sgn, modc[:, src0:src0 + 8],
                         0.0, ALU.is_ge, waits=[ev_toks[-1], clip_prev])
            t2 = plan.op("vector", "tensor_scalar", sgn, sgn, 2.0, 1.0,
                         ALU.mult, ALU.subtract, waits=[t1])
            t3a = plan.op("scalar", "activation", mag, modc[:, src0:src0 + 8],
                          AF.Abs, waits=[ev_toks[-1]])
            t3 = plan.op("vector", "tensor_scalar_max", mag, mag, 0.25, waits=[t3a])
            clip_prev = plan.op("vector", "tensor_mul", dcols[:, dst:dst + 8],
                                sgn, mag, waits=[t2, t3])
        gtok = plan.tok("vector")
        plan.op("vector", "tensor_mul", dcols[:, DC_GW:DC_GW + 8],
                dcols[:, DC_GA:DC_GA + 8], vecs[:, V_LNW, :],
                waits=[gtok, CONST])
        plan.op("vector", "tensor_mul", dcols[:, DC_GB2:DC_GB2 + 8],
                dcols[:, DC_GA:DC_GA + 8], vecs[:, V_LNB, :])
        GATES = plan.tok("vector")
        tap("modc", modc[:], [GATES])
        tap("dcols", dcols[:], [GATES])

        ct = plan.op("vector", "tensor_copy", rrbf[:, :], rrow[:, :], waits=[CONST])
        bank = takeA()
        rtok = mm_acc(bank.ap[:, 0:4],
                      [(ones[0:1, :], rrbf[:, 0:4]), (ones[0:1, :], rrbf[:, 4:8])],
                      waits=[ct, bank.free_tok, CONST])
        prev = plan.op("vector", "tensor_copy", rblr[:, 0:4], bank.ap[:, 0:4], waits=[rtok])
        bank.free_tok = prev
        for t8 in range(1, 8):
            prev = plan.op("vector", "tensor_copy", rblr[:, t8 * 4:(t8 + 1) * 4],
                           rblr[:, 0:4], waits=[prev])
        RBL = prev

        cb = plan.op("vector", "tensor_copy", bvrbf[:, :], bvrow[:, :], waits=[CONST])
        for h in range(2):
            bank = takeA()
            tok = mm_acc(bank.ap[:], [(ones[0:1, :], bvrbf[:, h * 512:(h + 1) * 512])],
                         waits=[cb, bank.free_tok])
            ftok = plan.op("vector", "tensor_copy", bvrows[:, h * 512:(h + 1) * 512],
                           bank.ap[:], waits=[tok])
            bank.free_tok = ftok
        BVROWS = plan.tok("vector")

        kslots = work[:, 0:8192].rearrange("p (s e) -> p s e", e=1024)
        vslots = work[:, 8192:16384].rearrange("p (s e) -> p s e", e=1024)
        zq = work[:, 16384:24576].rearrange("p (dc t) -> p dc t", t=1024)
        zst = [work[:, 24576 + r * 384:24576 + (r + 1) * 384].rearrange(
            "p (i t) -> p i t", t=128) for r in range(2)]
        osb = work[:, 25600:29696].rearrange("p (dc t) -> p dc t", t=512)
        kh = halo[:, 0:2048].rearrange("p (s e) -> p s e", e=1024)
        vh = halo[:, 2048:4096].rearrange("p (s e) -> p s e", e=1024)
        uview = work[:, 0:32768].rearrange("p (hf t) -> p hf t", t=1024)

        def chain(base, nfc=8):
            return work[:, base:base + nfc * 1024].rearrange("p (fc t) -> p fc t", t=1024)

        lnwar = [None]

        def ln(width, csl, src_fn, out_fn, pre_toks=()):
            nh = width // 512
            if width == 1024:
                s1b, s2b = takeO(), takeO()
            else:
                s1b, s2b = takeA(), takeA()
            s1, s2 = s1b.ap, s2b.ap
            seg_t = {}
            last1 = last2 = None
            for fc in range(FC):
                slot = fc % 2
                ap, is_f32 = src_fn(fc)
                ap = ap[:, csl]
                wl = list(pre_toks) if fc < 2 else [seg_t[fc - 2]]
                if is_f32:
                    c1 = plan.op("vector", "tensor_copy", xbsq[:, slot, 0:width], ap,
                                 waits=wl)
                    src_bf = xbsq[:, slot, 0:width]
                    c2 = plan.op("vector", "tensor_mul", xbsq[:, 2 + slot, 0:width],
                                 src_bf, src_bf, waits=[c1])
                else:
                    src_bf = ap
                    c2 = plan.op("vector", "tensor_mul", xbsq[:, 2 + slot, 0:width],
                                 ap, ap, waits=wl)
                w0 = [c2] + ([s1b.free_tok, s2b.free_tok] if fc == 0 else [])
                for h in range(nh):
                    hs = slice(h * 512, (h + 1) * 512)
                    last1 = mm_acc(s1[:, hs], [(ones[:], src_bf[:, hs])], waits=w0,
                                   flags=[(fc == 0, fc == FC - 1)])
                    w0 = []
                    last2 = mm_acc(s2[:, hs], [(ones[:], xbsq[:, 2 + slot, hs])],
                                   flags=[(fc == 0, fc == FC - 1)])
                seg_t[fc] = last2
            m = mstat[:, 0, 0:width]
            qq = mstat[:, 1, 0:width]
            tp = mstat[:, 2, 0:width]
            a1 = plan.op("scalar", "activation", m, s1[:, 0:width], AF.Copy,
                         scale=RINV, waits=[last1, lnwar[0]])
            a2 = plan.op("scalar", "activation", qq, s2[:, 0:width], AF.Copy,
                         scale=RINV, waits=[last2])
            s1b.free_tok = a1
            s2b.free_tok = a2
            v1 = plan.op("vector", "tensor_mul", tp, m, m, waits=[a1])
            v2 = plan.op("vector", "tensor_tensor", tp, qq, tp, ALU.subtract,
                         waits=[v1, a2])
            l1 = plan.op("scalar", "activation", tp, tp, AF.Ln, bias=epsc[:, 0:1],
                         waits=[v2, eps_tok])
            l2 = plan.op("scalar", "activation", tp, tp, AF.Exp, scale=-0.5,
                         waits=[l1])
            mb = plan.op("vector", "tensor_copy", statbf[:, 0, 0:width], m, waits=[a1])
            rb_ = plan.op("vector", "tensor_copy", statbf[:, 1, 0:width], tp, waits=[l2])
            outs = []
            prev_out = None
            for fc in range(FC):
                ap, is_f32 = src_fn(fc)
                ap = ap[:, csl]
                slot = fc % 2
                wl = [mb, rb_]
                if fc >= 2:
                    wl.append(outs[fc - 2])
                if is_f32:
                    tt = plan.op("vector", "tensor_tensor", xbsq[:, slot, 0:width],
                                 ap, m, ALU.subtract, waits=wl)
                else:
                    tt = plan.op("vector", "tensor_tensor", xbsq[:, slot, 0:width],
                                 ap, statbf[:, 0, 0:width], ALU.subtract, waits=wl)
                t2 = plan.op("vector", "tensor_mul", xbsq[:, slot, 0:width],
                             xbsq[:, slot, 0:width], statbf[:, 1, 0:width],
                             waits=[tt])
                prev_out = out_fn(fc, xbsq[:, slot, 0:width], t2, prev_out)
                outs.append(prev_out)
            lnwar[0] = plan.tok("vector")
            return outs

        prev_yhalf = [None, None]
        for q in range(NQ):
            xin_h = []
            for n in range(2):
                nsl = slice(n * 512, (n + 1) * 512)
                xin_h.append(plan.op(
                    "sync", "dma_start",
                    out=xres[:, :, nsl],
                    in_=x_in[:, q * TQ + n * 512:q * TQ + (n + 1) * 512].rearrange(
                        "(fc p) t -> p fc t", p=128),
                    waits=([prev_yhalf[n]] if prev_yhalf[n] else []),
                    inc=f"xload{n}", inc_k=16))

            def src_x(fc):
                return (xres[:, fc, :], True)

            plan.mark("LN1")
            h1tok = [None, None]
            for chnk in range(2):
                csl = slice(chnk * 512, (chnk + 1) * 512)

                def out_h1(fc, tile, tok, prev, _csl=csl):
                    return plan.op("vector", "tensor_scalar", hbuf[:, fc, _csl],
                                   tile,
                                   dcols[:, DC_CAPL + fc:DC_CAPL + fc + 1],
                                   modc[:, 0 + fc:0 + fc + 1],
                                   ALU.mult, ALU.add, waits=[tok, GATES])

                ln(512, csl, src_x, out_h1, pre_toks=[xin_h[chnk], GATES])
                h1tok[chnk] = plan.tok("vector")
            if q == 0:
                tap("h1", hbuf[:, :, :], [plan.tok("vector")])

            plan.mark("qkv")
            for j in range(6):
                lid = load_cols(w_qkv, j * 512, (j + 1) * 512)
                wv = ring.view(lid)
                if j < 2:
                    for i in range(4):
                        dc = j * 4 + i
                        for ch in range(2):
                            tsl = slice(ch * 512, (ch + 1) * 512)
                            bank = takeA()
                            tok = mm_acc(bank.ap[:],
                                         [(wv[:, kc, i * 128:(i + 1) * 128],
                                           hbuf[:, kc, tsl]) for kc in range(8)],
                                         waits=[ring.use_tok(lid), bank.free_tok,
                                                h1tok[ch]])
                            et = plan.op("scalar", "activation", zq[:, dc, tsl],
                                         bank.ap[:], AF.Identity,
                                         bias=vecs[:, V_BQ, dc:dc + 1], waits=[tok])
                            bank.free_tok = et
                elif j < 4:
                    for i in range(4):
                        dc = (j - 2) * 4 + i
                        for ch in range(2):
                            tsl = slice(ch * 512, (ch + 1) * 512)
                            bank = takeA()
                            tok = mm_acc(bank.ap[:],
                                         [(wv[:, kc, i * 128:(i + 1) * 128],
                                           hbuf[:, kc, tsl]) for kc in range(8)],
                                         waits=[ring.use_tok(lid), bank.free_tok,
                                                h1tok[ch]])
                            dst = kslots[:, ch * 4:(ch + 1) * 4,
                                         dc * 128:(dc + 1) * 128]
                            src = bank.ap[:].rearrange("p (s t) -> p s t", t=128)
                            et = plan.op("scalar", "activation", dst, src, AF.Identity,
                                         bias=vecs[:, V_BK, dc:dc + 1], waits=[tok])
                            bank.free_tok = et
                else:
                    dv = j - 4
                    for ch in range(2):
                        for lt in range(4):
                            ttok = slice(ch * 512 + lt * 128, ch * 512 + (lt + 1) * 128)
                            bank = takeA()
                            tok = mm_acc(bank.ap[:],
                                         [(hbuf[:, kc, ttok], wv[:, kc, :])
                                          for kc in range(8)],
                                         waits=[ring.use_tok(lid), bank.free_tok,
                                                h1tok[ch]])
                            et = plan.op("vector", "tensor_add",
                                         vslots[:, ch * 4 + lt,
                                                dv * 512:(dv + 1) * 512],
                                         bank.ap[:], bvrows[:, dv * 512:(dv + 1) * 512],
                                         waits=[tok, BVROWS])
                            bank.free_tok = et
                ring.release(lid, plan.tok("tensor") if j < 4 else plan.tok("vector"))
            plan.barrier()
            if q == 0:
                tap("zq", work[:, 16384:24576], [plan.tok("vector")])
                tap("kv", work[:, 0:16384], [plan.tok("vector")])

            qkv_done = plan.tok("tensor")
            o_mm_tok = [None, None]

            def emit_st(ch, lt):
                QTl = ch * 4 + lt
                QTg = q * 8 + QTl
                pairs_kt = [kt for kt in range(QTg - 2, QTg + 1) if kt >= 0]
                r = QTl % 2
                st_toks = []
                for pi, ktg in enumerate(pairs_kt):
                    if ktg >= q * 8:
                        kv_ = kslots[:, ktg - q * 8, :].rearrange(
                            "p (dc t) -> p dc t", t=128)
                    else:
                        kv_ = kh[:, ktg - (q * 8 - 2), :].rearrange(
                            "p (dc t) -> p dc t", t=128)
                    bank = takeA()
                    tok = mm_acc(bank.ap[:, 0:128],
                                 [(kv_[:, dc, :],
                                   zq[:, dc, ch * 512 + lt * 128:
                                      ch * 512 + (lt + 1) * 128])
                                  for dc in range(8)],
                                 waits=[bank.free_tok])
                    ewl = [tok, o_mm_tok[r]]
                    if ktg == QTg:
                        et = plan.op("vector", "tensor_mul", zst[r][:, pi, :],
                                     bank.ap[:, 0:128], maskC[:], waits=ewl)
                    elif ktg == QTg - 1:
                        et = plan.op("scalar", "activation", zst[r][:, pi, :],
                                     bank.ap[:, 0:128], AF.Copy, scale=SCALE,
                                     waits=ewl)
                    else:
                        et = plan.op("vector", "tensor_mul", zst[r][:, pi, :],
                                     bank.ap[:, 0:128], maskA[:], waits=ewl)
                    bank.free_tok = et
                    st_toks.append(et)
                return pairs_kt, r, st_toks

            def emit_o(ch, lt, pairs_kt, r, st_toks):
                ob = takeO()
                plan.wait("tensor", [ob.free_tok] + st_toks)
                otok = plan._bump("tensorprog", 1)
                plan.rec("tensor",
                         reads=[vslots[:, :, :], vh[:, :, :], zst[r][:, :, :]],
                         writes=[ob.ap[:]], incs=[("tensorprog", otok[1])],
                         tag="o_mm")

                def go_o(t, _pairs=tuple(pairs_kt), _q=q, _r=r, _ap=ob.ap, _p=plan):
                    ins = None
                    for dc in range(8):
                        for pi, ktg in enumerate(_pairs):
                            if ktg >= _q * 8:
                                vv = vslots[:, ktg - _q * 8, :]
                            else:
                                vv = vh[:, ktg - (_q * 8 - 2), :]
                            ins = t.matmul(_ap[:, dc * 128:(dc + 1) * 128],
                                           vv[:, dc * 128:(dc + 1) * 128],
                                           zst[_r][:, pi, :],
                                           start=(pi == 0),
                                           stop=(pi == len(_pairs) - 1))
                    ins.then_inc(_p.handle("tensorprog"), 1)
                plan.ops["tensor"].append(go_o)
                o_mm_tok[r] = ("tensorprog", otok[1])
                et = plan.op("vector", "tensor_copy",
                             osb[:, :, lt * 128:(lt + 1) * 128],
                             ob.ap[:].rearrange("p (dc t) -> p dc t", t=128),
                             waits=[o_mm_tok[r]])
                ob.free_tok = et

            def attn_chunk(ch):
                pend = emit_st(ch, 0)
                for lt in range(4):
                    nxt = emit_st(ch, lt + 1) if lt < 3 else None
                    emit_o(ch, lt, *pend)
                    pend = nxt

            def attn_ln(ch):
                tsl = slice(ch * 512, (ch + 1) * 512)

                def src_o(fc):
                    return (osb[:, fc, :], False)

                def out_attn(fc, tile, tok, prev, _tsl=tsl):
                    u = plan.op("vector", "tensor_scalar", gtmp[:, :], tile,
                                dcols[:, DC_GW + fc:DC_GW + fc + 1],
                                dcols[:, DC_GB2 + fc:DC_GB2 + fc + 1],
                                ALU.mult, ALU.add, waits=[tok, prev, GATES])
                    return plan.op("vector", "tensor_tensor", xres[:, fc, _tsl],
                                   xres[:, fc, _tsl], gtmp[:, :], ALU.add,
                                   waits=[u])

                ln(512, slice(0, 512), src_o, out_attn,
                   pre_toks=[plan.tok("vector")])

            h2tok = [None, None]

            def ln2_chunk(ch):
                csl = slice(ch * 512, (ch + 1) * 512)

                def out_h2(fc, tile, tok, prev, _csl=csl):
                    return plan.op("vector", "tensor_scalar", hbuf[:, fc, _csl],
                                   tile,
                                   dcols[:, DC_CBPL + fc:DC_CBPL + fc + 1],
                                   modc[:, 24 + fc:24 + fc + 1],
                                   ALU.mult, ALU.add, waits=[tok, GATES, qkv_done])

                ln(512, csl, src_x, out_h2, pre_toks=[plan.tok("vector"), GATES])
                h2tok[ch] = plan.tok("vector")

            plan.mark("attn0")
            attn_chunk(0)
            if q == 0:
                tap("osb", work[:, 25600:29696], [plan.tok("vector")])
            plan.mark("attnln0")
            attn_ln(0)
            plan.mark("attn1")
            attn_chunk(1)
            plan.mark("ln2_0")
            ln2_chunk(0)
            plan.mark("attnln1")
            attn_ln(1)
            plan.mark("ln2_1")
            ln2_chunk(1)
            tap(f"x1_q{q}", xres[:, :, :], [plan.tok("vector")])
            tap(f"h2_q{q}", hbuf[:, :, :], [plan.tok("vector")])

            ttk = plan.tok("tensor")
            for i in range(2):
                plan.op("vector", "tensor_copy", kh[:, i, :], kslots[:, 6 + i, :],
                        waits=[ttk])
                plan.op("vector", "tensor_copy", vh[:, i, :], vslots[:, 6 + i, :])

            gate_prev = [plan.tok("vector")]

            def make_wb(e):
                ob = takeO()
                plan.wait("tensor", [ob.free_tok, w32t, CONST])
                tok = plan._bump("tensorprog", 1)
                plan.rec("tensor", reads=[hot[:, :], w32[:, :]], writes=[ob.ap[:]],
                         incs=[("tensorprog", tok[1])], tag="wb_mm")

                def go(t, _e=e, _ap=ob.ap, _p=plan):
                    ins = None
                    for t8 in range(8):
                        qrow = t8 * 4 + _e
                        ins = t.matmul(_ap[:, t8 * 128:(t8 + 1) * 128],
                                       hot[:, qrow * 128:(qrow + 1) * 128], w32[:],
                                       start=True, stop=True)
                    ins.then_inc(_p.handle("tensorprog"), 1)
                plan.ops["tensor"].append(go)
                et = plan.op("vector", "tensor_copy", wbg[:], ob.ap[:],
                             waits=[("tensorprog", tok[1]), gate_prev[0]])
                ob.free_tok = et
                return et

            def gate_evac(bank_ap, fc, n, bias_col, wb_tok, mmtok):
                nsl = slice(n * 512, (n + 1) * 512)
                wl = [mmtok, wb_tok, gate_prev[0], GATES]
                if bias_col is not None:
                    g1 = plan.op("vector", "scalar_tensor_tensor", gtmp[:, :],
                                 bank_ap, bias_col, wbg[:, nsl],
                                 ALU.add, ALU.mult, waits=wl)
                else:
                    g1 = plan.op("vector", "tensor_mul", gtmp[:, :],
                                 bank_ap, wbg[:, nsl], waits=wl)
                g2 = plan.op("vector", "scalar_tensor_tensor", xres[:, fc, nsl],
                             gtmp[:, :], dcols[:, DC_GB + fc:DC_GB + fc + 1],
                             xres[:, fc, nsl], ALU.mult, ALU.add, waits=[g1])
                gate_prev[0] = g2
                return g2

            h_rhs = lambda kc, nsl: hbuf[:, kc, nsl]

            def dense_stage(w_ids, rhs_view, out_cb):
                toks = []
                for fc in range(FC):
                    for n in range(2):
                        nsl = slice(n * 512, (n + 1) * 512)
                        lid = w_ids[fc // 4]
                        off = (fc % 4) * 128
                        wv = ring.view(lid)
                        bank = takeA()
                        tok = mm_acc(bank.ap[:],
                                     [(wv[:, kc, off:off + 128], rhs_view(kc, nsl))
                                      for kc in range(8)],
                                     waits=[ring.use_tok(lid), bank.free_tok])
                        toks.append(out_cb(fc, n, bank, tok))
                return toks

            def evac_to(chain_view, bias_row):
                def cb(fc, n, bank, tok):
                    et = plan.op("scalar", "activation",
                                 chain_view[:, fc, n * 512:(n + 1) * 512],
                                 bank.ap[:], AF.Identity,
                                 bias=vecs[:, bias_row, fc:fc + 1], waits=[tok])
                    bank.free_tok = et
                    return et
                return cb

            def big_out(w_ids, rhs_fcview, nkc, out_cb, rhs_rec=None):
                assert len(w_ids) <= 2
                for fc in range(FC):
                    for n in range(2):
                        nsl = slice(n * 512, (n + 1) * 512)
                        bank = takeA()
                        plan.wait("tensor", [bank.free_tok] +
                                  [ring.use_tok(i) for i in w_ids])
                        tok = plan._bump("tensorprog", 1)
                        plan.rec("tensor",
                                 reads=[ring.view(i) for i in w_ids] +
                                       [rhs_rec if rhs_rec is not None
                                        else work[:, :]],
                                 writes=[bank.ap[:]],
                                 incs=[("tensorprog", tok[1])], tag="bigout_mm")

                        def go(t, _fc=fc, _nsl=nsl, _ap=bank.ap, _ids=tuple(w_ids),
                               _rv=rhs_fcview, _nkc=nkc, _p=plan):
                            ins = None
                            per = _nkc // len(_ids)
                            for seg, lid in enumerate(_ids):
                                wv = ring.view(lid)
                                for s in range(per):
                                    kk = seg * per + s
                                    ins = t.matmul(_ap,
                                                   wv[:, s, _fc * 128:(_fc + 1) * 128],
                                                   _rv(kk, _nsl),
                                                   start=(kk == 0), stop=(kk == _nkc - 1))
                            ins.then_inc(_p.handle("tensorprog"), 1)
                        plan.ops["tensor"].append(go)
                        out_cb(fc, n, bank, ("tensorprog", tok[1]))

            def rounds_out(w_dram, rhs_fcview, gate_cb):
                for r in range(2):
                    tb = [takeA() for _ in range(4)] + [takeO(), takeO()]
                    aps = [tb[i].ap for i in range(4)] + \
                          [tb[4].ap[:, 0:512], tb[4].ap[:, 512:1024],
                           tb[5].ap[:, 0:512], tb[5].ap[:, 512:1024]]
                    tiles = [(4 * r + i, n) for i in range(4) for n in range(2)]
                    base = plan.cnt["tensorprog"]
                    seg_ids = []
                    for s in range(4):
                        seg_ids.append(load_rows(w_dram, s * 1024, (s + 1) * 1024))
                        ring.release(seg_ids[s], ("tensorprog", base + s + 1))
                    plan.wait("tensor", [b.free_tok for b in tb] +
                              [ring.use_tok(seg_ids[0]), plan.tok("vector"),
                               plan.tok("scalar")])
                    plan._bump("tensorprog", 4)
                    segw = [ring.use_tok(i) for i in seg_ids]
                    for seg in range(4):
                        if seg > 0:
                            plan.log.append(("wait", "tensor", ((segw[seg][0],
                                                                segw[seg][1]),)))
                        plan.rec("tensor",
                                 reads=[ring.view(seg_ids[seg]), work[:, :]],
                                 writes=list(aps),
                                 incs=[("tensorprog", base + seg + 1)],
                                 tag="rounds_mm")

                    def go(t, _tiles=tuple(tiles), _aps=tuple(aps),
                           _ids=tuple(seg_ids), _rv=rhs_fcview,
                           _segw=tuple(segw), _p=plan):
                        for seg in range(4):
                            if seg > 0:
                                t.wait_ge(_p.handle(_segw[seg][0]), _segw[seg][1])
                            wv = ring.view(_ids[seg])
                            ins = None
                            for s in range(8):
                                kk = seg * 8 + s
                                for ti, (outf, n) in enumerate(_tiles):
                                    ins = t.matmul(
                                        _aps[ti],
                                        wv[:, s, outf * 128:(outf + 1) * 128],
                                        _rv(kk, slice(n * 512, (n + 1) * 512)),
                                        start=(kk == 0), stop=(kk == 31))
                            ins.then_inc(_p.handle("tensorprog"), 1)
                    plan.ops["tensor"].append(go)
                    done = ("tensorprog", base + 4)
                    for ti, (outf, n) in enumerate(tiles):
                        g = gate_cb(aps[ti], outf, n, done)
                        if ti < 4:
                            tb[ti].free_tok = g
                        else:
                            tb[4 + (ti - 4) // 2].free_tok = g


            u_rhs = lambda kk, nsl: uview[:, kk, nsl]

            u_toks = [None] * 3
            rwq = [None]

            def emit_router():
                rt_id = ring.plan_load(w_src(w_router, 0, D, 0, 4), 8, 4)
                rtv = ring.view(rt_id)
                bank = takeA()
                plan.wait("tensor", [bank.free_tok, ring.use_tok(rt_id), h2tok[1]])
                lt_tok = plan._bump("tensorprog", 1)
                plan.rec("tensor", reads=[hbuf[:, :, :], rtv[:, :, :]],
                         writes=[bank.ap[:, 0:32]],
                         incs=[("tensorprog", lt_tok[1])], tag="router_mm")

                def go_rt(t, _ap=bank.ap, _rtv=rtv, _p=plan):
                    ins = None
                    for t8 in range(8):
                        for kc in range(8):
                            ins = t.matmul(_ap[:, t8 * 4:(t8 + 1) * 4],
                                           hbuf[:, kc, t8 * 128:(t8 + 1) * 128],
                                           _rtv[:, kc, 0:4],
                                           start=(kc == 0), stop=(kc == 7))
                    ins.then_inc(_p.handle("tensorprog"), 1)
                plan.ops["tensor"].append(go_rt)
                ring.release(rt_id, ("tensorprog", lt_tok[1]))
                lg = plan.op("vector", "tensor_add", wtm[:], bank.ap[:, 0:32], rblr[:],
                             waits=[("tensorprog", lt_tok[1]), RBL])
                bank.free_tok = lg
                l_ = lambda e: wtm[:, e:32:4]
                ra = lambda i: rmath[:, i, :]
                ta = plan.op("vector", "tensor_tensor", ra(0), l_(0), l_(1), ALU.max,
                             waits=[lg])
                tb = plan.op("vector", "tensor_tensor", ra(1), l_(0), l_(1), ALU.min)
                tc_ = plan.op("vector", "tensor_tensor", ra(2), l_(2), l_(3), ALU.max)
                td = plan.op("vector", "tensor_tensor", ra(3), l_(2), l_(3), ALU.min)
                t4 = plan.op("vector", "tensor_tensor", ra(4), ra(0), ra(2), ALU.max,
                             waits=[ta, tc_])
                t5 = plan.op("vector", "tensor_tensor", ra(5), ra(0), ra(2), ALU.min)
                t6 = plan.op("vector", "tensor_tensor", ra(6), ra(1), ra(3), ALU.max,
                             waits=[tb, td])
                t7 = plan.op("vector", "tensor_tensor", ra(7), ra(5), ra(6), ALU.max,
                             waits=[t5, t6])
                t8_ = plan.op("vector", "tensor_tensor", ra(8), ra(7), ra(4), ALU.subtract,
                              waits=[t4, t7])
                e2_ = plan.op("scalar", "activation", ra(8), ra(8), AF.Exp, waits=[t8_])
                t9 = plan.op("vector", "tensor_scalar_add", ra(8), ra(8), 1.0, waits=[e2_])
                t10 = plan.op("vector", "reciprocal", ra(9), ra(8), waits=[t9])
                wq_toks = rwq[0] = []
                for e in range(4):
                    de = plan.op("vector", "tensor_tensor", ra(10), l_(e), ra(4),
                                 ALU.subtract, waits=[t4, t10] + wq_toks[-1:])
                    ee = plan.op("scalar", "activation", ra(10), ra(10), AF.Exp, waits=[de])
                    se = plan.op("vector", "tensor_tensor", ra(11), l_(e), ra(7),
                                 ALU.is_ge, waits=[t7])
                    m1_ = plan.op("vector", "tensor_tensor", ra(10), ra(10), ra(11),
                                  ALU.mult, waits=[ee, se])
                    wq_toks.append(plan.op("vector", "tensor_tensor", wq[:, e:32:4],
                                           ra(10), ra(9), ALU.mult, waits=[m1_]))
                if q == 0:
                    tap("wtm", wtm[:], [plan.tok("vector")])
                    tap("wq", wq[:], [plan.tok("vector")])

            plan.mark("e0u")
            for hf in range(32):
                if hf % 4 == 0:
                    j = hf // 4
                    lid1 = load_cols(sw_w1, j * 512, (j + 1) * 512)
                    lid3 = load_cols(sw_w3, j * 512, (j + 1) * 512)
                off = (hf % 4) * 128
                w1v, w3v = ring.view(lid1), ring.view(lid3)
                for n in range(2):
                    nsl = slice(n * 512, (n + 1) * 512)
                    b1, b3 = takeA(), takeA()
                    t1 = mm_acc(b1.ap[:],
                                [(w1v[:, kc, off:off + 128], hbuf[:, kc, nsl])
                                 for kc in range(8)],
                                waits=[ring.use_tok(lid1), b1.free_tok, h2tok[n]])
                    t3 = mm_acc(b3.ap[:],
                                [(w3v[:, kc, off:off + 128], hbuf[:, kc, nsl])
                                 for kc in range(8)],
                                waits=[ring.use_tok(lid3), b3.free_tok])
                    si = (2 * hf + n) % 3
                    ss = stmp[:, si, :]
                    st = plan.op("scalar", "activation", ss, b1.ap[:], AF.Silu,
                                 waits=[t1, u_toks[si]])
                    b1.free_tok = st
                    ut = plan.op("vector", "tensor_mul", uview[:, hf, nsl],
                                 ss, b3.ap[:], waits=[st, t3])
                    b3.free_tok = ut
                    u_toks[si] = ut
                if hf % 4 == 3:
                    ring.release(lid1, plan.tok("tensor"))
                    ring.release(lid3, plan.tok("tensor"))
                if hf == 3:
                    emit_router()
            wq_toks = rwq[0]

            bank = takeA()
            trt = plan.op("tensor", "transpose", bank.ap[0:32, 0:128], wq[:], ident[:],
                          waits=wq_toks + [bank.free_tok, CONST], inc="tensorprog")
            w32t = plan.op("vector", "tensor_copy", w32[:], bank.ap[0:32, 0:128],
                           waits=[trt])
            bank.free_tok = w32t
            wb_tok = make_wb(0)
            plan.barrier()
            if q == 0:
                tap("u0", work[:, 0:32768], [plan.tok("vector")])

            def gate_cb_e0(ap, outf, n, tok, _wb=wb_tok):
                return gate_evac(ap, outf, n, None, _wb, tok)

            plan.mark("e0out")
            rounds_out(sw_w2, u_rhs, gate_cb_e0)
            plan.barrier()

            plan.mark("e1")
            wb_tok = make_wb(1)
            c0, c1v, c2v = chain(0), chain(8192), chain(16384)
            gz = work[:, 0:16384].rearrange("p (fc t) -> p fc t", t=1024)
            eq_ids = [load_cols(me_eq, j * 512, (j + 1) * 512) for j in range(2)]
            dense_stage(eq_ids, h_rhs, evac_to(c2v, V_EQB))
            for i in eq_ids:
                ring.release(i, plan.tok("tensor"))
            plan.barrier()
            wv_ids = [load_cols(me_wv, j * 512, (j + 1) * 512) for j in range(2)]
            c2_rhs = lambda kc, nsl: c2v[:, kc, nsl]
            dense_stage(wv_ids, c2_rhs, evac_to(c1v, V_MEBV))
            for i in wv_ids:
                ring.release(i, plan.tok("tensor"))
            plan.barrier()
            wo_ids = [load_cols(me_wo, j * 512, (j + 1) * 512) for j in range(2)]
            c1_rhs = lambda kc, nsl: c1v[:, kc, nsl]
            dense_stage(wo_ids, c1_rhs, evac_to(c2v, V_MEBO))
            for i in wo_ids:
                ring.release(i, plan.tok("tensor"))
            plan.barrier()
            c1_ids = [load_cols(me_c1, j * 512, (j + 1) * 512) for j in range(4)]
            for fc in range(16):
                for n in range(2):
                    nsl = slice(n * 512, (n + 1) * 512)
                    lid = c1_ids[fc // 4]
                    off = (fc % 4) * 128
                    wv = ring.view(lid)
                    bank = takeA()
                    tok = mm_acc(bank.ap[:],
                                 [(wv[:, kc, off:off + 128], c2v[:, kc, nsl])
                                  for kc in range(8)],
                                 waits=[ring.use_tok(lid), bank.free_tok])
                    et = plan.op("scalar", "activation", gz[:, fc, nsl], bank.ap[:],
                                 AF.Gelu,
                                 bias=vecs[:, V_C1B0 + fc // 8, fc % 8:fc % 8 + 1],
                                 waits=[tok])
                    bank.free_tok = et
            for i in c1_ids:
                ring.release(i, plan.tok("tensor"))
            plan.barrier()
            c2_ids = [load_rows(me_c2, j * 1024, (j + 1) * 1024) for j in range(2)]
            gz_rhs = lambda kk, nsl: gz[:, kk, nsl]

            def cb_e1(fc, n, bank, tok, _wb=wb_tok):
                g = gate_evac(bank.ap[:], fc, n, vecs[:, V_C2B, fc:fc + 1], _wb, tok)
                bank.free_tok = g
                return g

            big_out(c2_ids, gz_rhs, 16, cb_e1, rhs_rec=work[:, 0:16384])
            for i in c2_ids:
                ring.release(i, plan.tok("vector"))
            plan.barrier()

            plan.mark("e2")
            wb_tok = make_wb(2)
            syn_ids = [load_cols(ce_syn, j * 512, (j + 1) * 512) for j in range(2)]
            dense_stage(syn_ids, h_rhs, evac_to(c0, V_SYNB))
            for i in syn_ids:
                ring.release(i, plan.tok("tensor"))
            plan.barrier()
            cewv_ids = [load_cols(ce_wv, j * 512, (j + 1) * 512) for j in range(2)]
            c0_rhs = lambda kc, nsl: c0[:, kc, nsl]
            dense_stage(cewv_ids, c0_rhs, evac_to(c1v, V_CEBV))
            for i in cewv_ids:
                ring.release(i, plan.tok("tensor"))
            plan.barrier()
            cewo_ids = [load_cols(ce_wo, j * 512, (j + 1) * 512) for j in range(2)]

            def cb_ssum(fc, n, bank, tok):
                nsl = slice(n * 512, (n + 1) * 512)
                et = plan.op("vector", "scalar_tensor_tensor", c2v[:, fc, nsl],
                             bank.ap[:], vecs[:, V_CEBO, fc:fc + 1], c0[:, fc, nsl],
                             ALU.add, ALU.add, waits=[tok])
                bank.free_tok = et
                return et

            dense_stage(cewo_ids, c1_rhs, cb_ssum)
            for i in cewo_ids:
                ring.release(i, plan.tok("tensor"))
            plan.barrier()

            def src_ssum(fc):
                return (c2v[:, fc, :], False)
            plan.mark("n1")
            h2a_tok = [None, None]
            for chnk in range(2):
                csl = slice(chnk * 512, (chnk + 1) * 512)
                ln(512, csl, src_ssum,
                   lambda fc, tile, tok, prev, _csl=csl:
                       plan.op("vector", "tensor_scalar", c0[:, fc, _csl], tile,
                               vecs[:, V_N1G, fc:fc + 1], vecs[:, V_N1B, fc:fc + 1],
                               ALU.mult, ALU.add, waits=[tok]),
                   pre_toks=[plan.tok("vector")])
                h2a_tok[chnk] = plan.tok("vector")
            gz2 = work[:, 8192:24576].rearrange("p (fc t) -> p fc t", t=1024)
            f1_ids = [load_cols(ce_f1, j * 512, (j + 1) * 512) for j in range(4)]
            for n in range(2):
                nsl = slice(n * 512, (n + 1) * 512)
                for fc in range(16):
                    lid = f1_ids[fc // 4]
                    off = (fc % 4) * 128
                    wv = ring.view(lid)
                    bank = takeA()
                    tok = mm_acc(bank.ap[:],
                                 [(wv[:, kc, off:off + 128], c0[:, kc, nsl])
                                  for kc in range(8)],
                                 waits=[ring.use_tok(lid), bank.free_tok,
                                        h2a_tok[n]])
                    et = plan.op("scalar", "activation", gz2[:, fc, nsl], bank.ap[:],
                                 AF.Relu,
                                 bias=vecs[:, V_F1B0 + fc // 8, fc % 8:fc % 8 + 1],
                                 waits=[tok])
                    bank.free_tok = et
            for i in f1_ids:
                ring.release(i, plan.tok("tensor"))
            plan.barrier()
            c3 = work[:, 24576:32768].rearrange("p (fc t) -> p fc t", t=1024)
            f2_ids = [load_rows(ce_f2, j * 1024, (j + 1) * 1024) for j in range(2)]
            gz2_rhs = lambda kk, nsl: gz2[:, kk, nsl]

            def cb_s2(fc, n, bank, tok):
                nsl = slice(n * 512, (n + 1) * 512)
                et = plan.op("vector", "scalar_tensor_tensor", c3[:, fc, nsl],
                             bank.ap[:], vecs[:, V_F2B, fc:fc + 1], c0[:, fc, nsl],
                             ALU.add, ALU.add, waits=[tok])
                bank.free_tok = et
                return et

            big_out(f2_ids, gz2_rhs, 16, cb_s2, rhs_rec=work[:, 8192:24576])
            for i in f2_ids:
                ring.release(i, plan.tok("vector"))
            plan.barrier()
            h2b = work[:, 8192:16384].rearrange("p (fc t) -> p fc t", t=1024)

            def src_s2(fc):
                return (c3[:, fc, :], False)

            plan.mark("n2")
            h2b_tok = [None, None]
            for chnk in range(2):
                csl = slice(chnk * 512, (chnk + 1) * 512)
                ln(512, csl, src_s2,
                   lambda fc, tile, tok, prev, _csl=csl:
                       plan.op("vector", "tensor_scalar", h2b[:, fc, _csl], tile,
                               vecs[:, V_N2G, fc:fc + 1], vecs[:, V_N2B, fc:fc + 1],
                               ALU.mult, ALU.add, waits=[tok]),
                   pre_toks=[plan.tok("vector")])
                h2b_tok[chnk] = plan.tok("vector")
            gen_ids = [load_cols(ce_gen, j * 512, (j + 1) * 512) for j in range(2)]
            for n in range(2):
                nsl = slice(n * 512, (n + 1) * 512)
                for fc in range(FC):
                    lid = gen_ids[fc // 4]
                    off = (fc % 4) * 128
                    wv = ring.view(lid)
                    bank = takeA()
                    tok = mm_acc(bank.ap[:],
                                 [(wv[:, kc, off:off + 128], h2b[:, kc, nsl])
                                  for kc in range(8)],
                                 waits=[ring.use_tok(lid), bank.free_tok,
                                        h2b_tok[n]])
                    g = gate_evac(bank.ap[:], fc, n, vecs[:, V_GENB, fc:fc + 1],
                                  wb_tok, tok)
                    bank.free_tok = g
            for i in gen_ids:
                ring.release(i, plan.tok("vector"))
            plan.barrier()

            plan.mark("e3")
            wb_tok = make_wb(3)
            for hf in range(32):
                if hf % 4 == 0:
                    lid = load_cols(ml_w1, (hf // 4) * 512, (hf // 4 + 1) * 512)
                off = (hf % 4) * 128
                wv = ring.view(lid)
                for n in range(2):
                    nsl = slice(n * 512, (n + 1) * 512)
                    bank = takeA()
                    tok = mm_acc(bank.ap[:],
                                 [(wv[:, kc, off:off + 128], hbuf[:, kc, nsl])
                                  for kc in range(8)],
                                 waits=[ring.use_tok(lid), bank.free_tok])
                    et = plan.op("scalar", "activation", uview[:, hf, nsl], bank.ap[:],
                                 AF.Gelu,
                                 bias=vecs[:, V_MLB1 + hf // 8, hf % 8:hf % 8 + 1],
                                 waits=[tok])
                    bank.free_tok = et
                if hf % 4 == 3:
                    ring.release(lid, plan.tok("scalar"))
            plan.barrier()

            def gate_cb_e3(ap, outf, n, tok, _wb=wb_tok):
                return gate_evac(ap, outf, n, vecs[:, V_MLB2, outf:outf + 1],
                                 _wb, tok)

            plan.mark("e3out")
            rounds_out(ml_w2, u_rhs, gate_cb_e3)
            xdone = plan.tok("vector")

            yt = plan.op("sync", "dma_start",
                         out=y_out[:, q * TQ:(q + 1) * TQ].rearrange(
                             "(fc p) t -> p fc t", p=128),
                         in_=xres[:, :, :],
                         waits=[xdone], inc="ystore", inc_k=16)
            prev_yhalf = [yt, yt]
            plan.barrier()

        tap("fin_modc", modc[:], [plan.tok("vector")])
        tap("fin_dcols", dcols[:], [plan.tok("vector")])
        tap("fin_rblr", rblr[:], [plan.tok("vector")])
        tap("fin_bvrows", bvrows[:], [plan.tok("vector")])
        tap("fin_maskA", maskA[:], [plan.tok("vector")])
        tap("fin_ones", ones[:], [plan.tok("vector")])
        tap("fin_vecs", vecs[:], [plan.tok("vector")])
        tap("fin_silc", silc[:], [plan.tok("vector")])
        tap("fin_epsc", epsc[:], [plan.tok("vector")])
        plan.wait("sync", [("ystore", plan.cnt["ystore"])] +
                  ([("tapst", plan.cnt["tapst"])] if "tapst" in plan.cnt else []))

        for n in sorted(plan.cnt.keys()):
            plan.handles[n] = es.enter_context(nc.semaphore(n))
        block = es.enter_context(nc.Block())
        plan.emit(block)

    LAST_MARKS[:] = plan.marks
    LAST_LOG[:] = plan.log
    return nc, din


_CACHE = {}


def _host_inputs(inputs):
    x = np.asarray(inputs["L0_attn_x"])
    c = np.asarray(inputs["L0_c"])
    vec_rows = [
        inputs["L0_attn_ln_w"], inputs["L0_attn_ln_b"],
        inputs["L0_ffn_me_eq_b"], inputs["L0_ffn_me_bv"], inputs["L0_ffn_me_bo"],
        inputs["L0_ffn_me_c2b"],
        inputs["L0_ffn_ce_syn_b"], inputs["L0_ffn_ce_bv"], inputs["L0_ffn_ce_bo"],
        inputs["L0_ffn_ce_n1g"], inputs["L0_ffn_ce_n1b"], inputs["L0_ffn_ce_f2b"],
        inputs["L0_ffn_ce_n2g"], inputs["L0_ffn_ce_n2b"], inputs["L0_ffn_ce_gen_b"],
        inputs["L0_ffn_ml_b2"],
        inputs["L0_attn_b_qkv"][0:1024], inputs["L0_attn_b_qkv"][1024:2048],
        inputs["L0_attn_b_qkv"][2048:3072],
        inputs["L0_ffn_me_c1b"][0:1024], inputs["L0_ffn_me_c1b"][1024:2048],
        inputs["L0_ffn_ce_f1b"][0:1024], inputs["L0_ffn_ce_f1b"][1024:2048],
        inputs["L0_ffn_ml_b1"][0:1024], inputs["L0_ffn_ml_b1"][1024:2048],
        inputs["L0_ffn_ml_b1"][2048:3072], inputs["L0_ffn_ml_b1"][3072:4096],
    ]
    vecs = np.stack([np.asarray(r, dtype=np.float32) for r in vec_rows])
    vecs_fm = np.ascontiguousarray(vecs.reshape(27, 8, 128).transpose(2, 0, 1))
    onesb = np.ones((128, 128), dtype=ml_dtypes.bfloat16)
    ident = np.eye(128, dtype=np.float32)
    p = np.arange(128)[:, None]
    qc = np.arange(128)[None, :]
    maskA = ((qc < p).astype(np.float32)) * np.float32(SCALE)
    maskC = ((qc >= p).astype(np.float32)) * np.float32(SCALE)
    hotrows = np.zeros((32, 4096), dtype=ml_dtypes.bfloat16)
    for qrow in range(32):
        hotrows[qrow, qrow * 128:(qrow + 1) * 128] = 1
    shared = dict(
        w_qkv=np.asarray(inputs["L0_attn_W_qkv"]),
        w_mod=np.asarray(inputs["L0_wmod"]),
        w_router=np.asarray(inputs["L0_ffn_router_w"]),
        rb=np.asarray(inputs["L0_ffn_router_b"]).reshape(1, 4),
        lb=np.asarray(inputs["L0_ffn_load_balancer"]).reshape(1, 4),
        sw_w1=np.asarray(inputs["L0_ffn_sw_w1"]),
        sw_w3=np.asarray(inputs["L0_ffn_sw_w3"]),
        sw_w2=np.asarray(inputs["L0_ffn_sw_w2"]),
        me_eq=np.asarray(inputs["L0_ffn_me_eq_w"]),
        me_wv=np.asarray(inputs["L0_ffn_me_wv"]),
        me_wo=np.asarray(inputs["L0_ffn_me_wo"]),
        me_c1=np.asarray(inputs["L0_ffn_me_c1w"]),
        me_c2=np.asarray(inputs["L0_ffn_me_c2w"]),
        ce_syn=np.asarray(inputs["L0_ffn_ce_syn_w"]),
        ce_wv=np.asarray(inputs["L0_ffn_ce_wv"]),
        ce_wo=np.asarray(inputs["L0_ffn_ce_wo"]),
        ce_f1=np.asarray(inputs["L0_ffn_ce_f1w"]),
        ce_f2=np.asarray(inputs["L0_ffn_ce_f2w"]),
        ce_gen=np.asarray(inputs["L0_ffn_ce_gen_w"]),
        ml_w1=np.asarray(inputs["L0_ffn_ml_w1"]),
        ml_w2=np.asarray(inputs["L0_ffn_ml_w2"]),
        vecs_fm=vecs_fm,
        bvrow=np.ascontiguousarray(
            np.asarray(inputs["L0_attn_b_qkv"][2048:3072])).reshape(1, 1024),
        ones_bf=onesb, ident=ident, maskA=maskA, maskC=maskC, hotrows=hotrows,
    )
    in_maps = []
    for b in range(B):
        m = dict(shared)
        m["x_fm"] = np.ascontiguousarray(np.asarray(x[b]).T)
        m["c_fm"] = np.ascontiguousarray(np.asarray(c[b]).reshape(8, 128).T)
        in_maps.append(m)
    return in_maps


def kernel(**inputs):
    if "nc" not in _CACHE:
        _CACHE["nc"] = build_nc()
    nc, _ = _CACHE["nc"]
    in_maps = _host_inputs(inputs)
    res = run_bass_kernel_spmd(nc, in_maps, core_ids=list(range(B)))
    out = np.stack([np.asarray(res.results[b]["y_fm"]).T for b in range(B)])
    return np.ascontiguousarray(out.astype(np.float32))


# version 10
# speedup vs baseline: 1.0371x; 1.0371x over previous
import contextlib
import numpy as np
import ml_dtypes
import concourse.bass as bass
import concourse.mybir as mybir
from concourse.bass_utils import run_bass_kernel_spmd

F32 = mybir.dt.float32
BF16 = mybir.dt.bfloat16
AF = mybir.ActivationFunctionType
ALU = mybir.AluOpType

B, T, D = 8, 4096, 1024
NQ = 4
TQ = T // NQ
FC = D // 128
EPS = 1e-5
SCALE = 1.0 / 512.0
RINV = 1.0 / 1024.0

V_LNW, V_LNB = 0, 1
V_EQB, V_MEBV, V_MEBO, V_C2B = 2, 3, 4, 5
(V_SYNB, V_CEBV, V_CEBO, V_N1G, V_N1B, V_F2B, V_N2G, V_N2B, V_GENB,
 V_MLB2) = 6, 7, 8, 9, 10, 11, 12, 13, 14, 15
V_BQ, V_BK, V_BV = 16, 17, 18
V_C1B0 = 19
V_F1B0 = 21
V_MLB1 = 23

DC_CAPL, DC_CBPL, DC_GA, DC_GB, DC_GW, DC_GB2 = 0, 8, 16, 24, 32, 40

ENGS = ("tensor", "vector", "scalar", "gpsimd", "sync")
LAST_MARKS = []
LAST_LOG = []


class Plan:

    def __init__(self, nc):
        self.nc = nc
        self.ops = {e: [] for e in ENGS}
        self.cnt = {}
        self.seen = {e: {} for e in ENGS}
        self.handles = {}
        self.marks = []
        self.mm_count = [0]
        self.log = []

    def mark(self, label):
        def go(e, _l=label, _p=self):
            _p.marks.append((_p.mm_count[0], _l))
        self.ops["tensor"].append(go)

    def sem(self, name):
        if name not in self.cnt:
            self.cnt[name] = 0
        return name

    def handle(self, name):
        return self.handles[name]

    def _bump(self, name, k):
        self.sem(name)
        self.cnt[name] += k
        return (name, self.cnt[name])

    def tok(self, eng):
        return (eng + "prog", self.cnt.get(eng + "prog", 0))

    def wait(self, eng, toks):
        best = {}
        for t in toks:
            if t is None:
                continue
            n, v = t
            if v > 0 and v > best.get(n, 0):
                best[n] = v
        todo = []
        for n, v in sorted(best.items()):
            if self.seen[eng].get(n, 0) < v:
                self.sem(n)
                todo.append((n, v))
                self.seen[eng][n] = v
        if todo:
            self.log.append(("wait", eng, tuple(todo)))

            def go(e, _todo=tuple(todo), _p=self):
                for n, v in _todo:
                    e.wait_ge(_p.handle(n), v)
            self.ops[eng].append(go)

    def op(self, eng, fn_name, *args, waits=(), inc=None, inc_k=1, **kw):
        self.wait(eng, list(waits))
        sem_name = inc if inc is not None else eng + "prog"
        tok = self._bump(sem_name, inc_k)
        aps = [a for a in args if hasattr(a, "ap") and hasattr(a, "tensor")]
        wr, rd = [], []
        if "out" in kw:
            wr.append(kw["out"])
        if "in_" in kw:
            rd.append(kw["in_"])
        if aps:
            wr.append(aps[0])
            rd.extend(aps[1:])
        for k2 in ("bias", "scale"):
            if k2 in kw and hasattr(kw[k2], "ap"):
                rd.append(kw[k2])
        self.log.append(("op", eng, tuple(rd), tuple(wr), ((sem_name, tok[1]),),
                         fn_name))

        def go(e, _fn=fn_name, _args=args, _kw=kw, _sem=sem_name, _k=inc_k, _p=self):
            ins = getattr(e, _fn)(*_args, **_kw)
            ins.then_inc(_p.handle(_sem), _k)
        self.ops[eng].append(go)
        return tok

    def rec(self, eng, reads=(), writes=(), incs=(), tag=""):
        self.log.append(("op", eng, tuple(reads), tuple(writes), tuple(incs), tag))

    def barrier(self, engs=("tensor", "vector", "scalar")):
        cur = {e: (e + "prog", self.cnt.get(e + "prog", 0)) for e in engs}
        for e in engs:
            self.wait(e, [cur[o] for o in engs if o != e])

    def emit(self, block):
        class _CountingTensor:
            def __init__(s, eng, p):
                s._e, s._p = eng, p

            def matmul(s, *a, **k):
                s._p.mm_count[0] += 1
                return s._e.matmul(*a, **k)

            def __getattr__(s, n):
                return getattr(s._e, n)

        for eng in ENGS:
            lst = self.ops[eng]
            if not lst:
                continue

            def run(engine, _lst=lst, _eng=eng):
                if _eng == "tensor":
                    engine = _CountingTensor(engine, self)
                for f in _lst:
                    f(engine)
            getattr(block, eng)(run)


class Bank:
    def __init__(self, ap, name):
        self.ap = ap
        self.name = name
        self.free_tok = None


def build_nc(debug=(), pace=()):
    nc = bass.Bass("TRN2", target_bir_lowering=False)
    plan = Plan(nc)
    din = {}

    def dt_in(name, shape, dtype=F32):
        din[name] = nc.dram_tensor(name, shape, dtype, kind="ExternalInput").ap()
        return din[name]

    x_in = dt_in("x_fm", [D, T])
    c_in = dt_in("c_fm", [128, FC])
    w_qkv = dt_in("w_qkv", [D, 3 * D])
    w_mod = dt_in("w_mod", [D, 6 * D])
    w_router = dt_in("w_router", [D, 4])
    rb_in = dt_in("rb", [1, 4])
    lb_in = dt_in("lb", [1, 4])
    sw_w1 = dt_in("sw_w1", [D, 4096])
    sw_w3 = dt_in("sw_w3", [D, 4096])
    sw_w2 = dt_in("sw_w2", [4096, D])
    me_eq = dt_in("me_eq", [D, D])
    me_wv = dt_in("me_wv", [D, D])
    me_wo = dt_in("me_wo", [D, D])
    me_c1 = dt_in("me_c1", [D, 2048])
    me_c2 = dt_in("me_c2", [2048, D])
    ce_syn = dt_in("ce_syn", [D, D])
    ce_wv = dt_in("ce_wv", [D, D])
    ce_wo = dt_in("ce_wo", [D, D])
    ce_f1 = dt_in("ce_f1", [D, 2048])
    ce_f2 = dt_in("ce_f2", [2048, D])
    ce_gen = dt_in("ce_gen", [D, D])
    ml_w1 = dt_in("ml_w1", [D, 4096])
    ml_w2 = dt_in("ml_w2", [4096, D])
    vecs_in = dt_in("vecs_fm", [128, 27, 8])
    bvrow_in = dt_in("bvrow", [1, D])
    ones_in = dt_in("ones_bf", [128, 128], BF16)
    ident_in = dt_in("ident", [128, 128])
    maskA_in = dt_in("maskA", [128, 128])
    maskC_in = dt_in("maskC", [128, 128])
    hot_in = dt_in("hotrows", [32, 4096], BF16)

    y_out = nc.dram_tensor("y_fm", [D, T], F32, kind="ExternalOutput").ap()
    dbg = {}
    for k, shape, dtp in debug:
        dbg[k] = nc.dram_tensor("dbg_" + k, list(shape), dtp, kind="ExternalOutput").ap()

    es = contextlib.ExitStack()

    def sbt(name, shape, dtype=F32):
        return es.enter_context(nc.sbuf_tensor(name, shape, dtype))

    with es:
        xres = sbt("xres", [128, FC, TQ], F32)
        hbuf = sbt("hbuf", [128, FC, TQ], BF16)
        work = sbt("work", [128, 32768], BF16)
        wring = sbt("wring", [128, 20480], BF16)
        mstat = sbt("mstat", [128, 3, 1024], F32)
        statbf = sbt("statbf", [128, 2, 1024], BF16)
        xbsq = sbt("xbsq", [128, 4, 1024], BF16)
        wbg = sbt("wbg", [128, 1024], F32)
        gtmp = sbt("gtmp", [128, 512], F32)
        halo = sbt("halo", [128, 4096], BF16)
        stmp = sbt("stmp", [128, 3, 512], BF16)
        vecs = sbt("vecs_s", [128, 27, 8], F32)
        modc = sbt("modc", [128, 48], F32)
        dcols = sbt("dcols", [128, 64], F32)
        ones = sbt("ones_s", [128, 128], BF16)
        ident = sbt("ident_s", [128, 128], F32)
        maskA = sbt("maskA_s", [128, 128], F32)
        maskC = sbt("maskC_s", [128, 128], F32)
        hot = sbt("hot_s", [32, 4096], BF16)
        w32 = sbt("w32", [32, 128], BF16)
        wtm = sbt("wtm", [128, 32], F32)
        wq = sbt("wq", [128, 32], F32)
        rmath = sbt("rmath", [128, 12, 8], F32)
        rblr = sbt("rblr", [128, 32], F32)
        bvrows = sbt("bvrows", [128, D], BF16)
        epsc = sbt("epsc", [128, 1], F32)
        cfm = sbt("cfm", [128, FC], F32)
        silc = sbt("silc", [128, FC], BF16)
        rrow = mstat[0:1, 1, 0:8]
        rrbf = statbf[0:1, 1, 0:8]
        bvrow = mstat[0:1, 0, 0:1024]
        bvrbf = statbf[0:1, 0, 0:1024]

        pA = [es.enter_context(nc.psum_tensor(f"pA{i}", [128, 512], F32)) for i in range(4)]
        pO = [es.enter_context(nc.psum_tensor(f"pO{i}", [128, 1024], F32)) for i in range(2)]
        bankA = [Bank(pA[i][:], f"A{i}") for i in range(4)]
        bankO = [Bank(pO[i][:], f"O{i}") for i in range(2)]

        rrA = [0]

        def takeA():
            b = bankA[rrA[0] % 4]
            rrA[0] += 1
            return b

        rrO = [0]

        def takeO():
            b = bankO[rrO[0] % 2]
            rrO[0] += 1
            return b

        def mm_acc(bank_ap, pairs, waits=(), flags=None):
            plan.wait("tensor", list(waits))
            tok = plan._bump("tensorprog", 1)
            plan.rec("tensor", reads=[a for pr in pairs for a in pr],
                     writes=[bank_ap], incs=[("tensorprog", tok[1])], tag="mm")
            n = len(pairs)
            if flags is None:
                flags = [(i == 0, i == n - 1) for i in range(n)]

            def go(t, _ap=bank_ap, _pairs=tuple(pairs), _flags=tuple(flags), _p=plan):
                ins = None
                for (lh, rh), (st, sp) in zip(_pairs, _flags):
                    ins = t.matmul(_ap, lh, rh, start=st, stop=sp)
                ins.then_inc(_p.handle("tensorprog"), 1)
            plan.ops["tensor"].append(go)
            return tok

        RING_UNITS, UNIT = 10, 2048

        class WRing:
            def __init__(self):
                self.head = 0
                self.loads = []
                self.ld_tok = []
                self.unit_free = [None] * RING_UNITS

            def plan_load(self, src_ap, kc, inner):
                elems = kc * inner
                n_units = (elems + UNIT - 1) // UNIT
                if (self.head % RING_UNITS) + n_units > RING_UNITS:
                    while self.head % RING_UNITS != 0:
                        self.head += 1
                off = self.head % RING_UNITS
                idx = len(self.loads)
                self.loads.append((off, n_units, src_ap, kc, inner))
                self.head += n_units
                waits = [self.unit_free[u] for u in range(off, off + n_units)]
                plan.wait("gpsimd", waits)
                sem = f"wl{idx % 16}"
                tok = plan._bump(sem, 16)
                plan.rec("gpsimd", reads=[],
                         writes=[wring[:, off * UNIT:off * UNIT + kc * inner]],
                         incs=[(sem, tok[1])], tag="wload")

                def go(g, _off=off, _src=src_ap, _kc=kc, _in=inner, _p=plan,
                       _sem=sem):
                    dst = wring[:, _off * UNIT:_off * UNIT + _kc * _in].rearrange(
                        "p (kc n) -> p kc n", kc=_kc)
                    g.dma_start(out=dst, in_=_src).then_inc(_p.handle(_sem), 16)
                plan.ops["gpsimd"].append(go)
                self.ld_tok.append((sem, tok[1]))
                return idx

            def view(self, idx):
                off, n, _src, kc, inner = self.loads[idx]
                return wring[:, off * UNIT:off * UNIT + kc * inner].rearrange(
                    "p (kc n) -> p kc n", kc=kc)

            def use_tok(self, idx):
                return self.ld_tok[idx]

            def release(self, idx, tok):
                off, n, _src, _kc, _in = self.loads[idx]
                for u in range(off, off + n):
                    self.unit_free[u] = tok

        ring = WRing()
        plan.sem("wload")

        def w_src(w_ap, r0, r1, c0, c1):
            return w_ap[r0:r1, c0:c1].rearrange("(kc p) n -> p kc n", p=128)

        def load_cols(w_ap, c0, c1):
            return ring.plan_load(w_src(w_ap, 0, D, c0, c1), 8, c1 - c0)

        def load_rows(w_ap, r0, r1):
            return ring.plan_load(w_src(w_ap, r0, r1, 0, D), (r1 - r0) // 128, D)

        nconst = [0]

        def const_dma(dst, src):
            plan.op("sync", "dma_start", out=dst, in_=src, inc="cload", inc_k=16)
            nconst[0] += 1

        const_dma(vecs[:], vecs_in[:])
        const_dma(ones[:], ones_in[:])
        const_dma(ident[:], ident_in[:])
        const_dma(maskA[:], maskA_in[:])
        const_dma(maskC[:], maskC_in[:])
        const_dma(hot[:], hot_in[:])
        const_dma(cfm[:], c_in[:])
        const_dma(rrow[:, 0:4], rb_in[:])
        const_dma(rrow[:, 4:8], lb_in[:])
        const_dma(bvrow[:, :], bvrow_in[:])
        CONST = ("cload", 16 * nconst[0])

        def tap(name, ap, waits):
            if name in dbg:
                plan.op("sync", "dma_start", out=dbg[name][:], in_=ap,
                        waits=list(waits) + [CONST], inc="tapst", inc_k=16)
            elif name.split("_q")[0] in pace:
                plan.wait("sync", list(waits))

        eps_tok = plan.op("vector", "memset", epsc[:], EPS)
        sil_tok = plan.op("scalar", "activation", silc[:], cfm[:], AF.Silu,
                          waits=[CONST])

        wmod_ids = []
        ev_toks = []
        for mc in range(48):
            j, off = mc // 4, (mc % 4) * 128
            if mc % 4 == 0:
                wmod_ids.append(load_cols(w_mod, j * 512, (j + 1) * 512))
            lid = wmod_ids[j]
            wv = ring.view(lid)
            bank = takeA()
            pairs = [(wv[:, kc, off:off + 128], silc[:, kc:kc + 1]) for kc in range(8)]
            tok = mm_acc(bank.ap[:, 0:1], pairs,
                         waits=[ring.use_tok(lid), sil_tok, bank.free_tok])
            if mc % 4 == 3:
                ring.release(lid, tok)
            et = plan.op("vector", "tensor_copy", modc[:, mc:mc + 1], bank.ap[:, 0:1],
                         waits=[tok])
            bank.free_tok = et
            ev_toks.append(et)

        plan.op("vector", "tensor_scalar_add", dcols[:, DC_CAPL:DC_CAPL + 8],
                modc[:, 8:16], 1.0, waits=[ev_toks[15]])
        plan.op("vector", "tensor_scalar_add", dcols[:, DC_CBPL:DC_CBPL + 8],
                modc[:, 32:40], 1.0, waits=[ev_toks[39]])
        clip_prev = None
        for gi, (src0, dst) in enumerate(((16, DC_GA), (40, DC_GB))):
            sgn, mag = rmath[:, 2 * gi, :], rmath[:, 2 * gi + 1, :]
            t1 = plan.op("vector", "tensor_single_scalar", sgn, modc[:, src0:src0 + 8],
                         0.0, ALU.is_ge, waits=[ev_toks[-1], clip_prev])
            t2 = plan.op("vector", "tensor_scalar", sgn, sgn, 2.0, 1.0,
                         ALU.mult, ALU.subtract, waits=[t1])
            t3a = plan.op("scalar", "activation", mag, modc[:, src0:src0 + 8],
                          AF.Abs, waits=[ev_toks[-1]])
            t3 = plan.op("vector", "tensor_scalar_max", mag, mag, 0.25, waits=[t3a])
            clip_prev = plan.op("vector", "tensor_mul", dcols[:, dst:dst + 8],
                                sgn, mag, waits=[t2, t3])
        gtok = plan.tok("vector")
        plan.op("vector", "tensor_mul", dcols[:, DC_GW:DC_GW + 8],
                dcols[:, DC_GA:DC_GA + 8], vecs[:, V_LNW, :],
                waits=[gtok, CONST])
        plan.op("vector", "tensor_mul", dcols[:, DC_GB2:DC_GB2 + 8],
                dcols[:, DC_GA:DC_GA + 8], vecs[:, V_LNB, :])
        GATES = plan.tok("vector")
        tap("modc", modc[:], [GATES])
        tap("dcols", dcols[:], [GATES])

        ct = plan.op("vector", "tensor_copy", rrbf[:, :], rrow[:, :], waits=[CONST])
        bank = takeA()
        rtok = mm_acc(bank.ap[:, 0:4],
                      [(ones[0:1, :], rrbf[:, 0:4]), (ones[0:1, :], rrbf[:, 4:8])],
                      waits=[ct, bank.free_tok, CONST])
        prev = plan.op("vector", "tensor_copy", rblr[:, 0:4], bank.ap[:, 0:4], waits=[rtok])
        bank.free_tok = prev
        for t8 in range(1, 8):
            prev = plan.op("vector", "tensor_copy", rblr[:, t8 * 4:(t8 + 1) * 4],
                           rblr[:, 0:4], waits=[prev])
        RBL = prev

        cb = plan.op("vector", "tensor_copy", bvrbf[:, :], bvrow[:, :], waits=[CONST])
        for h in range(2):
            bank = takeA()
            tok = mm_acc(bank.ap[:], [(ones[0:1, :], bvrbf[:, h * 512:(h + 1) * 512])],
                         waits=[cb, bank.free_tok])
            ftok = plan.op("vector", "tensor_copy", bvrows[:, h * 512:(h + 1) * 512],
                           bank.ap[:], waits=[tok])
            bank.free_tok = ftok
        BVROWS = plan.tok("vector")

        kslots = work[:, 0:8192].rearrange("p (s e) -> p s e", e=1024)
        vslots = work[:, 8192:16384].rearrange("p (s e) -> p s e", e=1024)
        zq = work[:, 16384:24576].rearrange("p (dc t) -> p dc t", t=1024)
        zst = [work[:, 24576 + r * 384:24576 + (r + 1) * 384].rearrange(
            "p (i t) -> p i t", t=128) for r in range(2)]
        osb = work[:, 25600:29696].rearrange("p (dc t) -> p dc t", t=512)
        kh = halo[:, 0:2048].rearrange("p (s e) -> p s e", e=1024)
        vh = halo[:, 2048:4096].rearrange("p (s e) -> p s e", e=1024)
        uview = work[:, 0:32768].rearrange("p (hf t) -> p hf t", t=1024)

        def chain(base, nfc=8):
            return work[:, base:base + nfc * 1024].rearrange("p (fc t) -> p fc t", t=1024)

        lnwar = [None]

        def ln(width, csl, src_fn, out_fn, pre_toks=()):
            nh = width // 512
            if width == 1024:
                s1b, s2b = takeO(), takeO()
            else:
                s1b, s2b = takeA(), takeA()
            s1, s2 = s1b.ap, s2b.ap
            seg_t = {}
            last1 = last2 = None
            for fc in range(FC):
                slot = fc % 2
                ap, is_f32 = src_fn(fc)
                ap = ap[:, csl]
                wl = list(pre_toks) if fc < 2 else [seg_t[fc - 2]]
                if is_f32:
                    c1 = plan.op("vector", "tensor_copy", xbsq[:, slot, 0:width], ap,
                                 waits=wl)
                    src_bf = xbsq[:, slot, 0:width]
                    c2 = plan.op("vector", "tensor_mul", xbsq[:, 2 + slot, 0:width],
                                 src_bf, src_bf, waits=[c1])
                else:
                    src_bf = ap
                    c2 = plan.op("vector", "tensor_mul", xbsq[:, 2 + slot, 0:width],
                                 ap, ap, waits=wl)
                w0 = [c2] + ([s1b.free_tok, s2b.free_tok] if fc == 0 else [])
                for h in range(nh):
                    hs = slice(h * 512, (h + 1) * 512)
                    last1 = mm_acc(s1[:, hs], [(ones[:], src_bf[:, hs])], waits=w0,
                                   flags=[(fc == 0, fc == FC - 1)])
                    w0 = []
                    last2 = mm_acc(s2[:, hs], [(ones[:], xbsq[:, 2 + slot, hs])],
                                   flags=[(fc == 0, fc == FC - 1)])
                seg_t[fc] = last2
            m = mstat[:, 0, 0:width]
            qq = mstat[:, 1, 0:width]
            tp = mstat[:, 2, 0:width]
            a1 = plan.op("scalar", "activation", m, s1[:, 0:width], AF.Copy,
                         scale=RINV, waits=[last1, lnwar[0]])
            a2 = plan.op("scalar", "activation", qq, s2[:, 0:width], AF.Copy,
                         scale=RINV, waits=[last2])
            s1b.free_tok = a1
            s2b.free_tok = a2
            v1 = plan.op("vector", "tensor_mul", tp, m, m, waits=[a1])
            v2 = plan.op("vector", "tensor_tensor", tp, qq, tp, ALU.subtract,
                         waits=[v1, a2])
            l1 = plan.op("scalar", "activation", tp, tp, AF.Ln, bias=epsc[:, 0:1],
                         waits=[v2, eps_tok])
            l2 = plan.op("scalar", "activation", tp, tp, AF.Exp, scale=-0.5,
                         waits=[l1])
            mb = plan.op("vector", "tensor_copy", statbf[:, 0, 0:width], m, waits=[a1])
            rb_ = plan.op("vector", "tensor_copy", statbf[:, 1, 0:width], tp, waits=[l2])
            outs = []
            prev_out = None
            for fc in range(FC):
                ap, is_f32 = src_fn(fc)
                ap = ap[:, csl]
                slot = fc % 2
                wl = [mb, rb_]
                if fc >= 2:
                    wl.append(outs[fc - 2])
                if is_f32:
                    tt = plan.op("vector", "tensor_tensor", xbsq[:, slot, 0:width],
                                 ap, m, ALU.subtract, waits=wl)
                else:
                    tt = plan.op("vector", "tensor_tensor", xbsq[:, slot, 0:width],
                                 ap, statbf[:, 0, 0:width], ALU.subtract, waits=wl)
                t2 = plan.op("vector", "tensor_mul", xbsq[:, slot, 0:width],
                             xbsq[:, slot, 0:width], statbf[:, 1, 0:width],
                             waits=[tt])
                prev_out = out_fn(fc, xbsq[:, slot, 0:width], t2, prev_out)
                outs.append(prev_out)
            lnwar[0] = plan.tok("vector")
            return outs

        prev_yhalf = [None, None]
        for q in range(NQ):
            xin_h = []
            for n in range(2):
                nsl = slice(n * 512, (n + 1) * 512)
                xin_h.append(plan.op(
                    "sync", "dma_start",
                    out=xres[:, :, nsl],
                    in_=x_in[:, q * TQ + n * 512:q * TQ + (n + 1) * 512].rearrange(
                        "(fc p) t -> p fc t", p=128),
                    waits=([prev_yhalf[n]] if prev_yhalf[n] else []),
                    inc=f"xload{n}", inc_k=16))

            def src_x(fc):
                return (xres[:, fc, :], True)

            plan.mark("LN1")
            h1tok = [None, None]
            for chnk in range(2):
                csl = slice(chnk * 512, (chnk + 1) * 512)

                def out_h1(fc, tile, tok, prev, _csl=csl):
                    return plan.op("vector", "tensor_scalar", hbuf[:, fc, _csl],
                                   tile,
                                   dcols[:, DC_CAPL + fc:DC_CAPL + fc + 1],
                                   modc[:, 0 + fc:0 + fc + 1],
                                   ALU.mult, ALU.add, waits=[tok, GATES])

                ln(512, csl, src_x, out_h1, pre_toks=[xin_h[chnk], GATES])
                h1tok[chnk] = plan.tok("vector")
            if q == 0:
                tap("h1", hbuf[:, :, :], [plan.tok("vector")])

            plan.mark("qkv")
            for j in range(6):
                lid = load_cols(w_qkv, j * 512, (j + 1) * 512)
                wv = ring.view(lid)
                if j < 2:
                    for i in range(4):
                        dc = j * 4 + i
                        for ch in range(2):
                            tsl = slice(ch * 512, (ch + 1) * 512)
                            bank = takeA()
                            tok = mm_acc(bank.ap[:],
                                         [(wv[:, kc, i * 128:(i + 1) * 128],
                                           hbuf[:, kc, tsl]) for kc in range(8)],
                                         waits=[ring.use_tok(lid), bank.free_tok,
                                                h1tok[ch]])
                            et = plan.op("scalar", "activation", zq[:, dc, tsl],
                                         bank.ap[:], AF.Identity,
                                         bias=vecs[:, V_BQ, dc:dc + 1], waits=[tok])
                            bank.free_tok = et
                elif j < 4:
                    for i in range(4):
                        dc = (j - 2) * 4 + i
                        for ch in range(2):
                            tsl = slice(ch * 512, (ch + 1) * 512)
                            bank = takeA()
                            tok = mm_acc(bank.ap[:],
                                         [(wv[:, kc, i * 128:(i + 1) * 128],
                                           hbuf[:, kc, tsl]) for kc in range(8)],
                                         waits=[ring.use_tok(lid), bank.free_tok,
                                                h1tok[ch]])
                            dst = kslots[:, ch * 4:(ch + 1) * 4,
                                         dc * 128:(dc + 1) * 128]
                            src = bank.ap[:].rearrange("p (s t) -> p s t", t=128)
                            et = plan.op("scalar", "activation", dst, src, AF.Identity,
                                         bias=vecs[:, V_BK, dc:dc + 1], waits=[tok])
                            bank.free_tok = et
                else:
                    dv = j - 4
                    for ch in range(2):
                        for lt in range(4):
                            ttok = slice(ch * 512 + lt * 128, ch * 512 + (lt + 1) * 128)
                            bank = takeA()
                            tok = mm_acc(bank.ap[:],
                                         [(hbuf[:, kc, ttok], wv[:, kc, :])
                                          for kc in range(8)],
                                         waits=[ring.use_tok(lid), bank.free_tok,
                                                h1tok[ch]])
                            et = plan.op("vector", "tensor_add",
                                         vslots[:, ch * 4 + lt,
                                                dv * 512:(dv + 1) * 512],
                                         bank.ap[:], bvrows[:, dv * 512:(dv + 1) * 512],
                                         waits=[tok, BVROWS])
                            bank.free_tok = et
                ring.release(lid, plan.tok("tensor") if j < 4 else plan.tok("vector"))
            plan.barrier()
            if q == 0:
                tap("zq", work[:, 16384:24576], [plan.tok("vector")])
                tap("kv", work[:, 0:16384], [plan.tok("vector")])

            qkv_done = plan.tok("tensor")
            o_mm_tok = [None, None]

            def emit_st(ch, lt):
                QTl = ch * 4 + lt
                QTg = q * 8 + QTl
                pairs_kt = [kt for kt in range(QTg - 2, QTg + 1) if kt >= 0]
                r = QTl % 2
                st_toks = []
                for pi, ktg in enumerate(pairs_kt):
                    if ktg >= q * 8:
                        kv_ = kslots[:, ktg - q * 8, :].rearrange(
                            "p (dc t) -> p dc t", t=128)
                    else:
                        kv_ = kh[:, ktg - (q * 8 - 2), :].rearrange(
                            "p (dc t) -> p dc t", t=128)
                    bank = takeA()
                    tok = mm_acc(bank.ap[:, 0:128],
                                 [(kv_[:, dc, :],
                                   zq[:, dc, ch * 512 + lt * 128:
                                      ch * 512 + (lt + 1) * 128])
                                  for dc in range(8)],
                                 waits=[bank.free_tok])
                    ewl = [tok, o_mm_tok[r]]
                    if ktg == QTg:
                        et = plan.op("vector", "tensor_mul", zst[r][:, pi, :],
                                     bank.ap[:, 0:128], maskC[:], waits=ewl)
                    elif ktg == QTg - 1:
                        et = plan.op("scalar", "activation", zst[r][:, pi, :],
                                     bank.ap[:, 0:128], AF.Copy, scale=SCALE,
                                     waits=ewl)
                    else:
                        et = plan.op("vector", "tensor_mul", zst[r][:, pi, :],
                                     bank.ap[:, 0:128], maskA[:], waits=ewl)
                    bank.free_tok = et
                    st_toks.append(et)
                return pairs_kt, r, st_toks

            def emit_o(ch, lt, pairs_kt, r, st_toks):
                ob = takeO()
                plan.wait("tensor", [ob.free_tok] + st_toks)
                otok = plan._bump("tensorprog", 1)
                plan.rec("tensor",
                         reads=[vslots[:, :, :], vh[:, :, :], zst[r][:, :, :]],
                         writes=[ob.ap[:]], incs=[("tensorprog", otok[1])],
                         tag="o_mm")

                def go_o(t, _pairs=tuple(pairs_kt), _q=q, _r=r, _ap=ob.ap, _p=plan):
                    ins = None
                    for dc in range(8):
                        for pi, ktg in enumerate(_pairs):
                            if ktg >= _q * 8:
                                vv = vslots[:, ktg - _q * 8, :]
                            else:
                                vv = vh[:, ktg - (_q * 8 - 2), :]
                            ins = t.matmul(_ap[:, dc * 128:(dc + 1) * 128],
                                           vv[:, dc * 128:(dc + 1) * 128],
                                           zst[_r][:, pi, :],
                                           start=(pi == 0),
                                           stop=(pi == len(_pairs) - 1))
                    ins.then_inc(_p.handle("tensorprog"), 1)
                plan.ops["tensor"].append(go_o)
                o_mm_tok[r] = ("tensorprog", otok[1])
                et = plan.op("vector", "tensor_copy",
                             osb[:, :, lt * 128:(lt + 1) * 128],
                             ob.ap[:].rearrange("p (dc t) -> p dc t", t=128),
                             waits=[o_mm_tok[r]])
                ob.free_tok = et

            def attn_chunk(ch):
                pend = emit_st(ch, 0)
                for lt in range(4):
                    nxt = emit_st(ch, lt + 1) if lt < 3 else None
                    emit_o(ch, lt, *pend)
                    pend = nxt

            def attn_ln(ch):
                tsl = slice(ch * 512, (ch + 1) * 512)

                def src_o(fc):
                    return (osb[:, fc, :], False)

                def out_attn(fc, tile, tok, prev, _tsl=tsl):
                    u = plan.op("vector", "tensor_scalar", gtmp[:, :], tile,
                                dcols[:, DC_GW + fc:DC_GW + fc + 1],
                                dcols[:, DC_GB2 + fc:DC_GB2 + fc + 1],
                                ALU.mult, ALU.add, waits=[tok, prev, GATES])
                    return plan.op("vector", "tensor_tensor", xres[:, fc, _tsl],
                                   xres[:, fc, _tsl], gtmp[:, :], ALU.add,
                                   waits=[u])

                ln(512, slice(0, 512), src_o, out_attn,
                   pre_toks=[plan.tok("vector")])

            h2tok = [None, None]

            def ln2_chunk(ch):
                csl = slice(ch * 512, (ch + 1) * 512)

                def out_h2(fc, tile, tok, prev, _csl=csl):
                    return plan.op("vector", "tensor_scalar", hbuf[:, fc, _csl],
                                   tile,
                                   dcols[:, DC_CBPL + fc:DC_CBPL + fc + 1],
                                   modc[:, 24 + fc:24 + fc + 1],
                                   ALU.mult, ALU.add, waits=[tok, GATES, qkv_done])

                ln(512, csl, src_x, out_h2, pre_toks=[plan.tok("vector"), GATES])
                h2tok[ch] = plan.tok("vector")

            plan.mark("attn0")
            attn_chunk(0)
            if q == 0:
                tap("osb", work[:, 25600:29696], [plan.tok("vector")])
            plan.mark("attnln0")
            attn_ln(0)
            plan.mark("attn1")
            attn_chunk(1)
            plan.mark("ln2_0")
            ln2_chunk(0)
            plan.mark("attnln1")
            attn_ln(1)
            plan.mark("ln2_1")
            ln2_chunk(1)
            tap(f"x1_q{q}", xres[:, :, :], [plan.tok("vector")])
            tap(f"h2_q{q}", hbuf[:, :, :], [plan.tok("vector")])

            ttk = plan.tok("tensor")
            for i in range(2):
                plan.op("vector", "tensor_copy", kh[:, i, :], kslots[:, 6 + i, :],
                        waits=[ttk])
                plan.op("vector", "tensor_copy", vh[:, i, :], vslots[:, 6 + i, :])

            gate_prev = [plan.tok("vector")]

            def make_wb(e):
                ob = takeO()
                plan.wait("tensor", [ob.free_tok, w32t, CONST])
                tok = plan._bump("tensorprog", 1)
                plan.rec("tensor", reads=[hot[:, :], w32[:, :]], writes=[ob.ap[:]],
                         incs=[("tensorprog", tok[1])], tag="wb_mm")

                def go(t, _e=e, _ap=ob.ap, _p=plan):
                    ins = None
                    for t8 in range(8):
                        qrow = t8 * 4 + _e
                        ins = t.matmul(_ap[:, t8 * 128:(t8 + 1) * 128],
                                       hot[:, qrow * 128:(qrow + 1) * 128], w32[:],
                                       start=True, stop=True)
                    ins.then_inc(_p.handle("tensorprog"), 1)
                plan.ops["tensor"].append(go)
                et = plan.op("vector", "tensor_copy", wbg[:], ob.ap[:],
                             waits=[("tensorprog", tok[1]), gate_prev[0]])
                ob.free_tok = et
                return et

            def gate_evac(bank_ap, fc, n, bias_col, wb_tok, mmtok):
                nsl = slice(n * 512, (n + 1) * 512)
                wl = [mmtok, wb_tok, gate_prev[0], GATES]
                if bias_col is not None:
                    g1 = plan.op("vector", "scalar_tensor_tensor", gtmp[:, :],
                                 bank_ap, bias_col, wbg[:, nsl],
                                 ALU.add, ALU.mult, waits=wl)
                else:
                    g1 = plan.op("vector", "tensor_mul", gtmp[:, :],
                                 bank_ap, wbg[:, nsl], waits=wl)
                g2 = plan.op("vector", "scalar_tensor_tensor", xres[:, fc, nsl],
                             gtmp[:, :], dcols[:, DC_GB + fc:DC_GB + fc + 1],
                             xres[:, fc, nsl], ALU.mult, ALU.add, waits=[g1])
                gate_prev[0] = g2
                return g2

            h_rhs = lambda kc, nsl: hbuf[:, kc, nsl]

            def dense_stage(w_ids, rhs_view, out_cb):
                toks = []
                for fc in range(FC):
                    for n in range(2):
                        nsl = slice(n * 512, (n + 1) * 512)
                        lid = w_ids[fc // 4]
                        off = (fc % 4) * 128
                        wv = ring.view(lid)
                        bank = takeA()
                        tok = mm_acc(bank.ap[:],
                                     [(wv[:, kc, off:off + 128], rhs_view(kc, nsl))
                                      for kc in range(8)],
                                     waits=[ring.use_tok(lid), bank.free_tok])
                        toks.append(out_cb(fc, n, bank, tok))
                return toks

            def evac_to(chain_view, bias_row):
                def cb(fc, n, bank, tok):
                    et = plan.op("scalar", "activation",
                                 chain_view[:, fc, n * 512:(n + 1) * 512],
                                 bank.ap[:], AF.Identity,
                                 bias=vecs[:, bias_row, fc:fc + 1], waits=[tok])
                    bank.free_tok = et
                    return et
                return cb

            def big_out(w_ids, rhs_fcview, nkc, out_cb, rhs_rec=None):
                assert len(w_ids) <= 2
                for fc in range(FC):
                    for n in range(2):
                        nsl = slice(n * 512, (n + 1) * 512)
                        bank = takeA()
                        plan.wait("tensor", [bank.free_tok] +
                                  [ring.use_tok(i) for i in w_ids])
                        tok = plan._bump("tensorprog", 1)
                        plan.rec("tensor",
                                 reads=[ring.view(i) for i in w_ids] +
                                       [rhs_rec if rhs_rec is not None
                                        else work[:, :]],
                                 writes=[bank.ap[:]],
                                 incs=[("tensorprog", tok[1])], tag="bigout_mm")

                        def go(t, _fc=fc, _nsl=nsl, _ap=bank.ap, _ids=tuple(w_ids),
                               _rv=rhs_fcview, _nkc=nkc, _p=plan):
                            ins = None
                            per = _nkc // len(_ids)
                            for seg, lid in enumerate(_ids):
                                wv = ring.view(lid)
                                for s in range(per):
                                    kk = seg * per + s
                                    ins = t.matmul(_ap,
                                                   wv[:, s, _fc * 128:(_fc + 1) * 128],
                                                   _rv(kk, _nsl),
                                                   start=(kk == 0), stop=(kk == _nkc - 1))
                            ins.then_inc(_p.handle("tensorprog"), 1)
                        plan.ops["tensor"].append(go)
                        out_cb(fc, n, bank, ("tensorprog", tok[1]))

            def rounds_out(w_dram, rhs_fcview, gate_cb):
                for r in range(2):
                    tb = [takeA() for _ in range(4)] + [takeO(), takeO()]
                    aps = [tb[i].ap for i in range(4)] + \
                          [tb[4].ap[:, 0:512], tb[4].ap[:, 512:1024],
                           tb[5].ap[:, 0:512], tb[5].ap[:, 512:1024]]
                    tiles = [(4 * r + i, n) for i in range(4) for n in range(2)]
                    base = plan.cnt["tensorprog"]
                    seg_ids = []
                    for s in range(4):
                        seg_ids.append(ring.plan_load(
                            w_src(w_dram, s * 1024, (s + 1) * 1024,
                                  r * 512, (r + 1) * 512), 8, 512))
                        ring.release(seg_ids[s], ("tensorprog", base + s + 1))
                    plan.wait("tensor", [b.free_tok for b in tb] +
                              [ring.use_tok(seg_ids[0]), plan.tok("vector"),
                               plan.tok("scalar")])
                    plan._bump("tensorprog", 4)
                    segw = [ring.use_tok(i) for i in seg_ids]
                    for seg in range(4):
                        if seg > 0:
                            plan.log.append(("wait", "tensor", ((segw[seg][0],
                                                                segw[seg][1]),)))
                        plan.rec("tensor",
                                 reads=[ring.view(seg_ids[seg]), work[:, :]],
                                 writes=list(aps),
                                 incs=[("tensorprog", base + seg + 1)],
                                 tag="rounds_mm")

                    def go(t, _tiles=tuple(tiles), _aps=tuple(aps),
                           _ids=tuple(seg_ids), _rv=rhs_fcview,
                           _segw=tuple(segw), _p=plan, _r=r):
                        for seg in range(4):
                            if seg > 0:
                                t.wait_ge(_p.handle(_segw[seg][0]), _segw[seg][1])
                            wv = ring.view(_ids[seg])
                            ins = None
                            for s in range(8):
                                kk = seg * 8 + s
                                for ti, (outf, n) in enumerate(_tiles):
                                    ins = t.matmul(
                                        _aps[ti],
                                        wv[:, s, (outf - 4 * _r) * 128:
                                           (outf - 4 * _r + 1) * 128],
                                        _rv(kk, slice(n * 512, (n + 1) * 512)),
                                        start=(kk == 0), stop=(kk == 31))
                            ins.then_inc(_p.handle("tensorprog"), 1)
                    plan.ops["tensor"].append(go)
                    done = ("tensorprog", base + 4)
                    for ti, (outf, n) in enumerate(tiles):
                        g = gate_cb(aps[ti], outf, n, done)
                        if ti < 4:
                            tb[ti].free_tok = g
                        else:
                            tb[4 + (ti - 4) // 2].free_tok = g


            u_rhs = lambda kk, nsl: uview[:, kk, nsl]

            u_toks = [None] * 3
            rwq = [None]

            def emit_router():
                rt_id = ring.plan_load(w_src(w_router, 0, D, 0, 4), 8, 4)
                rtv = ring.view(rt_id)
                bank = takeA()
                plan.wait("tensor", [bank.free_tok, ring.use_tok(rt_id), h2tok[1]])
                lt_tok = plan._bump("tensorprog", 1)
                plan.rec("tensor", reads=[hbuf[:, :, :], rtv[:, :, :]],
                         writes=[bank.ap[:, 0:32]],
                         incs=[("tensorprog", lt_tok[1])], tag="router_mm")

                def go_rt(t, _ap=bank.ap, _rtv=rtv, _p=plan):
                    ins = None
                    for t8 in range(8):
                        for kc in range(8):
                            ins = t.matmul(_ap[:, t8 * 4:(t8 + 1) * 4],
                                           hbuf[:, kc, t8 * 128:(t8 + 1) * 128],
                                           _rtv[:, kc, 0:4],
                                           start=(kc == 0), stop=(kc == 7))
                    ins.then_inc(_p.handle("tensorprog"), 1)
                plan.ops["tensor"].append(go_rt)
                ring.release(rt_id, ("tensorprog", lt_tok[1]))
                lg = plan.op("vector", "tensor_add", wtm[:], bank.ap[:, 0:32], rblr[:],
                             waits=[("tensorprog", lt_tok[1]), RBL])
                bank.free_tok = lg
                l_ = lambda e: wtm[:, e:32:4]
                ra = lambda i: rmath[:, i, :]
                ta = plan.op("vector", "tensor_tensor", ra(0), l_(0), l_(1), ALU.max,
                             waits=[lg])
                tb = plan.op("vector", "tensor_tensor", ra(1), l_(0), l_(1), ALU.min)
                tc_ = plan.op("vector", "tensor_tensor", ra(2), l_(2), l_(3), ALU.max)
                td = plan.op("vector", "tensor_tensor", ra(3), l_(2), l_(3), ALU.min)
                t4 = plan.op("vector", "tensor_tensor", ra(4), ra(0), ra(2), ALU.max,
                             waits=[ta, tc_])
                t5 = plan.op("vector", "tensor_tensor", ra(5), ra(0), ra(2), ALU.min)
                t6 = plan.op("vector", "tensor_tensor", ra(6), ra(1), ra(3), ALU.max,
                             waits=[tb, td])
                t7 = plan.op("vector", "tensor_tensor", ra(7), ra(5), ra(6), ALU.max,
                             waits=[t5, t6])
                t8_ = plan.op("vector", "tensor_tensor", ra(8), ra(7), ra(4), ALU.subtract,
                              waits=[t4, t7])
                e2_ = plan.op("scalar", "activation", ra(8), ra(8), AF.Exp, waits=[t8_])
                t9 = plan.op("vector", "tensor_scalar_add", ra(8), ra(8), 1.0, waits=[e2_])
                t10 = plan.op("vector", "reciprocal", ra(9), ra(8), waits=[t9])
                wq_toks = rwq[0] = []
                for e in range(4):
                    de = plan.op("vector", "tensor_tensor", ra(10), l_(e), ra(4),
                                 ALU.subtract, waits=[t4, t10] + wq_toks[-1:])
                    ee = plan.op("scalar", "activation", ra(10), ra(10), AF.Exp, waits=[de])
                    se = plan.op("vector", "tensor_tensor", ra(11), l_(e), ra(7),
                                 ALU.is_ge, waits=[t7])
                    m1_ = plan.op("vector", "tensor_tensor", ra(10), ra(10), ra(11),
                                  ALU.mult, waits=[ee, se])
                    wq_toks.append(plan.op("vector", "tensor_tensor", wq[:, e:32:4],
                                           ra(10), ra(9), ALU.mult, waits=[m1_]))
                if q == 0:
                    tap("wtm", wtm[:], [plan.tok("vector")])
                    tap("wq", wq[:], [plan.tok("vector")])

            plan.mark("e0u")
            for hf in range(32):
                if hf % 4 == 0:
                    j = hf // 4
                    lid1 = load_cols(sw_w1, j * 512, (j + 1) * 512)
                    lid3 = load_cols(sw_w3, j * 512, (j + 1) * 512)
                off = (hf % 4) * 128
                w1v, w3v = ring.view(lid1), ring.view(lid3)
                for n in range(2):
                    nsl = slice(n * 512, (n + 1) * 512)
                    b1, b3 = takeA(), takeA()
                    t1 = mm_acc(b1.ap[:],
                                [(w1v[:, kc, off:off + 128], hbuf[:, kc, nsl])
                                 for kc in range(8)],
                                waits=[ring.use_tok(lid1), b1.free_tok, h2tok[n]])
                    t3 = mm_acc(b3.ap[:],
                                [(w3v[:, kc, off:off + 128], hbuf[:, kc, nsl])
                                 for kc in range(8)],
                                waits=[ring.use_tok(lid3), b3.free_tok])
                    si = (2 * hf + n) % 3
                    ss = stmp[:, si, :]
                    st = plan.op("scalar", "activation", ss, b1.ap[:], AF.Silu,
                                 waits=[t1, u_toks[si]])
                    b1.free_tok = st
                    ut = plan.op("vector", "tensor_mul", uview[:, hf, nsl],
                                 ss, b3.ap[:], waits=[st, t3])
                    b3.free_tok = ut
                    u_toks[si] = ut
                if hf % 4 == 3:
                    ring.release(lid1, plan.tok("tensor"))
                    ring.release(lid3, plan.tok("tensor"))
                if hf == 3:
                    emit_router()
            wq_toks = rwq[0]

            bank = takeA()
            trt = plan.op("tensor", "transpose", bank.ap[0:32, 0:128], wq[:], ident[:],
                          waits=wq_toks + [bank.free_tok, CONST], inc="tensorprog")
            w32t = plan.op("vector", "tensor_copy", w32[:], bank.ap[0:32, 0:128],
                           waits=[trt])
            bank.free_tok = w32t
            wb_tok = make_wb(0)
            plan.barrier()
            if q == 0:
                tap("u0", work[:, 0:32768], [plan.tok("vector")])

            def gate_cb_e0(ap, outf, n, tok, _wb=wb_tok):
                return gate_evac(ap, outf, n, None, _wb, tok)

            plan.mark("e0out")
            rounds_out(sw_w2, u_rhs, gate_cb_e0)
            plan.barrier()

            plan.mark("e1")
            wb_tok = make_wb(1)
            c0, c1v, c2v = chain(0), chain(8192), chain(16384)
            gz = work[:, 0:16384].rearrange("p (fc t) -> p fc t", t=1024)
            eq_ids = [load_cols(me_eq, j * 512, (j + 1) * 512) for j in range(2)]
            dense_stage(eq_ids, h_rhs, evac_to(c2v, V_EQB))
            for i in eq_ids:
                ring.release(i, plan.tok("tensor"))
            plan.barrier()
            wv_ids = [load_cols(me_wv, j * 512, (j + 1) * 512) for j in range(2)]
            c2_rhs = lambda kc, nsl: c2v[:, kc, nsl]
            dense_stage(wv_ids, c2_rhs, evac_to(c1v, V_MEBV))
            for i in wv_ids:
                ring.release(i, plan.tok("tensor"))
            plan.barrier()
            wo_ids = [load_cols(me_wo, j * 512, (j + 1) * 512) for j in range(2)]
            c1_rhs = lambda kc, nsl: c1v[:, kc, nsl]
            dense_stage(wo_ids, c1_rhs, evac_to(c2v, V_MEBO))
            for i in wo_ids:
                ring.release(i, plan.tok("tensor"))
            plan.barrier()
            c1_ids = [load_cols(me_c1, j * 512, (j + 1) * 512) for j in range(4)]
            for fc in range(16):
                for n in range(2):
                    nsl = slice(n * 512, (n + 1) * 512)
                    lid = c1_ids[fc // 4]
                    off = (fc % 4) * 128
                    wv = ring.view(lid)
                    bank = takeA()
                    tok = mm_acc(bank.ap[:],
                                 [(wv[:, kc, off:off + 128], c2v[:, kc, nsl])
                                  for kc in range(8)],
                                 waits=[ring.use_tok(lid), bank.free_tok])
                    et = plan.op("scalar", "activation", gz[:, fc, nsl], bank.ap[:],
                                 AF.Gelu,
                                 bias=vecs[:, V_C1B0 + fc // 8, fc % 8:fc % 8 + 1],
                                 waits=[tok])
                    bank.free_tok = et
            for i in c1_ids:
                ring.release(i, plan.tok("tensor"))
            plan.barrier()
            c2_ids = [load_rows(me_c2, j * 1024, (j + 1) * 1024) for j in range(2)]
            gz_rhs = lambda kk, nsl: gz[:, kk, nsl]

            def cb_e1(fc, n, bank, tok, _wb=wb_tok):
                g = gate_evac(bank.ap[:], fc, n, vecs[:, V_C2B, fc:fc + 1], _wb, tok)
                bank.free_tok = g
                return g

            big_out(c2_ids, gz_rhs, 16, cb_e1, rhs_rec=work[:, 0:16384])
            for i in c2_ids:
                ring.release(i, plan.tok("vector"))
            plan.barrier()

            plan.mark("e2")
            wb_tok = make_wb(2)
            syn_ids = [load_cols(ce_syn, j * 512, (j + 1) * 512) for j in range(2)]
            dense_stage(syn_ids, h_rhs, evac_to(c0, V_SYNB))
            for i in syn_ids:
                ring.release(i, plan.tok("tensor"))
            plan.barrier()
            cewv_ids = [load_cols(ce_wv, j * 512, (j + 1) * 512) for j in range(2)]
            c0_rhs = lambda kc, nsl: c0[:, kc, nsl]
            dense_stage(cewv_ids, c0_rhs, evac_to(c1v, V_CEBV))
            for i in cewv_ids:
                ring.release(i, plan.tok("tensor"))
            plan.barrier()
            cewo_ids = [load_cols(ce_wo, j * 512, (j + 1) * 512) for j in range(2)]

            def cb_ssum(fc, n, bank, tok):
                nsl = slice(n * 512, (n + 1) * 512)
                et = plan.op("vector", "scalar_tensor_tensor", c2v[:, fc, nsl],
                             bank.ap[:], vecs[:, V_CEBO, fc:fc + 1], c0[:, fc, nsl],
                             ALU.add, ALU.add, waits=[tok])
                bank.free_tok = et
                return et

            dense_stage(cewo_ids, c1_rhs, cb_ssum)
            for i in cewo_ids:
                ring.release(i, plan.tok("tensor"))
            plan.barrier()

            def src_ssum(fc):
                return (c2v[:, fc, :], False)
            plan.mark("n1")
            h2a_tok = [None, None]
            for chnk in range(2):
                csl = slice(chnk * 512, (chnk + 1) * 512)
                ln(512, csl, src_ssum,
                   lambda fc, tile, tok, prev, _csl=csl:
                       plan.op("vector", "tensor_scalar", c0[:, fc, _csl], tile,
                               vecs[:, V_N1G, fc:fc + 1], vecs[:, V_N1B, fc:fc + 1],
                               ALU.mult, ALU.add, waits=[tok]),
                   pre_toks=[plan.tok("vector")])
                h2a_tok[chnk] = plan.tok("vector")
            gz2 = work[:, 8192:24576].rearrange("p (fc t) -> p fc t", t=1024)
            f1_ids = [load_cols(ce_f1, j * 512, (j + 1) * 512) for j in range(4)]
            for n in range(2):
                nsl = slice(n * 512, (n + 1) * 512)
                for fc in range(16):
                    lid = f1_ids[fc // 4]
                    off = (fc % 4) * 128
                    wv = ring.view(lid)
                    bank = takeA()
                    tok = mm_acc(bank.ap[:],
                                 [(wv[:, kc, off:off + 128], c0[:, kc, nsl])
                                  for kc in range(8)],
                                 waits=[ring.use_tok(lid), bank.free_tok,
                                        h2a_tok[n]])
                    et = plan.op("scalar", "activation", gz2[:, fc, nsl], bank.ap[:],
                                 AF.Relu,
                                 bias=vecs[:, V_F1B0 + fc // 8, fc % 8:fc % 8 + 1],
                                 waits=[tok])
                    bank.free_tok = et
            for i in f1_ids:
                ring.release(i, plan.tok("tensor"))
            plan.barrier()
            c3 = work[:, 24576:32768].rearrange("p (fc t) -> p fc t", t=1024)
            f2_ids = [load_rows(ce_f2, j * 1024, (j + 1) * 1024) for j in range(2)]
            gz2_rhs = lambda kk, nsl: gz2[:, kk, nsl]

            def cb_s2(fc, n, bank, tok):
                nsl = slice(n * 512, (n + 1) * 512)
                et = plan.op("vector", "scalar_tensor_tensor", c3[:, fc, nsl],
                             bank.ap[:], vecs[:, V_F2B, fc:fc + 1], c0[:, fc, nsl],
                             ALU.add, ALU.add, waits=[tok])
                bank.free_tok = et
                return et

            big_out(f2_ids, gz2_rhs, 16, cb_s2, rhs_rec=work[:, 8192:24576])
            for i in f2_ids:
                ring.release(i, plan.tok("vector"))
            plan.barrier()
            h2b = work[:, 8192:16384].rearrange("p (fc t) -> p fc t", t=1024)

            def src_s2(fc):
                return (c3[:, fc, :], False)

            plan.mark("n2")
            h2b_tok = [None, None]
            for chnk in range(2):
                csl = slice(chnk * 512, (chnk + 1) * 512)
                ln(512, csl, src_s2,
                   lambda fc, tile, tok, prev, _csl=csl:
                       plan.op("vector", "tensor_scalar", h2b[:, fc, _csl], tile,
                               vecs[:, V_N2G, fc:fc + 1], vecs[:, V_N2B, fc:fc + 1],
                               ALU.mult, ALU.add, waits=[tok]),
                   pre_toks=[plan.tok("vector")])
                h2b_tok[chnk] = plan.tok("vector")
            gen_ids = [load_cols(ce_gen, j * 512, (j + 1) * 512) for j in range(2)]
            for n in range(2):
                nsl = slice(n * 512, (n + 1) * 512)
                for fc in range(FC):
                    lid = gen_ids[fc // 4]
                    off = (fc % 4) * 128
                    wv = ring.view(lid)
                    bank = takeA()
                    tok = mm_acc(bank.ap[:],
                                 [(wv[:, kc, off:off + 128], h2b[:, kc, nsl])
                                  for kc in range(8)],
                                 waits=[ring.use_tok(lid), bank.free_tok,
                                        h2b_tok[n]])
                    g = gate_evac(bank.ap[:], fc, n, vecs[:, V_GENB, fc:fc + 1],
                                  wb_tok, tok)
                    bank.free_tok = g
            for i in gen_ids:
                ring.release(i, plan.tok("vector"))
            plan.barrier()

            plan.mark("e3")
            wb_tok = make_wb(3)
            for hf in range(32):
                if hf % 4 == 0:
                    lid = load_cols(ml_w1, (hf // 4) * 512, (hf // 4 + 1) * 512)
                off = (hf % 4) * 128
                wv = ring.view(lid)
                for n in range(2):
                    nsl = slice(n * 512, (n + 1) * 512)
                    bank = takeA()
                    tok = mm_acc(bank.ap[:],
                                 [(wv[:, kc, off:off + 128], hbuf[:, kc, nsl])
                                  for kc in range(8)],
                                 waits=[ring.use_tok(lid), bank.free_tok])
                    et = plan.op("scalar", "activation", uview[:, hf, nsl], bank.ap[:],
                                 AF.Gelu,
                                 bias=vecs[:, V_MLB1 + hf // 8, hf % 8:hf % 8 + 1],
                                 waits=[tok])
                    bank.free_tok = et
                if hf % 4 == 3:
                    ring.release(lid, plan.tok("scalar"))
            plan.barrier()

            def gate_cb_e3(ap, outf, n, tok, _wb=wb_tok):
                return gate_evac(ap, outf, n, vecs[:, V_MLB2, outf:outf + 1],
                                 _wb, tok)

            plan.mark("e3out")
            rounds_out(ml_w2, u_rhs, gate_cb_e3)
            xdone = plan.tok("vector")

            yt = plan.op("sync", "dma_start",
                         out=y_out[:, q * TQ:(q + 1) * TQ].rearrange(
                             "(fc p) t -> p fc t", p=128),
                         in_=xres[:, :, :],
                         waits=[xdone], inc="ystore", inc_k=16)
            prev_yhalf = [yt, yt]
            plan.barrier()

        tap("fin_modc", modc[:], [plan.tok("vector")])
        tap("fin_dcols", dcols[:], [plan.tok("vector")])
        tap("fin_rblr", rblr[:], [plan.tok("vector")])
        tap("fin_bvrows", bvrows[:], [plan.tok("vector")])
        tap("fin_maskA", maskA[:], [plan.tok("vector")])
        tap("fin_ones", ones[:], [plan.tok("vector")])
        tap("fin_vecs", vecs[:], [plan.tok("vector")])
        tap("fin_silc", silc[:], [plan.tok("vector")])
        tap("fin_epsc", epsc[:], [plan.tok("vector")])
        plan.wait("sync", [("ystore", plan.cnt["ystore"])] +
                  ([("tapst", plan.cnt["tapst"])] if "tapst" in plan.cnt else []))

        for n in sorted(plan.cnt.keys()):
            plan.handles[n] = es.enter_context(nc.semaphore(n))
        block = es.enter_context(nc.Block())
        plan.emit(block)

    LAST_MARKS[:] = plan.marks
    LAST_LOG[:] = plan.log
    return nc, din


_CACHE = {}


def _host_inputs(inputs):
    x = np.asarray(inputs["L0_attn_x"])
    c = np.asarray(inputs["L0_c"])
    vec_rows = [
        inputs["L0_attn_ln_w"], inputs["L0_attn_ln_b"],
        inputs["L0_ffn_me_eq_b"], inputs["L0_ffn_me_bv"], inputs["L0_ffn_me_bo"],
        inputs["L0_ffn_me_c2b"],
        inputs["L0_ffn_ce_syn_b"], inputs["L0_ffn_ce_bv"], inputs["L0_ffn_ce_bo"],
        inputs["L0_ffn_ce_n1g"], inputs["L0_ffn_ce_n1b"], inputs["L0_ffn_ce_f2b"],
        inputs["L0_ffn_ce_n2g"], inputs["L0_ffn_ce_n2b"], inputs["L0_ffn_ce_gen_b"],
        inputs["L0_ffn_ml_b2"],
        inputs["L0_attn_b_qkv"][0:1024], inputs["L0_attn_b_qkv"][1024:2048],
        inputs["L0_attn_b_qkv"][2048:3072],
        inputs["L0_ffn_me_c1b"][0:1024], inputs["L0_ffn_me_c1b"][1024:2048],
        inputs["L0_ffn_ce_f1b"][0:1024], inputs["L0_ffn_ce_f1b"][1024:2048],
        inputs["L0_ffn_ml_b1"][0:1024], inputs["L0_ffn_ml_b1"][1024:2048],
        inputs["L0_ffn_ml_b1"][2048:3072], inputs["L0_ffn_ml_b1"][3072:4096],
    ]
    vecs = np.stack([np.asarray(r, dtype=np.float32) for r in vec_rows])
    vecs_fm = np.ascontiguousarray(vecs.reshape(27, 8, 128).transpose(2, 0, 1))
    onesb = np.ones((128, 128), dtype=ml_dtypes.bfloat16)
    ident = np.eye(128, dtype=np.float32)
    p = np.arange(128)[:, None]
    qc = np.arange(128)[None, :]
    maskA = ((qc < p).astype(np.float32)) * np.float32(SCALE)
    maskC = ((qc >= p).astype(np.float32)) * np.float32(SCALE)
    hotrows = np.zeros((32, 4096), dtype=ml_dtypes.bfloat16)
    for qrow in range(32):
        hotrows[qrow, qrow * 128:(qrow + 1) * 128] = 1
    shared = dict(
        w_qkv=np.asarray(inputs["L0_attn_W_qkv"]),
        w_mod=np.asarray(inputs["L0_wmod"]),
        w_router=np.asarray(inputs["L0_ffn_router_w"]),
        rb=np.asarray(inputs["L0_ffn_router_b"]).reshape(1, 4),
        lb=np.asarray(inputs["L0_ffn_load_balancer"]).reshape(1, 4),
        sw_w1=np.asarray(inputs["L0_ffn_sw_w1"]),
        sw_w3=np.asarray(inputs["L0_ffn_sw_w3"]),
        sw_w2=np.asarray(inputs["L0_ffn_sw_w2"]),
        me_eq=np.asarray(inputs["L0_ffn_me_eq_w"]),
        me_wv=np.asarray(inputs["L0_ffn_me_wv"]),
        me_wo=np.asarray(inputs["L0_ffn_me_wo"]),
        me_c1=np.asarray(inputs["L0_ffn_me_c1w"]),
        me_c2=np.asarray(inputs["L0_ffn_me_c2w"]),
        ce_syn=np.asarray(inputs["L0_ffn_ce_syn_w"]),
        ce_wv=np.asarray(inputs["L0_ffn_ce_wv"]),
        ce_wo=np.asarray(inputs["L0_ffn_ce_wo"]),
        ce_f1=np.asarray(inputs["L0_ffn_ce_f1w"]),
        ce_f2=np.asarray(inputs["L0_ffn_ce_f2w"]),
        ce_gen=np.asarray(inputs["L0_ffn_ce_gen_w"]),
        ml_w1=np.asarray(inputs["L0_ffn_ml_w1"]),
        ml_w2=np.asarray(inputs["L0_ffn_ml_w2"]),
        vecs_fm=vecs_fm,
        bvrow=np.ascontiguousarray(
            np.asarray(inputs["L0_attn_b_qkv"][2048:3072])).reshape(1, 1024),
        ones_bf=onesb, ident=ident, maskA=maskA, maskC=maskC, hotrows=hotrows,
    )
    in_maps = []
    for b in range(B):
        m = dict(shared)
        m["x_fm"] = np.ascontiguousarray(np.asarray(x[b]).T)
        m["c_fm"] = np.ascontiguousarray(np.asarray(c[b]).reshape(8, 128).T)
        in_maps.append(m)
    return in_maps


def kernel(**inputs):
    if "nc" not in _CACHE:
        _CACHE["nc"] = build_nc()
    nc, _ = _CACHE["nc"]
    in_maps = _host_inputs(inputs)
    res = run_bass_kernel_spmd(nc, in_maps, core_ids=list(range(B)))
    out = np.stack([np.asarray(res.results[b]["y_fm"]).T for b in range(B)])
    return np.ascontiguousarray(out.astype(np.float32))


# version 11
# speedup vs baseline: 1.0395x; 1.0024x over previous
import contextlib
import numpy as np
import ml_dtypes
import concourse.bass as bass
import concourse.mybir as mybir
from concourse.bass_utils import run_bass_kernel_spmd

F32 = mybir.dt.float32
BF16 = mybir.dt.bfloat16
AF = mybir.ActivationFunctionType
ALU = mybir.AluOpType

B, T, D = 8, 4096, 1024
NQ = 4
TQ = T // NQ
FC = D // 128
EPS = 1e-5
SCALE = 1.0 / 512.0
RINV = 1.0 / 1024.0

V_LNW, V_LNB = 0, 1
V_EQB, V_MEBV, V_MEBO, V_C2B = 2, 3, 4, 5
(V_SYNB, V_CEBV, V_CEBO, V_N1G, V_N1B, V_F2B, V_N2G, V_N2B, V_GENB,
 V_MLB2) = 6, 7, 8, 9, 10, 11, 12, 13, 14, 15
V_BQ, V_BK, V_BV = 16, 17, 18
V_C1B0 = 19
V_F1B0 = 21
V_MLB1 = 23

DC_CAPL, DC_CBPL, DC_GA, DC_GB, DC_GW, DC_GB2 = 0, 8, 16, 24, 32, 40

ENGS = ("tensor", "vector", "scalar", "gpsimd", "sync")
LAST_MARKS = []
LAST_LOG = []


class Plan:

    def __init__(self, nc):
        self.nc = nc
        self.ops = {e: [] for e in ENGS}
        self.cnt = {}
        self.seen = {e: {} for e in ENGS}
        self.handles = {}
        self.marks = []
        self.mm_count = [0]
        self.log = []

    def mark(self, label):
        def go(e, _l=label, _p=self):
            _p.marks.append((_p.mm_count[0], _l))
        self.ops["tensor"].append(go)

    def sem(self, name):
        if name not in self.cnt:
            self.cnt[name] = 0
        return name

    def handle(self, name):
        return self.handles[name]

    def _bump(self, name, k):
        self.sem(name)
        self.cnt[name] += k
        return (name, self.cnt[name])

    def tok(self, eng):
        return (eng + "prog", self.cnt.get(eng + "prog", 0))

    def wait(self, eng, toks):
        best = {}
        for t in toks:
            if t is None:
                continue
            n, v = t
            if v > 0 and v > best.get(n, 0):
                best[n] = v
        todo = []
        for n, v in sorted(best.items()):
            if self.seen[eng].get(n, 0) < v:
                self.sem(n)
                todo.append((n, v))
                self.seen[eng][n] = v
        if todo:
            self.log.append(("wait", eng, tuple(todo)))

            def go(e, _todo=tuple(todo), _p=self):
                for n, v in _todo:
                    e.wait_ge(_p.handle(n), v)
            self.ops[eng].append(go)

    def op(self, eng, fn_name, *args, waits=(), inc=None, inc_k=1, **kw):
        self.wait(eng, list(waits))
        sem_name = inc if inc is not None else eng + "prog"
        tok = self._bump(sem_name, inc_k)
        aps = [a for a in args if hasattr(a, "ap") and hasattr(a, "tensor")]
        wr, rd = [], []
        if "out" in kw:
            wr.append(kw["out"])
        if "in_" in kw:
            rd.append(kw["in_"])
        if aps:
            wr.append(aps[0])
            rd.extend(aps[1:])
        for k2 in ("bias", "scale"):
            if k2 in kw and hasattr(kw[k2], "ap"):
                rd.append(kw[k2])
        self.log.append(("op", eng, tuple(rd), tuple(wr), ((sem_name, tok[1]),),
                         fn_name))

        def go(e, _fn=fn_name, _args=args, _kw=kw, _sem=sem_name, _k=inc_k, _p=self):
            ins = getattr(e, _fn)(*_args, **_kw)
            ins.then_inc(_p.handle(_sem), _k)
        self.ops[eng].append(go)
        return tok

    def rec(self, eng, reads=(), writes=(), incs=(), tag=""):
        self.log.append(("op", eng, tuple(reads), tuple(writes), tuple(incs), tag))

    def barrier(self, engs=("tensor", "vector", "scalar")):
        cur = {e: (e + "prog", self.cnt.get(e + "prog", 0)) for e in engs}
        for e in engs:
            self.wait(e, [cur[o] for o in engs if o != e])

    def emit(self, block):
        class _CountingTensor:
            def __init__(s, eng, p):
                s._e, s._p = eng, p

            def matmul(s, *a, **k):
                s._p.mm_count[0] += 1
                return s._e.matmul(*a, **k)

            def __getattr__(s, n):
                return getattr(s._e, n)

        for eng in ENGS:
            lst = self.ops[eng]
            if not lst:
                continue

            def run(engine, _lst=lst, _eng=eng):
                if _eng == "tensor":
                    engine = _CountingTensor(engine, self)
                for f in _lst:
                    f(engine)
            getattr(block, eng)(run)


class Bank:
    def __init__(self, ap, name):
        self.ap = ap
        self.name = name
        self.free_tok = None

    def waits(self):
        return [self.free_tok]

    def release(self, tok):
        self.free_tok = tok


def build_nc(debug=(), pace=()):
    nc = bass.Bass("TRN2", target_bir_lowering=False)
    plan = Plan(nc)
    din = {}

    def dt_in(name, shape, dtype=F32):
        din[name] = nc.dram_tensor(name, shape, dtype, kind="ExternalInput").ap()
        return din[name]

    x_in = dt_in("x_fm", [D, T])
    c_in = dt_in("c_fm", [128, FC])
    w_qkv = dt_in("w_qkv", [D, 3 * D])
    w_mod = dt_in("w_mod", [D, 6 * D])
    w_router = dt_in("w_router", [D, 4])
    rb_in = dt_in("rb", [1, 4])
    lb_in = dt_in("lb", [1, 4])
    sw_w1 = dt_in("sw_w1", [D, 4096])
    sw_w3 = dt_in("sw_w3", [D, 4096])
    sw_w2 = dt_in("sw_w2", [4096, D])
    me_eq = dt_in("me_eq", [D, D])
    me_wv = dt_in("me_wv", [D, D])
    me_wo = dt_in("me_wo", [D, D])
    me_c1 = dt_in("me_c1", [D, 2048])
    me_c2 = dt_in("me_c2", [2048, D])
    ce_syn = dt_in("ce_syn", [D, D])
    ce_wv = dt_in("ce_wv", [D, D])
    ce_wo = dt_in("ce_wo", [D, D])
    ce_f1 = dt_in("ce_f1", [D, 2048])
    ce_f2 = dt_in("ce_f2", [2048, D])
    ce_gen = dt_in("ce_gen", [D, D])
    ml_w1 = dt_in("ml_w1", [D, 4096])
    ml_w2 = dt_in("ml_w2", [4096, D])
    vecs_in = dt_in("vecs_fm", [128, 27, 8])
    bvrow_in = dt_in("bvrow", [1, D])
    ones_in = dt_in("ones_bf", [128, 128], BF16)
    ident_in = dt_in("ident", [128, 128])
    maskA_in = dt_in("maskA", [128, 128])
    maskC_in = dt_in("maskC", [128, 128])
    hot_in = dt_in("hotrows", [32, 4096], BF16)

    y_out = nc.dram_tensor("y_fm", [D, T], F32, kind="ExternalOutput").ap()
    dbg = {}
    for k, shape, dtp in debug:
        dbg[k] = nc.dram_tensor("dbg_" + k, list(shape), dtp, kind="ExternalOutput").ap()

    es = contextlib.ExitStack()

    def sbt(name, shape, dtype=F32):
        return es.enter_context(nc.sbuf_tensor(name, shape, dtype))

    with es:
        xres = sbt("xres", [128, FC, TQ], F32)
        hbuf = sbt("hbuf", [128, FC, TQ], BF16)
        work = sbt("work", [128, 32768], BF16)
        wring = sbt("wring", [128, 20480], BF16)
        mstat = sbt("mstat", [128, 3, 1024], F32)
        statbf = sbt("statbf", [128, 2, 1024], BF16)
        xbsq = sbt("xbsq", [128, 4, 1024], BF16)
        wbg = sbt("wbg", [128, 1024], F32)
        gtmp = sbt("gtmp", [128, 512], F32)
        halo = sbt("halo", [128, 4096], BF16)
        stmp = sbt("stmp", [128, 3, 512], BF16)
        vecs = sbt("vecs_s", [128, 27, 8], F32)
        modc = sbt("modc", [128, 48], F32)
        dcols = sbt("dcols", [128, 64], F32)
        ones = sbt("ones_s", [128, 128], BF16)
        ident = sbt("ident_s", [128, 128], F32)
        maskA = sbt("maskA_s", [128, 128], F32)
        maskC = sbt("maskC_s", [128, 128], F32)
        hot = sbt("hot_s", [32, 4096], BF16)
        w32 = sbt("w32", [32, 128], BF16)
        wtm = sbt("wtm", [128, 32], F32)
        wq = sbt("wq", [128, 32], F32)
        rmath = sbt("rmath", [128, 12, 8], F32)
        rblr = sbt("rblr", [128, 32], F32)
        bvrows = sbt("bvrows", [128, D], BF16)
        epsc = sbt("epsc", [128, 1], F32)
        cfm = sbt("cfm", [128, FC], F32)
        silc = sbt("silc", [128, FC], BF16)
        rrow = mstat[0:1, 1, 0:8]
        rrbf = statbf[0:1, 1, 0:8]
        bvrow = mstat[0:1, 0, 0:1024]
        bvrbf = statbf[0:1, 0, 0:1024]

        pA = [es.enter_context(nc.psum_tensor(f"pA{i}", [128, 512], F32)) for i in range(4)]
        pO = [es.enter_context(nc.psum_tensor(f"pO{i}", [128, 1024], F32)) for i in range(2)]
        bankA = [Bank(pA[i][:], f"A{i}") for i in range(4)] + \
                [Bank(pO[i][:, j * 512:(j + 1) * 512], f"O{i}h{j}")
                 for i in range(2) for j in range(2)]
        pairs_ = [(pO[i][:], bankA[4 + 2 * i], bankA[5 + 2 * i]) for i in range(2)]

        class Pair:
            def __init__(self, ap, b0, b1):
                self.ap = ap
                self.b0, self.b1 = b0, b1

            @property
            def free_tok(self):
                return None

            def waits(self):
                return [self.b0.free_tok, self.b1.free_tok]

            def release(self, tok):
                self.b0.free_tok = tok
                self.b1.free_tok = tok

        rrA = [0]

        def takeA():
            b = bankA[rrA[0] % 8]
            rrA[0] += 1
            return b

        rrO = [0]

        def takeO():
            i = rrO[0] % 2
            rrO[0] += 1
            return Pair(*pairs_[i])

        def mm_acc(bank_ap, pairs, waits=(), flags=None):
            plan.wait("tensor", list(waits))
            tok = plan._bump("tensorprog", 1)
            plan.rec("tensor", reads=[a for pr in pairs for a in pr],
                     writes=[bank_ap], incs=[("tensorprog", tok[1])], tag="mm")
            n = len(pairs)
            if flags is None:
                flags = [(i == 0, i == n - 1) for i in range(n)]

            def go(t, _ap=bank_ap, _pairs=tuple(pairs), _flags=tuple(flags), _p=plan):
                ins = None
                for (lh, rh), (st, sp) in zip(_pairs, _flags):
                    ins = t.matmul(_ap, lh, rh, start=st, stop=sp)
                ins.then_inc(_p.handle("tensorprog"), 1)
            plan.ops["tensor"].append(go)
            return tok

        RING_UNITS, UNIT = 10, 2048

        class WRing:
            def __init__(self):
                self.head = 0
                self.loads = []
                self.ld_tok = []
                self.unit_free = [None] * RING_UNITS

            def plan_load(self, src_ap, kc, inner):
                elems = kc * inner
                n_units = (elems + UNIT - 1) // UNIT
                if (self.head % RING_UNITS) + n_units > RING_UNITS:
                    while self.head % RING_UNITS != 0:
                        self.head += 1
                off = self.head % RING_UNITS
                idx = len(self.loads)
                self.loads.append((off, n_units, src_ap, kc, inner))
                self.head += n_units
                waits = [self.unit_free[u] for u in range(off, off + n_units)]
                plan.wait("gpsimd", waits)
                sem = f"wl{idx % 16}"
                tok = plan._bump(sem, 16)
                plan.rec("gpsimd", reads=[],
                         writes=[wring[:, off * UNIT:off * UNIT + kc * inner]],
                         incs=[(sem, tok[1])], tag="wload")

                def go(g, _off=off, _src=src_ap, _kc=kc, _in=inner, _p=plan,
                       _sem=sem):
                    dst = wring[:, _off * UNIT:_off * UNIT + _kc * _in].rearrange(
                        "p (kc n) -> p kc n", kc=_kc)
                    g.dma_start(out=dst, in_=_src).then_inc(_p.handle(_sem), 16)
                plan.ops["gpsimd"].append(go)
                self.ld_tok.append((sem, tok[1]))
                return idx

            def view(self, idx):
                off, n, _src, kc, inner = self.loads[idx]
                return wring[:, off * UNIT:off * UNIT + kc * inner].rearrange(
                    "p (kc n) -> p kc n", kc=kc)

            def use_tok(self, idx):
                return self.ld_tok[idx]

            def release(self, idx, tok):
                off, n, _src, _kc, _in = self.loads[idx]
                for u in range(off, off + n):
                    self.unit_free[u] = tok

        ring = WRing()
        plan.sem("wload")

        def w_src(w_ap, r0, r1, c0, c1):
            return w_ap[r0:r1, c0:c1].rearrange("(kc p) n -> p kc n", p=128)

        def load_cols(w_ap, c0, c1):
            return ring.plan_load(w_src(w_ap, 0, D, c0, c1), 8, c1 - c0)

        def load_rows(w_ap, r0, r1):
            return ring.plan_load(w_src(w_ap, r0, r1, 0, D), (r1 - r0) // 128, D)

        nconst = [0]

        def const_dma(dst, src):
            plan.op("sync", "dma_start", out=dst, in_=src, inc="cload", inc_k=16)
            nconst[0] += 1

        const_dma(vecs[:], vecs_in[:])
        const_dma(ones[:], ones_in[:])
        const_dma(ident[:], ident_in[:])
        const_dma(maskA[:], maskA_in[:])
        const_dma(maskC[:], maskC_in[:])
        const_dma(hot[:], hot_in[:])
        const_dma(cfm[:], c_in[:])
        const_dma(rrow[:, 0:4], rb_in[:])
        const_dma(rrow[:, 4:8], lb_in[:])
        const_dma(bvrow[:, :], bvrow_in[:])
        CONST = ("cload", 16 * nconst[0])

        def tap(name, ap, waits):
            if name in dbg:
                plan.op("sync", "dma_start", out=dbg[name][:], in_=ap,
                        waits=list(waits) + [CONST], inc="tapst", inc_k=16)
            elif name.split("_q")[0] in pace:
                plan.wait("sync", list(waits))

        eps_tok = plan.op("vector", "memset", epsc[:], EPS)
        sil_tok = plan.op("scalar", "activation", silc[:], cfm[:], AF.Silu,
                          waits=[CONST])

        wmod_ids = []
        ev_toks = []
        for mc in range(48):
            j, off = mc // 4, (mc % 4) * 128
            if mc % 4 == 0:
                wmod_ids.append(load_cols(w_mod, j * 512, (j + 1) * 512))
            lid = wmod_ids[j]
            wv = ring.view(lid)
            bank = takeA()
            pairs = [(wv[:, kc, off:off + 128], silc[:, kc:kc + 1]) for kc in range(8)]
            tok = mm_acc(bank.ap[:, 0:1], pairs,
                         waits=[ring.use_tok(lid), sil_tok, bank.free_tok])
            if mc % 4 == 3:
                ring.release(lid, tok)
            et = plan.op("vector", "tensor_copy", modc[:, mc:mc + 1], bank.ap[:, 0:1],
                         waits=[tok])
            bank.free_tok = et
            ev_toks.append(et)

        plan.op("vector", "tensor_scalar_add", dcols[:, DC_CAPL:DC_CAPL + 8],
                modc[:, 8:16], 1.0, waits=[ev_toks[15]])
        plan.op("vector", "tensor_scalar_add", dcols[:, DC_CBPL:DC_CBPL + 8],
                modc[:, 32:40], 1.0, waits=[ev_toks[39]])
        clip_prev = None
        for gi, (src0, dst) in enumerate(((16, DC_GA), (40, DC_GB))):
            sgn, mag = rmath[:, 2 * gi, :], rmath[:, 2 * gi + 1, :]
            t1 = plan.op("vector", "tensor_single_scalar", sgn, modc[:, src0:src0 + 8],
                         0.0, ALU.is_ge, waits=[ev_toks[-1], clip_prev])
            t2 = plan.op("vector", "tensor_scalar", sgn, sgn, 2.0, 1.0,
                         ALU.mult, ALU.subtract, waits=[t1])
            t3a = plan.op("scalar", "activation", mag, modc[:, src0:src0 + 8],
                          AF.Abs, waits=[ev_toks[-1]])
            t3 = plan.op("vector", "tensor_scalar_max", mag, mag, 0.25, waits=[t3a])
            clip_prev = plan.op("vector", "tensor_mul", dcols[:, dst:dst + 8],
                                sgn, mag, waits=[t2, t3])
        gtok = plan.tok("vector")
        plan.op("vector", "tensor_mul", dcols[:, DC_GW:DC_GW + 8],
                dcols[:, DC_GA:DC_GA + 8], vecs[:, V_LNW, :],
                waits=[gtok, CONST])
        plan.op("vector", "tensor_mul", dcols[:, DC_GB2:DC_GB2 + 8],
                dcols[:, DC_GA:DC_GA + 8], vecs[:, V_LNB, :])
        GATES = plan.tok("vector")
        tap("modc", modc[:], [GATES])
        tap("dcols", dcols[:], [GATES])

        ct = plan.op("vector", "tensor_copy", rrbf[:, :], rrow[:, :], waits=[CONST])
        bank = takeA()
        rtok = mm_acc(bank.ap[:, 0:4],
                      [(ones[0:1, :], rrbf[:, 0:4]), (ones[0:1, :], rrbf[:, 4:8])],
                      waits=[ct, bank.free_tok, CONST])
        prev = plan.op("vector", "tensor_copy", rblr[:, 0:4], bank.ap[:, 0:4], waits=[rtok])
        bank.free_tok = prev
        for t8 in range(1, 8):
            prev = plan.op("vector", "tensor_copy", rblr[:, t8 * 4:(t8 + 1) * 4],
                           rblr[:, 0:4], waits=[prev])
        RBL = prev

        cb = plan.op("vector", "tensor_copy", bvrbf[:, :], bvrow[:, :], waits=[CONST])
        for h in range(2):
            bank = takeA()
            tok = mm_acc(bank.ap[:], [(ones[0:1, :], bvrbf[:, h * 512:(h + 1) * 512])],
                         waits=[cb, bank.free_tok])
            ftok = plan.op("vector", "tensor_copy", bvrows[:, h * 512:(h + 1) * 512],
                           bank.ap[:], waits=[tok])
            bank.free_tok = ftok
        BVROWS = plan.tok("vector")

        kslots = work[:, 0:8192].rearrange("p (s e) -> p s e", e=1024)
        vslots = work[:, 8192:16384].rearrange("p (s e) -> p s e", e=1024)
        zq = work[:, 16384:24576].rearrange("p (dc t) -> p dc t", t=1024)
        zst = [work[:, 24576 + r * 384:24576 + (r + 1) * 384].rearrange(
            "p (i t) -> p i t", t=128) for r in range(2)]
        osb = work[:, 25600:29696].rearrange("p (dc t) -> p dc t", t=512)
        kh = halo[:, 0:2048].rearrange("p (s e) -> p s e", e=1024)
        vh = halo[:, 2048:4096].rearrange("p (s e) -> p s e", e=1024)
        uview = work[:, 0:32768].rearrange("p (hf t) -> p hf t", t=1024)

        def chain(base, nfc=8):
            return work[:, base:base + nfc * 1024].rearrange("p (fc t) -> p fc t", t=1024)

        lnwar = [None]

        def ln(width, csl, src_fn, out_fn, pre_toks=()):
            nh = width // 512
            if width == 1024:
                s1b, s2b = takeO(), takeO()
            else:
                s1b, s2b = takeA(), takeA()
            s1, s2 = s1b.ap, s2b.ap
            seg_t = {}
            last1 = last2 = None
            for fc in range(FC):
                slot = fc % 2
                ap, is_f32 = src_fn(fc)
                ap = ap[:, csl]
                wl = list(pre_toks) if fc < 2 else [seg_t[fc - 2]]
                if is_f32:
                    c1 = plan.op("vector", "tensor_copy", xbsq[:, slot, 0:width], ap,
                                 waits=wl)
                    src_bf = xbsq[:, slot, 0:width]
                    c2 = plan.op("vector", "tensor_mul", xbsq[:, 2 + slot, 0:width],
                                 src_bf, src_bf, waits=[c1])
                else:
                    src_bf = ap
                    c2 = plan.op("vector", "tensor_mul", xbsq[:, 2 + slot, 0:width],
                                 ap, ap, waits=wl)
                w0 = [c2] + (s1b.waits() + s2b.waits() if fc == 0 else [])
                for h in range(nh):
                    hs = slice(h * 512, (h + 1) * 512)
                    last1 = mm_acc(s1[:, hs], [(ones[:], src_bf[:, hs])], waits=w0,
                                   flags=[(fc == 0, fc == FC - 1)])
                    w0 = []
                    last2 = mm_acc(s2[:, hs], [(ones[:], xbsq[:, 2 + slot, hs])],
                                   flags=[(fc == 0, fc == FC - 1)])
                seg_t[fc] = last2
            m = mstat[:, 0, 0:width]
            qq = mstat[:, 1, 0:width]
            tp = mstat[:, 2, 0:width]
            a1 = plan.op("scalar", "activation", m, s1[:, 0:width], AF.Copy,
                         scale=RINV, waits=[last1, lnwar[0]])
            a2 = plan.op("scalar", "activation", qq, s2[:, 0:width], AF.Copy,
                         scale=RINV, waits=[last2])
            s1b.release(a1)
            s2b.release(a2)
            v1 = plan.op("vector", "tensor_mul", tp, m, m, waits=[a1])
            v2 = plan.op("vector", "tensor_tensor", tp, qq, tp, ALU.subtract,
                         waits=[v1, a2])
            l1 = plan.op("scalar", "activation", tp, tp, AF.Ln, bias=epsc[:, 0:1],
                         waits=[v2, eps_tok])
            l2 = plan.op("scalar", "activation", tp, tp, AF.Exp, scale=-0.5,
                         waits=[l1])
            mb = plan.op("vector", "tensor_copy", statbf[:, 0, 0:width], m, waits=[a1])
            rb_ = plan.op("vector", "tensor_copy", statbf[:, 1, 0:width], tp, waits=[l2])
            outs = []
            prev_out = None
            for fc in range(FC):
                ap, is_f32 = src_fn(fc)
                ap = ap[:, csl]
                slot = fc % 2
                wl = [mb, rb_]
                if fc >= 2:
                    wl.append(outs[fc - 2])
                if is_f32:
                    tt = plan.op("vector", "tensor_tensor", xbsq[:, slot, 0:width],
                                 ap, m, ALU.subtract, waits=wl)
                else:
                    tt = plan.op("vector", "tensor_tensor", xbsq[:, slot, 0:width],
                                 ap, statbf[:, 0, 0:width], ALU.subtract, waits=wl)
                t2 = plan.op("vector", "tensor_mul", xbsq[:, slot, 0:width],
                             xbsq[:, slot, 0:width], statbf[:, 1, 0:width],
                             waits=[tt])
                prev_out = out_fn(fc, xbsq[:, slot, 0:width], t2, prev_out)
                outs.append(prev_out)
            lnwar[0] = plan.tok("vector")
            return outs

        prev_yhalf = [None, None]
        for q in range(NQ):
            xin_h = []
            for n in range(2):
                nsl = slice(n * 512, (n + 1) * 512)
                xin_h.append(plan.op(
                    "sync", "dma_start",
                    out=xres[:, :, nsl],
                    in_=x_in[:, q * TQ + n * 512:q * TQ + (n + 1) * 512].rearrange(
                        "(fc p) t -> p fc t", p=128),
                    waits=([prev_yhalf[n]] if prev_yhalf[n] else []),
                    inc=f"xload{n}", inc_k=16))

            def src_x(fc):
                return (xres[:, fc, :], True)

            plan.mark("LN1")
            h1tok = [None, None]
            for chnk in range(2):
                csl = slice(chnk * 512, (chnk + 1) * 512)

                def out_h1(fc, tile, tok, prev, _csl=csl):
                    return plan.op("vector", "tensor_scalar", hbuf[:, fc, _csl],
                                   tile,
                                   dcols[:, DC_CAPL + fc:DC_CAPL + fc + 1],
                                   modc[:, 0 + fc:0 + fc + 1],
                                   ALU.mult, ALU.add, waits=[tok, GATES])

                ln(512, csl, src_x, out_h1, pre_toks=[xin_h[chnk], GATES])
                h1tok[chnk] = plan.tok("vector")
            if q == 0:
                tap("h1", hbuf[:, :, :], [plan.tok("vector")])

            plan.mark("qkv")
            for j in range(6):
                lid = load_cols(w_qkv, j * 512, (j + 1) * 512)
                wv = ring.view(lid)
                if j < 2:
                    for i in range(4):
                        dc = j * 4 + i
                        for ch in range(2):
                            tsl = slice(ch * 512, (ch + 1) * 512)
                            bank = takeA()
                            tok = mm_acc(bank.ap[:],
                                         [(wv[:, kc, i * 128:(i + 1) * 128],
                                           hbuf[:, kc, tsl]) for kc in range(8)],
                                         waits=[ring.use_tok(lid), bank.free_tok,
                                                h1tok[ch]])
                            et = plan.op("scalar", "activation", zq[:, dc, tsl],
                                         bank.ap[:], AF.Identity,
                                         bias=vecs[:, V_BQ, dc:dc + 1], waits=[tok])
                            bank.free_tok = et
                elif j < 4:
                    for i in range(4):
                        dc = (j - 2) * 4 + i
                        for ch in range(2):
                            tsl = slice(ch * 512, (ch + 1) * 512)
                            bank = takeA()
                            tok = mm_acc(bank.ap[:],
                                         [(wv[:, kc, i * 128:(i + 1) * 128],
                                           hbuf[:, kc, tsl]) for kc in range(8)],
                                         waits=[ring.use_tok(lid), bank.free_tok,
                                                h1tok[ch]])
                            dst = kslots[:, ch * 4:(ch + 1) * 4,
                                         dc * 128:(dc + 1) * 128]
                            src = bank.ap[:].rearrange("p (s t) -> p s t", t=128)
                            et = plan.op("scalar", "activation", dst, src, AF.Identity,
                                         bias=vecs[:, V_BK, dc:dc + 1], waits=[tok])
                            bank.free_tok = et
                else:
                    dv = j - 4
                    for ch in range(2):
                        for lt in range(4):
                            ttok = slice(ch * 512 + lt * 128, ch * 512 + (lt + 1) * 128)
                            bank = takeA()
                            tok = mm_acc(bank.ap[:],
                                         [(hbuf[:, kc, ttok], wv[:, kc, :])
                                          for kc in range(8)],
                                         waits=[ring.use_tok(lid), bank.free_tok,
                                                h1tok[ch]])
                            et = plan.op("vector", "tensor_add",
                                         vslots[:, ch * 4 + lt,
                                                dv * 512:(dv + 1) * 512],
                                         bank.ap[:], bvrows[:, dv * 512:(dv + 1) * 512],
                                         waits=[tok, BVROWS])
                            bank.free_tok = et
                ring.release(lid, plan.tok("tensor") if j < 4 else plan.tok("vector"))
            plan.barrier()
            if q == 0:
                tap("zq", work[:, 16384:24576], [plan.tok("vector")])
                tap("kv", work[:, 0:16384], [plan.tok("vector")])

            qkv_done = plan.tok("tensor")
            o_mm_tok = [None, None]

            def emit_st(ch, lt):
                QTl = ch * 4 + lt
                QTg = q * 8 + QTl
                pairs_kt = [kt for kt in range(QTg - 2, QTg + 1) if kt >= 0]
                r = QTl % 2
                st_toks = []
                for pi, ktg in enumerate(pairs_kt):
                    if ktg >= q * 8:
                        kv_ = kslots[:, ktg - q * 8, :].rearrange(
                            "p (dc t) -> p dc t", t=128)
                    else:
                        kv_ = kh[:, ktg - (q * 8 - 2), :].rearrange(
                            "p (dc t) -> p dc t", t=128)
                    bank = takeA()
                    tok = mm_acc(bank.ap[:, 0:128],
                                 [(kv_[:, dc, :],
                                   zq[:, dc, ch * 512 + lt * 128:
                                      ch * 512 + (lt + 1) * 128])
                                  for dc in range(8)],
                                 waits=[bank.free_tok])
                    ewl = [tok, o_mm_tok[r]]
                    if ktg == QTg:
                        et = plan.op("vector", "tensor_mul", zst[r][:, pi, :],
                                     bank.ap[:, 0:128], maskC[:], waits=ewl)
                    elif ktg == QTg - 1:
                        et = plan.op("scalar", "activation", zst[r][:, pi, :],
                                     bank.ap[:, 0:128], AF.Copy, scale=SCALE,
                                     waits=ewl)
                    else:
                        et = plan.op("vector", "tensor_mul", zst[r][:, pi, :],
                                     bank.ap[:, 0:128], maskA[:], waits=ewl)
                    bank.free_tok = et
                    st_toks.append(et)
                return pairs_kt, r, st_toks

            def emit_o(ch, lt, pairs_kt, r, st_toks):
                ob = takeO()
                plan.wait("tensor", ob.waits() + st_toks)
                otok = plan._bump("tensorprog", 1)
                plan.rec("tensor",
                         reads=[vslots[:, :, :], vh[:, :, :], zst[r][:, :, :]],
                         writes=[ob.ap[:]], incs=[("tensorprog", otok[1])],
                         tag="o_mm")

                def go_o(t, _pairs=tuple(pairs_kt), _q=q, _r=r, _ap=ob.ap, _p=plan):
                    ins = None
                    for dc in range(8):
                        for pi, ktg in enumerate(_pairs):
                            if ktg >= _q * 8:
                                vv = vslots[:, ktg - _q * 8, :]
                            else:
                                vv = vh[:, ktg - (_q * 8 - 2), :]
                            ins = t.matmul(_ap[:, dc * 128:(dc + 1) * 128],
                                           vv[:, dc * 128:(dc + 1) * 128],
                                           zst[_r][:, pi, :],
                                           start=(pi == 0),
                                           stop=(pi == len(_pairs) - 1))
                    ins.then_inc(_p.handle("tensorprog"), 1)
                plan.ops["tensor"].append(go_o)
                o_mm_tok[r] = ("tensorprog", otok[1])
                et = plan.op("vector", "tensor_copy",
                             osb[:, :, lt * 128:(lt + 1) * 128],
                             ob.ap[:].rearrange("p (dc t) -> p dc t", t=128),
                             waits=[o_mm_tok[r]])
                ob.release(et)

            def attn_chunk(ch):
                pend = emit_st(ch, 0)
                for lt in range(4):
                    nxt = emit_st(ch, lt + 1) if lt < 3 else None
                    emit_o(ch, lt, *pend)
                    pend = nxt

            def attn_ln(ch):
                tsl = slice(ch * 512, (ch + 1) * 512)

                def src_o(fc):
                    return (osb[:, fc, :], False)

                def out_attn(fc, tile, tok, prev, _tsl=tsl):
                    u = plan.op("vector", "tensor_scalar", gtmp[:, :], tile,
                                dcols[:, DC_GW + fc:DC_GW + fc + 1],
                                dcols[:, DC_GB2 + fc:DC_GB2 + fc + 1],
                                ALU.mult, ALU.add, waits=[tok, prev, GATES])
                    return plan.op("vector", "tensor_tensor", xres[:, fc, _tsl],
                                   xres[:, fc, _tsl], gtmp[:, :], ALU.add,
                                   waits=[u])

                ln(512, slice(0, 512), src_o, out_attn,
                   pre_toks=[plan.tok("vector")])

            h2tok = [None, None]

            def ln2_chunk(ch):
                csl = slice(ch * 512, (ch + 1) * 512)

                def out_h2(fc, tile, tok, prev, _csl=csl):
                    return plan.op("vector", "tensor_scalar", hbuf[:, fc, _csl],
                                   tile,
                                   dcols[:, DC_CBPL + fc:DC_CBPL + fc + 1],
                                   modc[:, 24 + fc:24 + fc + 1],
                                   ALU.mult, ALU.add, waits=[tok, GATES, qkv_done])

                ln(512, csl, src_x, out_h2, pre_toks=[plan.tok("vector"), GATES])
                h2tok[ch] = plan.tok("vector")

            plan.mark("attn0")
            attn_chunk(0)
            if q == 0:
                tap("osb", work[:, 25600:29696], [plan.tok("vector")])
            plan.mark("attnln0")
            attn_ln(0)
            plan.mark("attn1")
            attn_chunk(1)
            plan.mark("ln2_0")
            ln2_chunk(0)
            plan.mark("attnln1")
            attn_ln(1)
            plan.mark("ln2_1")
            ln2_chunk(1)
            tap(f"x1_q{q}", xres[:, :, :], [plan.tok("vector")])
            tap(f"h2_q{q}", hbuf[:, :, :], [plan.tok("vector")])

            ttk = plan.tok("tensor")
            for i in range(2):
                plan.op("vector", "tensor_copy", kh[:, i, :], kslots[:, 6 + i, :],
                        waits=[ttk])
                plan.op("vector", "tensor_copy", vh[:, i, :], vslots[:, 6 + i, :])

            gate_prev = [plan.tok("vector")]

            def make_wb(e):
                ob = takeO()
                plan.wait("tensor", ob.waits() + [w32t, CONST])
                tok = plan._bump("tensorprog", 1)
                plan.rec("tensor", reads=[hot[:, :], w32[:, :]], writes=[ob.ap[:]],
                         incs=[("tensorprog", tok[1])], tag="wb_mm")

                def go(t, _e=e, _ap=ob.ap, _p=plan):
                    ins = None
                    for t8 in range(8):
                        qrow = t8 * 4 + _e
                        ins = t.matmul(_ap[:, t8 * 128:(t8 + 1) * 128],
                                       hot[:, qrow * 128:(qrow + 1) * 128], w32[:],
                                       start=True, stop=True)
                    ins.then_inc(_p.handle("tensorprog"), 1)
                plan.ops["tensor"].append(go)
                et = plan.op("vector", "tensor_copy", wbg[:], ob.ap[:],
                             waits=[("tensorprog", tok[1]), gate_prev[0]])
                ob.release(et)
                return et

            def gate_evac(bank_ap, fc, n, bias_col, wb_tok, mmtok):
                nsl = slice(n * 512, (n + 1) * 512)
                wl = [mmtok, wb_tok, gate_prev[0], GATES]
                if bias_col is not None:
                    g1 = plan.op("vector", "scalar_tensor_tensor", gtmp[:, :],
                                 bank_ap, bias_col, wbg[:, nsl],
                                 ALU.add, ALU.mult, waits=wl)
                else:
                    g1 = plan.op("vector", "tensor_mul", gtmp[:, :],
                                 bank_ap, wbg[:, nsl], waits=wl)
                g2 = plan.op("vector", "scalar_tensor_tensor", xres[:, fc, nsl],
                             gtmp[:, :], dcols[:, DC_GB + fc:DC_GB + fc + 1],
                             xres[:, fc, nsl], ALU.mult, ALU.add, waits=[g1])
                gate_prev[0] = g2
                return g2

            h_rhs = lambda kc, nsl: hbuf[:, kc, nsl]

            def dense_stage(w_ids, rhs_view, out_cb):
                toks = []
                for fc in range(FC):
                    for n in range(2):
                        nsl = slice(n * 512, (n + 1) * 512)
                        lid = w_ids[fc // 4]
                        off = (fc % 4) * 128
                        wv = ring.view(lid)
                        bank = takeA()
                        tok = mm_acc(bank.ap[:],
                                     [(wv[:, kc, off:off + 128], rhs_view(kc, nsl))
                                      for kc in range(8)],
                                     waits=[ring.use_tok(lid), bank.free_tok])
                        toks.append(out_cb(fc, n, bank, tok))
                return toks

            def evac_to(chain_view, bias_row):
                def cb(fc, n, bank, tok):
                    et = plan.op("scalar", "activation",
                                 chain_view[:, fc, n * 512:(n + 1) * 512],
                                 bank.ap[:], AF.Identity,
                                 bias=vecs[:, bias_row, fc:fc + 1], waits=[tok])
                    bank.free_tok = et
                    return et
                return cb

            def big_out(w_ids, rhs_fcview, nkc, out_cb, rhs_rec=None):
                assert len(w_ids) <= 2
                for fc in range(FC):
                    for n in range(2):
                        nsl = slice(n * 512, (n + 1) * 512)
                        bank = takeA()
                        plan.wait("tensor", [bank.free_tok] +
                                  [ring.use_tok(i) for i in w_ids])
                        tok = plan._bump("tensorprog", 1)
                        plan.rec("tensor",
                                 reads=[ring.view(i) for i in w_ids] +
                                       [rhs_rec if rhs_rec is not None
                                        else work[:, :]],
                                 writes=[bank.ap[:]],
                                 incs=[("tensorprog", tok[1])], tag="bigout_mm")

                        def go(t, _fc=fc, _nsl=nsl, _ap=bank.ap, _ids=tuple(w_ids),
                               _rv=rhs_fcview, _nkc=nkc, _p=plan):
                            ins = None
                            per = _nkc // len(_ids)
                            for seg, lid in enumerate(_ids):
                                wv = ring.view(lid)
                                for s in range(per):
                                    kk = seg * per + s
                                    ins = t.matmul(_ap,
                                                   wv[:, s, _fc * 128:(_fc + 1) * 128],
                                                   _rv(kk, _nsl),
                                                   start=(kk == 0), stop=(kk == _nkc - 1))
                            ins.then_inc(_p.handle("tensorprog"), 1)
                        plan.ops["tensor"].append(go)
                        out_cb(fc, n, bank, ("tensorprog", tok[1]))

            def rounds_out(w_dram, rhs_fcview, gate_cb):
                for r in range(2):
                    tb = [takeA() for _ in range(4)] + [takeO(), takeO()]
                    aps = [tb[i].ap for i in range(4)] + \
                          [tb[4].ap[:, 0:512], tb[4].ap[:, 512:1024],
                           tb[5].ap[:, 0:512], tb[5].ap[:, 512:1024]]
                    tiles = [(4 * r + i, n) for i in range(4) for n in range(2)]
                    base = plan.cnt["tensorprog"]
                    seg_ids = []
                    for s in range(4):
                        seg_ids.append(ring.plan_load(
                            w_src(w_dram, s * 1024, (s + 1) * 1024,
                                  r * 512, (r + 1) * 512), 8, 512))
                        ring.release(seg_ids[s], ("tensorprog", base + s + 1))
                    plan.wait("tensor",
                              sum((b.waits() for b in tb), []) +
                              [ring.use_tok(seg_ids[0]), plan.tok("vector"),
                               plan.tok("scalar")])
                    plan._bump("tensorprog", 4)
                    segw = [ring.use_tok(i) for i in seg_ids]
                    for seg in range(4):
                        if seg > 0:
                            plan.log.append(("wait", "tensor", ((segw[seg][0],
                                                                segw[seg][1]),)))
                        plan.rec("tensor",
                                 reads=[ring.view(seg_ids[seg]), work[:, :]],
                                 writes=list(aps),
                                 incs=[("tensorprog", base + seg + 1)],
                                 tag="rounds_mm")

                    def go(t, _tiles=tuple(tiles), _aps=tuple(aps),
                           _ids=tuple(seg_ids), _rv=rhs_fcview,
                           _segw=tuple(segw), _p=plan, _r=r):
                        for seg in range(4):
                            if seg > 0:
                                t.wait_ge(_p.handle(_segw[seg][0]), _segw[seg][1])
                            wv = ring.view(_ids[seg])
                            ins = None
                            for s in range(8):
                                kk = seg * 8 + s
                                for ti, (outf, n) in enumerate(_tiles):
                                    ins = t.matmul(
                                        _aps[ti],
                                        wv[:, s, (outf - 4 * _r) * 128:
                                           (outf - 4 * _r + 1) * 128],
                                        _rv(kk, slice(n * 512, (n + 1) * 512)),
                                        start=(kk == 0), stop=(kk == 31))
                            ins.then_inc(_p.handle("tensorprog"), 1)
                    plan.ops["tensor"].append(go)
                    done = ("tensorprog", base + 4)
                    gs = []
                    for ti, (outf, n) in enumerate(tiles):
                        gs.append(gate_cb(aps[ti], outf, n, done))
                    for ti in range(4):
                        tb[ti].release(gs[ti])
                    tb[4].release(gs[5])
                    tb[5].release(gs[7])


            u_rhs = lambda kk, nsl: uview[:, kk, nsl]

            u_toks = [None] * 3
            rwq = [None]

            def emit_router():
                rt_id = ring.plan_load(w_src(w_router, 0, D, 0, 4), 8, 4)
                rtv = ring.view(rt_id)
                bank = takeA()
                plan.wait("tensor", [bank.free_tok, ring.use_tok(rt_id), h2tok[1]])
                lt_tok = plan._bump("tensorprog", 1)
                plan.rec("tensor", reads=[hbuf[:, :, :], rtv[:, :, :]],
                         writes=[bank.ap[:, 0:32]],
                         incs=[("tensorprog", lt_tok[1])], tag="router_mm")

                def go_rt(t, _ap=bank.ap, _rtv=rtv, _p=plan):
                    ins = None
                    for t8 in range(8):
                        for kc in range(8):
                            ins = t.matmul(_ap[:, t8 * 4:(t8 + 1) * 4],
                                           hbuf[:, kc, t8 * 128:(t8 + 1) * 128],
                                           _rtv[:, kc, 0:4],
                                           start=(kc == 0), stop=(kc == 7))
                    ins.then_inc(_p.handle("tensorprog"), 1)
                plan.ops["tensor"].append(go_rt)
                ring.release(rt_id, ("tensorprog", lt_tok[1]))
                lg = plan.op("vector", "tensor_add", wtm[:], bank.ap[:, 0:32], rblr[:],
                             waits=[("tensorprog", lt_tok[1]), RBL])
                bank.free_tok = lg
                l_ = lambda e: wtm[:, e:32:4]
                ra = lambda i: rmath[:, i, :]
                ta = plan.op("vector", "tensor_tensor", ra(0), l_(0), l_(1), ALU.max,
                             waits=[lg])
                tb = plan.op("vector", "tensor_tensor", ra(1), l_(0), l_(1), ALU.min)
                tc_ = plan.op("vector", "tensor_tensor", ra(2), l_(2), l_(3), ALU.max)
                td = plan.op("vector", "tensor_tensor", ra(3), l_(2), l_(3), ALU.min)
                t4 = plan.op("vector", "tensor_tensor", ra(4), ra(0), ra(2), ALU.max,
                             waits=[ta, tc_])
                t5 = plan.op("vector", "tensor_tensor", ra(5), ra(0), ra(2), ALU.min)
                t6 = plan.op("vector", "tensor_tensor", ra(6), ra(1), ra(3), ALU.max,
                             waits=[tb, td])
                t7 = plan.op("vector", "tensor_tensor", ra(7), ra(5), ra(6), ALU.max,
                             waits=[t5, t6])
                t8_ = plan.op("vector", "tensor_tensor", ra(8), ra(7), ra(4), ALU.subtract,
                              waits=[t4, t7])
                e2_ = plan.op("scalar", "activation", ra(8), ra(8), AF.Exp, waits=[t8_])
                t9 = plan.op("vector", "tensor_scalar_add", ra(8), ra(8), 1.0, waits=[e2_])
                t10 = plan.op("vector", "reciprocal", ra(9), ra(8), waits=[t9])
                wq_toks = rwq[0] = []
                for e in range(4):
                    de = plan.op("vector", "tensor_tensor", ra(10), l_(e), ra(4),
                                 ALU.subtract, waits=[t4, t10] + wq_toks[-1:])
                    ee = plan.op("scalar", "activation", ra(10), ra(10), AF.Exp, waits=[de])
                    se = plan.op("vector", "tensor_tensor", ra(11), l_(e), ra(7),
                                 ALU.is_ge, waits=[t7])
                    m1_ = plan.op("vector", "tensor_tensor", ra(10), ra(10), ra(11),
                                  ALU.mult, waits=[ee, se])
                    wq_toks.append(plan.op("vector", "tensor_tensor", wq[:, e:32:4],
                                           ra(10), ra(9), ALU.mult, waits=[m1_]))
                if q == 0:
                    tap("wtm", wtm[:], [plan.tok("vector")])
                    tap("wq", wq[:], [plan.tok("vector")])

            plan.mark("e0u")
            for hf in range(32):
                if hf % 4 == 0:
                    j = hf // 4
                    lid1 = load_cols(sw_w1, j * 512, (j + 1) * 512)
                    lid3 = load_cols(sw_w3, j * 512, (j + 1) * 512)
                off = (hf % 4) * 128
                w1v, w3v = ring.view(lid1), ring.view(lid3)
                for n in range(2):
                    nsl = slice(n * 512, (n + 1) * 512)
                    b1, b3 = takeA(), takeA()
                    t1 = mm_acc(b1.ap[:],
                                [(w1v[:, kc, off:off + 128], hbuf[:, kc, nsl])
                                 for kc in range(8)],
                                waits=[ring.use_tok(lid1), b1.free_tok, h2tok[n]])
                    t3 = mm_acc(b3.ap[:],
                                [(w3v[:, kc, off:off + 128], hbuf[:, kc, nsl])
                                 for kc in range(8)],
                                waits=[ring.use_tok(lid3), b3.free_tok])
                    si = (2 * hf + n) % 3
                    ss = stmp[:, si, :]
                    st = plan.op("scalar", "activation", ss, b1.ap[:], AF.Silu,
                                 waits=[t1, u_toks[si]])
                    b1.free_tok = st
                    ut = plan.op("vector", "tensor_mul", uview[:, hf, nsl],
                                 ss, b3.ap[:], waits=[st, t3])
                    b3.free_tok = ut
                    u_toks[si] = ut
                if hf % 4 == 3:
                    ring.release(lid1, plan.tok("tensor"))
                    ring.release(lid3, plan.tok("tensor"))
                if hf == 3:
                    emit_router()
            wq_toks = rwq[0]

            bank = takeA()
            trt = plan.op("tensor", "transpose", bank.ap[0:32, 0:128], wq[:], ident[:],
                          waits=wq_toks + [bank.free_tok, CONST], inc="tensorprog")
            w32t = plan.op("vector", "tensor_copy", w32[:], bank.ap[0:32, 0:128],
                           waits=[trt])
            bank.free_tok = w32t
            wb_tok = make_wb(0)
            plan.barrier()
            if q == 0:
                tap("u0", work[:, 0:32768], [plan.tok("vector")])

            def gate_cb_e0(ap, outf, n, tok, _wb=wb_tok):
                return gate_evac(ap, outf, n, None, _wb, tok)

            plan.mark("e0out")
            rounds_out(sw_w2, u_rhs, gate_cb_e0)
            plan.barrier()

            plan.mark("e1")
            wb_tok = make_wb(1)
            c0, c1v, c2v = chain(0), chain(8192), chain(16384)
            gz = work[:, 0:16384].rearrange("p (fc t) -> p fc t", t=1024)
            eq_ids = [load_cols(me_eq, j * 512, (j + 1) * 512) for j in range(2)]
            dense_stage(eq_ids, h_rhs, evac_to(c2v, V_EQB))
            for i in eq_ids:
                ring.release(i, plan.tok("tensor"))
            plan.barrier()
            wv_ids = [load_cols(me_wv, j * 512, (j + 1) * 512) for j in range(2)]
            c2_rhs = lambda kc, nsl: c2v[:, kc, nsl]
            dense_stage(wv_ids, c2_rhs, evac_to(c1v, V_MEBV))
            for i in wv_ids:
                ring.release(i, plan.tok("tensor"))
            plan.barrier()
            wo_ids = [load_cols(me_wo, j * 512, (j + 1) * 512) for j in range(2)]
            c1_rhs = lambda kc, nsl: c1v[:, kc, nsl]
            dense_stage(wo_ids, c1_rhs, evac_to(c2v, V_MEBO))
            for i in wo_ids:
                ring.release(i, plan.tok("tensor"))
            plan.barrier()
            c1_ids = [load_cols(me_c1, j * 512, (j + 1) * 512) for j in range(4)]
            for fc in range(16):
                for n in range(2):
                    nsl = slice(n * 512, (n + 1) * 512)
                    lid = c1_ids[fc // 4]
                    off = (fc % 4) * 128
                    wv = ring.view(lid)
                    bank = takeA()
                    tok = mm_acc(bank.ap[:],
                                 [(wv[:, kc, off:off + 128], c2v[:, kc, nsl])
                                  for kc in range(8)],
                                 waits=[ring.use_tok(lid), bank.free_tok])
                    et = plan.op("scalar", "activation", gz[:, fc, nsl], bank.ap[:],
                                 AF.Gelu,
                                 bias=vecs[:, V_C1B0 + fc // 8, fc % 8:fc % 8 + 1],
                                 waits=[tok])
                    bank.free_tok = et
            for i in c1_ids:
                ring.release(i, plan.tok("tensor"))
            plan.barrier()
            c2_ids = [load_rows(me_c2, j * 1024, (j + 1) * 1024) for j in range(2)]
            gz_rhs = lambda kk, nsl: gz[:, kk, nsl]

            def cb_e1(fc, n, bank, tok, _wb=wb_tok):
                g = gate_evac(bank.ap[:], fc, n, vecs[:, V_C2B, fc:fc + 1], _wb, tok)
                bank.free_tok = g
                return g

            big_out(c2_ids, gz_rhs, 16, cb_e1, rhs_rec=work[:, 0:16384])
            for i in c2_ids:
                ring.release(i, plan.tok("vector"))
            plan.barrier()

            plan.mark("e2")
            wb_tok = make_wb(2)
            syn_ids = [load_cols(ce_syn, j * 512, (j + 1) * 512) for j in range(2)]
            dense_stage(syn_ids, h_rhs, evac_to(c0, V_SYNB))
            for i in syn_ids:
                ring.release(i, plan.tok("tensor"))
            plan.barrier()
            cewv_ids = [load_cols(ce_wv, j * 512, (j + 1) * 512) for j in range(2)]
            c0_rhs = lambda kc, nsl: c0[:, kc, nsl]
            dense_stage(cewv_ids, c0_rhs, evac_to(c1v, V_CEBV))
            for i in cewv_ids:
                ring.release(i, plan.tok("tensor"))
            plan.barrier()
            cewo_ids = [load_cols(ce_wo, j * 512, (j + 1) * 512) for j in range(2)]

            def cb_ssum(fc, n, bank, tok):
                nsl = slice(n * 512, (n + 1) * 512)
                et = plan.op("vector", "scalar_tensor_tensor", c2v[:, fc, nsl],
                             bank.ap[:], vecs[:, V_CEBO, fc:fc + 1], c0[:, fc, nsl],
                             ALU.add, ALU.add, waits=[tok])
                bank.free_tok = et
                return et

            dense_stage(cewo_ids, c1_rhs, cb_ssum)
            for i in cewo_ids:
                ring.release(i, plan.tok("tensor"))
            plan.barrier()

            def src_ssum(fc):
                return (c2v[:, fc, :], False)
            plan.mark("n1")
            h2a_tok = [None, None]
            for chnk in range(2):
                csl = slice(chnk * 512, (chnk + 1) * 512)
                ln(512, csl, src_ssum,
                   lambda fc, tile, tok, prev, _csl=csl:
                       plan.op("vector", "tensor_scalar", c0[:, fc, _csl], tile,
                               vecs[:, V_N1G, fc:fc + 1], vecs[:, V_N1B, fc:fc + 1],
                               ALU.mult, ALU.add, waits=[tok]),
                   pre_toks=[plan.tok("vector")])
                h2a_tok[chnk] = plan.tok("vector")
            gz2 = work[:, 8192:24576].rearrange("p (fc t) -> p fc t", t=1024)
            f1_ids = [load_cols(ce_f1, j * 512, (j + 1) * 512) for j in range(4)]
            for n in range(2):
                nsl = slice(n * 512, (n + 1) * 512)
                for fc in range(16):
                    lid = f1_ids[fc // 4]
                    off = (fc % 4) * 128
                    wv = ring.view(lid)
                    bank = takeA()
                    tok = mm_acc(bank.ap[:],
                                 [(wv[:, kc, off:off + 128], c0[:, kc, nsl])
                                  for kc in range(8)],
                                 waits=[ring.use_tok(lid), bank.free_tok,
                                        h2a_tok[n]])
                    et = plan.op("scalar", "activation", gz2[:, fc, nsl], bank.ap[:],
                                 AF.Relu,
                                 bias=vecs[:, V_F1B0 + fc // 8, fc % 8:fc % 8 + 1],
                                 waits=[tok])
                    bank.free_tok = et
            for i in f1_ids:
                ring.release(i, plan.tok("tensor"))
            plan.barrier()
            c3 = work[:, 24576:32768].rearrange("p (fc t) -> p fc t", t=1024)
            f2_ids = [load_rows(ce_f2, j * 1024, (j + 1) * 1024) for j in range(2)]
            gz2_rhs = lambda kk, nsl: gz2[:, kk, nsl]

            def cb_s2(fc, n, bank, tok):
                nsl = slice(n * 512, (n + 1) * 512)
                et = plan.op("vector", "scalar_tensor_tensor", c3[:, fc, nsl],
                             bank.ap[:], vecs[:, V_F2B, fc:fc + 1], c0[:, fc, nsl],
                             ALU.add, ALU.add, waits=[tok])
                bank.free_tok = et
                return et

            big_out(f2_ids, gz2_rhs, 16, cb_s2, rhs_rec=work[:, 8192:24576])
            for i in f2_ids:
                ring.release(i, plan.tok("vector"))
            plan.barrier()
            h2b = work[:, 8192:16384].rearrange("p (fc t) -> p fc t", t=1024)

            def src_s2(fc):
                return (c3[:, fc, :], False)

            plan.mark("n2")
            h2b_tok = [None, None]
            for chnk in range(2):
                csl = slice(chnk * 512, (chnk + 1) * 512)
                ln(512, csl, src_s2,
                   lambda fc, tile, tok, prev, _csl=csl:
                       plan.op("vector", "tensor_scalar", h2b[:, fc, _csl], tile,
                               vecs[:, V_N2G, fc:fc + 1], vecs[:, V_N2B, fc:fc + 1],
                               ALU.mult, ALU.add, waits=[tok]),
                   pre_toks=[plan.tok("vector")])
                h2b_tok[chnk] = plan.tok("vector")
            gen_ids = [load_cols(ce_gen, j * 512, (j + 1) * 512) for j in range(2)]
            for n in range(2):
                nsl = slice(n * 512, (n + 1) * 512)
                for fc in range(FC):
                    lid = gen_ids[fc // 4]
                    off = (fc % 4) * 128
                    wv = ring.view(lid)
                    bank = takeA()
                    tok = mm_acc(bank.ap[:],
                                 [(wv[:, kc, off:off + 128], h2b[:, kc, nsl])
                                  for kc in range(8)],
                                 waits=[ring.use_tok(lid), bank.free_tok,
                                        h2b_tok[n]])
                    g = gate_evac(bank.ap[:], fc, n, vecs[:, V_GENB, fc:fc + 1],
                                  wb_tok, tok)
                    bank.free_tok = g
            for i in gen_ids:
                ring.release(i, plan.tok("vector"))
            plan.barrier()

            plan.mark("e3")
            wb_tok = make_wb(3)
            for hf in range(32):
                if hf % 4 == 0:
                    lid = load_cols(ml_w1, (hf // 4) * 512, (hf // 4 + 1) * 512)
                off = (hf % 4) * 128
                wv = ring.view(lid)
                for n in range(2):
                    nsl = slice(n * 512, (n + 1) * 512)
                    bank = takeA()
                    tok = mm_acc(bank.ap[:],
                                 [(wv[:, kc, off:off + 128], hbuf[:, kc, nsl])
                                  for kc in range(8)],
                                 waits=[ring.use_tok(lid), bank.free_tok])
                    et = plan.op("scalar", "activation", uview[:, hf, nsl], bank.ap[:],
                                 AF.Gelu,
                                 bias=vecs[:, V_MLB1 + hf // 8, hf % 8:hf % 8 + 1],
                                 waits=[tok])
                    bank.free_tok = et
                if hf % 4 == 3:
                    ring.release(lid, plan.tok("scalar"))
            plan.barrier()

            def gate_cb_e3(ap, outf, n, tok, _wb=wb_tok):
                return gate_evac(ap, outf, n, vecs[:, V_MLB2, outf:outf + 1],
                                 _wb, tok)

            plan.mark("e3out")
            rounds_out(ml_w2, u_rhs, gate_cb_e3)
            xdone = plan.tok("vector")

            yt = plan.op("sync", "dma_start",
                         out=y_out[:, q * TQ:(q + 1) * TQ].rearrange(
                             "(fc p) t -> p fc t", p=128),
                         in_=xres[:, :, :],
                         waits=[xdone], inc="ystore", inc_k=16)
            prev_yhalf = [yt, yt]
            plan.barrier()

        tap("fin_modc", modc[:], [plan.tok("vector")])
        tap("fin_dcols", dcols[:], [plan.tok("vector")])
        tap("fin_rblr", rblr[:], [plan.tok("vector")])
        tap("fin_bvrows", bvrows[:], [plan.tok("vector")])
        tap("fin_maskA", maskA[:], [plan.tok("vector")])
        tap("fin_ones", ones[:], [plan.tok("vector")])
        tap("fin_vecs", vecs[:], [plan.tok("vector")])
        tap("fin_silc", silc[:], [plan.tok("vector")])
        tap("fin_epsc", epsc[:], [plan.tok("vector")])
        plan.wait("sync", [("ystore", plan.cnt["ystore"])] +
                  ([("tapst", plan.cnt["tapst"])] if "tapst" in plan.cnt else []))

        for n in sorted(plan.cnt.keys()):
            plan.handles[n] = es.enter_context(nc.semaphore(n))
        block = es.enter_context(nc.Block())
        plan.emit(block)

    LAST_MARKS[:] = plan.marks
    LAST_LOG[:] = plan.log
    return nc, din


_CACHE = {}


def _host_inputs(inputs):
    x = np.asarray(inputs["L0_attn_x"])
    c = np.asarray(inputs["L0_c"])
    vec_rows = [
        inputs["L0_attn_ln_w"], inputs["L0_attn_ln_b"],
        inputs["L0_ffn_me_eq_b"], inputs["L0_ffn_me_bv"], inputs["L0_ffn_me_bo"],
        inputs["L0_ffn_me_c2b"],
        inputs["L0_ffn_ce_syn_b"], inputs["L0_ffn_ce_bv"], inputs["L0_ffn_ce_bo"],
        inputs["L0_ffn_ce_n1g"], inputs["L0_ffn_ce_n1b"], inputs["L0_ffn_ce_f2b"],
        inputs["L0_ffn_ce_n2g"], inputs["L0_ffn_ce_n2b"], inputs["L0_ffn_ce_gen_b"],
        inputs["L0_ffn_ml_b2"],
        inputs["L0_attn_b_qkv"][0:1024], inputs["L0_attn_b_qkv"][1024:2048],
        inputs["L0_attn_b_qkv"][2048:3072],
        inputs["L0_ffn_me_c1b"][0:1024], inputs["L0_ffn_me_c1b"][1024:2048],
        inputs["L0_ffn_ce_f1b"][0:1024], inputs["L0_ffn_ce_f1b"][1024:2048],
        inputs["L0_ffn_ml_b1"][0:1024], inputs["L0_ffn_ml_b1"][1024:2048],
        inputs["L0_ffn_ml_b1"][2048:3072], inputs["L0_ffn_ml_b1"][3072:4096],
    ]
    vecs = np.stack([np.asarray(r, dtype=np.float32) for r in vec_rows])
    vecs_fm = np.ascontiguousarray(vecs.reshape(27, 8, 128).transpose(2, 0, 1))
    onesb = np.ones((128, 128), dtype=ml_dtypes.bfloat16)
    ident = np.eye(128, dtype=np.float32)
    p = np.arange(128)[:, None]
    qc = np.arange(128)[None, :]
    maskA = ((qc < p).astype(np.float32)) * np.float32(SCALE)
    maskC = ((qc >= p).astype(np.float32)) * np.float32(SCALE)
    hotrows = np.zeros((32, 4096), dtype=ml_dtypes.bfloat16)
    for qrow in range(32):
        hotrows[qrow, qrow * 128:(qrow + 1) * 128] = 1
    shared = dict(
        w_qkv=np.asarray(inputs["L0_attn_W_qkv"]),
        w_mod=np.asarray(inputs["L0_wmod"]),
        w_router=np.asarray(inputs["L0_ffn_router_w"]),
        rb=np.asarray(inputs["L0_ffn_router_b"]).reshape(1, 4),
        lb=np.asarray(inputs["L0_ffn_load_balancer"]).reshape(1, 4),
        sw_w1=np.asarray(inputs["L0_ffn_sw_w1"]),
        sw_w3=np.asarray(inputs["L0_ffn_sw_w3"]),
        sw_w2=np.asarray(inputs["L0_ffn_sw_w2"]),
        me_eq=np.asarray(inputs["L0_ffn_me_eq_w"]),
        me_wv=np.asarray(inputs["L0_ffn_me_wv"]),
        me_wo=np.asarray(inputs["L0_ffn_me_wo"]),
        me_c1=np.asarray(inputs["L0_ffn_me_c1w"]),
        me_c2=np.asarray(inputs["L0_ffn_me_c2w"]),
        ce_syn=np.asarray(inputs["L0_ffn_ce_syn_w"]),
        ce_wv=np.asarray(inputs["L0_ffn_ce_wv"]),
        ce_wo=np.asarray(inputs["L0_ffn_ce_wo"]),
        ce_f1=np.asarray(inputs["L0_ffn_ce_f1w"]),
        ce_f2=np.asarray(inputs["L0_ffn_ce_f2w"]),
        ce_gen=np.asarray(inputs["L0_ffn_ce_gen_w"]),
        ml_w1=np.asarray(inputs["L0_ffn_ml_w1"]),
        ml_w2=np.asarray(inputs["L0_ffn_ml_w2"]),
        vecs_fm=vecs_fm,
        bvrow=np.ascontiguousarray(
            np.asarray(inputs["L0_attn_b_qkv"][2048:3072])).reshape(1, 1024),
        ones_bf=onesb, ident=ident, maskA=maskA, maskC=maskC, hotrows=hotrows,
    )
    in_maps = []
    for b in range(B):
        m = dict(shared)
        m["x_fm"] = np.ascontiguousarray(np.asarray(x[b]).T)
        m["c_fm"] = np.ascontiguousarray(np.asarray(c[b]).reshape(8, 128).T)
        in_maps.append(m)
    return in_maps


def kernel(**inputs):
    if "nc" not in _CACHE:
        _CACHE["nc"] = build_nc()
    nc, _ = _CACHE["nc"]
    in_maps = _host_inputs(inputs)
    res = run_bass_kernel_spmd(nc, in_maps, core_ids=list(range(B)))
    out = np.stack([np.asarray(res.results[b]["y_fm"]).T for b in range(B)])
    return np.ascontiguousarray(out.astype(np.float32))


# version 12
# speedup vs baseline: 1.0467x; 1.0069x over previous
import contextlib
import numpy as np
import ml_dtypes
import concourse.bass as bass
import concourse.mybir as mybir
from concourse.bass_utils import run_bass_kernel_spmd

F32 = mybir.dt.float32
BF16 = mybir.dt.bfloat16
AF = mybir.ActivationFunctionType
ALU = mybir.AluOpType

B, T, D = 8, 4096, 1024
NQ = 4
TQ = T // NQ
FC = D // 128
EPS = 1e-5
SCALE = 1.0 / 512.0
RINV = 1.0 / 1024.0

V_LNW, V_LNB = 0, 1
V_EQB, V_MEBV, V_MEBO, V_C2B = 2, 3, 4, 5
(V_SYNB, V_CEBV, V_CEBO, V_N1G, V_N1B, V_F2B, V_N2G, V_N2B, V_GENB,
 V_MLB2) = 6, 7, 8, 9, 10, 11, 12, 13, 14, 15
V_BQ, V_BK, V_BV = 16, 17, 18
V_C1B0 = 19
V_F1B0 = 21
V_MLB1 = 23

DC_CAPL, DC_CBPL, DC_GA, DC_GB, DC_GW, DC_GB2 = 0, 8, 16, 24, 32, 40

ENGS = ("tensor", "vector", "scalar", "gpsimd", "sync")
LAST_MARKS = []
LAST_LOG = []


class Plan:

    def __init__(self, nc):
        self.nc = nc
        self.ops = {e: [] for e in ENGS}
        self.cnt = {}
        self.seen = {e: {} for e in ENGS}
        self.handles = {}
        self.marks = []
        self.mm_count = [0]
        self.log = []

    def mark(self, label):
        def go(e, _l=label, _p=self):
            _p.marks.append((_p.mm_count[0], _l))
        self.ops["tensor"].append(go)

    def sem(self, name):
        if name not in self.cnt:
            self.cnt[name] = 0
        return name

    def handle(self, name):
        return self.handles[name]

    def _bump(self, name, k):
        self.sem(name)
        self.cnt[name] += k
        return (name, self.cnt[name])

    def tok(self, eng):
        return (eng + "prog", self.cnt.get(eng + "prog", 0))

    def wait(self, eng, toks):
        best = {}
        for t in toks:
            if t is None:
                continue
            n, v = t
            if v > 0 and v > best.get(n, 0):
                best[n] = v
        todo = []
        for n, v in sorted(best.items()):
            if self.seen[eng].get(n, 0) < v:
                self.sem(n)
                todo.append((n, v))
                self.seen[eng][n] = v
        if todo:
            self.log.append(("wait", eng, tuple(todo)))

            def go(e, _todo=tuple(todo), _p=self):
                for n, v in _todo:
                    e.wait_ge(_p.handle(n), v)
            self.ops[eng].append(go)

    def op(self, eng, fn_name, *args, waits=(), inc=None, inc_k=1, **kw):
        self.wait(eng, list(waits))
        sem_name = inc if inc is not None else eng + "prog"
        tok = self._bump(sem_name, inc_k)
        aps = [a for a in args if hasattr(a, "ap") and hasattr(a, "tensor")]
        wr, rd = [], []
        if "out" in kw:
            wr.append(kw["out"])
        if "in_" in kw:
            rd.append(kw["in_"])
        if aps:
            wr.append(aps[0])
            rd.extend(aps[1:])
        for k2 in ("bias", "scale"):
            if k2 in kw and hasattr(kw[k2], "ap"):
                rd.append(kw[k2])
        self.log.append(("op", eng, tuple(rd), tuple(wr), ((sem_name, tok[1]),),
                         fn_name))

        def go(e, _fn=fn_name, _args=args, _kw=kw, _sem=sem_name, _k=inc_k, _p=self):
            ins = getattr(e, _fn)(*_args, **_kw)
            ins.then_inc(_p.handle(_sem), _k)
        self.ops[eng].append(go)
        return tok

    def rec(self, eng, reads=(), writes=(), incs=(), tag=""):
        self.log.append(("op", eng, tuple(reads), tuple(writes), tuple(incs), tag))

    def barrier(self, engs=("tensor", "vector", "scalar")):
        cur = {e: (e + "prog", self.cnt.get(e + "prog", 0)) for e in engs}
        for e in engs:
            self.wait(e, [cur[o] for o in engs if o != e])

    def emit(self, block):
        class _CountingTensor:
            def __init__(s, eng, p):
                s._e, s._p = eng, p

            def matmul(s, *a, **k):
                s._p.mm_count[0] += 1
                return s._e.matmul(*a, **k)

            def __getattr__(s, n):
                return getattr(s._e, n)

        for eng in ENGS:
            lst = self.ops[eng]
            if not lst:
                continue

            def run(engine, _lst=lst, _eng=eng):
                if _eng == "tensor":
                    engine = _CountingTensor(engine, self)
                for f in _lst:
                    f(engine)
            getattr(block, eng)(run)


class Bank:
    def __init__(self, ap, name):
        self.ap = ap
        self.name = name
        self.free_tok = None

    def waits(self):
        return [self.free_tok]

    def release(self, tok):
        self.free_tok = tok


def build_nc(debug=(), pace=()):
    nc = bass.Bass("TRN2", target_bir_lowering=False)
    plan = Plan(nc)
    din = {}

    def dt_in(name, shape, dtype=F32):
        din[name] = nc.dram_tensor(name, shape, dtype, kind="ExternalInput").ap()
        return din[name]

    x_in = dt_in("x_fm", [D, T])
    c_in = dt_in("c_fm", [128, FC])
    w_qkv = dt_in("w_qkv", [D, 3 * D])
    w_mod = dt_in("w_mod", [D, 6 * D])
    w_router = dt_in("w_router", [D, 4])
    rb_in = dt_in("rb", [1, 4])
    lb_in = dt_in("lb", [1, 4])
    sw_w1 = dt_in("sw_w1", [D, 4096])
    sw_w3 = dt_in("sw_w3", [D, 4096])
    sw_w2 = dt_in("sw_w2", [4096, D])
    me_eq = dt_in("me_eq", [D, D])
    me_wv = dt_in("me_wv", [D, D])
    me_wo = dt_in("me_wo", [D, D])
    me_c1 = dt_in("me_c1", [D, 2048])
    me_c2 = dt_in("me_c2", [2048, D])
    ce_syn = dt_in("ce_syn", [D, D])
    ce_wv = dt_in("ce_wv", [D, D])
    ce_wo = dt_in("ce_wo", [D, D])
    ce_f1 = dt_in("ce_f1", [D, 2048])
    ce_f2 = dt_in("ce_f2", [2048, D])
    ce_gen = dt_in("ce_gen", [D, D])
    ml_w1 = dt_in("ml_w1", [D, 4096])
    ml_w2 = dt_in("ml_w2", [4096, D])
    vecs_in = dt_in("vecs_fm", [128, 27, 8])
    bvrow_in = dt_in("bvrow", [1, D])
    ones_in = dt_in("ones_bf", [128, 128], BF16)
    ident_in = dt_in("ident", [128, 128])
    maskA_in = dt_in("maskA", [128, 128])
    maskC_in = dt_in("maskC", [128, 128])
    hot_in = dt_in("hotrows", [32, 4096], BF16)

    y_out = nc.dram_tensor("y_fm", [D, T], F32, kind="ExternalOutput").ap()
    wcache = nc.dram_tensor("wcache", [128, 327680], BF16, kind="Internal").ap()
    dbg = {}
    for k, shape, dtp in debug:
        dbg[k] = nc.dram_tensor("dbg_" + k, list(shape), dtp, kind="ExternalOutput").ap()

    es = contextlib.ExitStack()

    def sbt(name, shape, dtype=F32):
        return es.enter_context(nc.sbuf_tensor(name, shape, dtype))

    with es:
        xres = sbt("xres", [128, FC, TQ], F32)
        hbuf = sbt("hbuf", [128, FC, TQ], BF16)
        work = sbt("work", [128, 32768], BF16)
        wring = sbt("wring", [128, 20480], BF16)
        mstat = sbt("mstat", [128, 3, 1024], F32)
        statbf = sbt("statbf", [128, 2, 1024], BF16)
        xbsq = sbt("xbsq", [128, 4, 1024], BF16)
        wbg = sbt("wbg", [128, 1024], F32)
        gtmp = sbt("gtmp", [128, 512], F32)
        halo = sbt("halo", [128, 4096], BF16)
        stmp = sbt("stmp", [128, 3, 512], BF16)
        vecs = sbt("vecs_s", [128, 27, 8], F32)
        modc = sbt("modc", [128, 48], F32)
        dcols = sbt("dcols", [128, 64], F32)
        ones = sbt("ones_s", [128, 128], BF16)
        ident = sbt("ident_s", [128, 128], F32)
        maskA = sbt("maskA_s", [128, 128], F32)
        maskC = sbt("maskC_s", [128, 128], F32)
        hot = sbt("hot_s", [32, 4096], BF16)
        w32 = sbt("w32", [32, 128], BF16)
        wtm = sbt("wtm", [128, 32], F32)
        wq = sbt("wq", [128, 32], F32)
        rmath = sbt("rmath", [128, 12, 8], F32)
        rblr = sbt("rblr", [128, 32], F32)
        bvrows = sbt("bvrows", [128, D], BF16)
        epsc = sbt("epsc", [128, 1], F32)
        cfm = sbt("cfm", [128, FC], F32)
        silc = sbt("silc", [128, FC], BF16)
        rrow = mstat[0:1, 1, 0:8]
        rrbf = statbf[0:1, 1, 0:8]
        bvrow = mstat[0:1, 0, 0:1024]
        bvrbf = statbf[0:1, 0, 0:1024]

        pA = [es.enter_context(nc.psum_tensor(f"pA{i}", [128, 512], F32)) for i in range(4)]
        pO = [es.enter_context(nc.psum_tensor(f"pO{i}", [128, 1024], F32)) for i in range(2)]
        bankA = [Bank(pA[i][:], f"A{i}") for i in range(4)] + \
                [Bank(pO[i][:, j * 512:(j + 1) * 512], f"O{i}h{j}")
                 for i in range(2) for j in range(2)]
        pairs_ = [(pO[i][:], bankA[4 + 2 * i], bankA[5 + 2 * i]) for i in range(2)]

        class Pair:
            def __init__(self, ap, b0, b1):
                self.ap = ap
                self.b0, self.b1 = b0, b1

            @property
            def free_tok(self):
                return None

            def waits(self):
                return [self.b0.free_tok, self.b1.free_tok]

            def release(self, tok):
                self.b0.free_tok = tok
                self.b1.free_tok = tok

        rrA = [0]

        def takeA():
            b = bankA[rrA[0] % 8]
            rrA[0] += 1
            return b

        rrO = [0]

        def takeO():
            i = rrO[0] % 2
            rrO[0] += 1
            return Pair(*pairs_[i])

        def mm_acc(bank_ap, pairs, waits=(), flags=None):
            plan.wait("tensor", list(waits))
            tok = plan._bump("tensorprog", 1)
            plan.rec("tensor", reads=[a for pr in pairs for a in pr],
                     writes=[bank_ap], incs=[("tensorprog", tok[1])], tag="mm")
            n = len(pairs)
            if flags is None:
                flags = [(i == 0, i == n - 1) for i in range(n)]

            def go(t, _ap=bank_ap, _pairs=tuple(pairs), _flags=tuple(flags), _p=plan):
                ins = None
                for (lh, rh), (st, sp) in zip(_pairs, _flags):
                    ins = t.matmul(_ap, lh, rh, start=st, stop=sp)
                ins.then_inc(_p.handle("tensorprog"), 1)
            plan.ops["tensor"].append(go)
            return tok

        RING_UNITS, UNIT = 10, 2048

        class WRing:
            def __init__(self):
                self.head = 0
                self.loads = []
                self.ld_tok = []
                self.unit_free = [[] for _ in range(RING_UNITS)]
                self.cache_mode = None
                self.q_idx = 0
                self.fill_offs = []
                self.arena_top = 0

            def start_quarter(self, q):
                self.cache_mode = "fill" if q == 0 else "use"
                self.q_idx = 0

            def plan_load(self, src_ap, kc, inner):
                elems = kc * inner
                n_units = (elems + UNIT - 1) // UNIT
                if (self.head % RING_UNITS) + n_units > RING_UNITS:
                    while self.head % RING_UNITS != 0:
                        self.head += 1
                off = self.head % RING_UNITS
                idx = len(self.loads)
                wbk_tok = None
                if self.cache_mode == "fill":
                    aoff = self.arena_top
                    self.arena_top += elems
                    assert self.arena_top <= 327680, "wcache overflow"
                    self.fill_offs.append([aoff, kc, inner, None])
                    self.q_idx += 1
                elif self.cache_mode == "use":
                    aoff, kc0, in0, wtok = self.fill_offs[self.q_idx]
                    assert (kc0, in0) == (kc, inner), "load sequence mismatch"
                    src_ap = wcache[:, aoff:aoff + elems].rearrange(
                        "p (kc n) -> p kc n", kc=kc)
                    plan.wait("gpsimd", [wtok])
                    self.q_idx += 1
                self.loads.append((off, n_units, src_ap, kc, inner))
                self.head += n_units
                waits = [t for u in range(off, off + n_units)
                         for t in self.unit_free[u]]
                plan.wait("gpsimd", waits)
                sem = f"wl{idx % 16}"
                tok = plan._bump(sem, 16)
                plan.rec("gpsimd", reads=[],
                         writes=[wring[:, off * UNIT:off * UNIT + kc * inner]],
                         incs=[(sem, tok[1])], tag="wload")

                def go(g, _off=off, _src=src_ap, _kc=kc, _in=inner, _p=plan,
                       _sem=sem):
                    dst = wring[:, _off * UNIT:_off * UNIT + _kc * _in].rearrange(
                        "p (kc n) -> p kc n", kc=_kc)
                    g.dma_start(out=dst, in_=_src).then_inc(_p.handle(_sem), 16)
                plan.ops["gpsimd"].append(go)
                self.ld_tok.append((sem, tok[1]))
                if self.cache_mode == "fill":
                    aoff = self.fill_offs[-1][0]
                    wsem = f"wb{idx % 16}"
                    wtok = plan.op("sync", "dma_start",
                                   out=wcache[:, aoff:aoff + elems],
                                   in_=wring[:, off * UNIT:off * UNIT + elems],
                                   waits=[(sem, tok[1])], inc=wsem, inc_k=16)
                    self.fill_offs[-1][3] = wtok
                    for u in range(off, off + n_units):
                        self.unit_free[u] = list(self.unit_free[u]) + [wtok]
                return idx

            def view(self, idx):
                off, n, _src, kc, inner = self.loads[idx]
                return wring[:, off * UNIT:off * UNIT + kc * inner].rearrange(
                    "p (kc n) -> p kc n", kc=kc)

            def use_tok(self, idx):
                return self.ld_tok[idx]

            def release(self, idx, tok):
                off, n, _src, _kc, _in = self.loads[idx]
                for u in range(off, off + n):
                    keep = [t for t in self.unit_free[u]
                            if t and t[0].startswith("wb")]
                    self.unit_free[u] = keep + [tok]

        ring = WRing()
        plan.sem("wload")

        def w_src(w_ap, r0, r1, c0, c1):
            return w_ap[r0:r1, c0:c1].rearrange("(kc p) n -> p kc n", p=128)

        def load_cols(w_ap, c0, c1):
            return ring.plan_load(w_src(w_ap, 0, D, c0, c1), 8, c1 - c0)

        def load_rows(w_ap, r0, r1):
            return ring.plan_load(w_src(w_ap, r0, r1, 0, D), (r1 - r0) // 128, D)

        nconst = [0]

        def const_dma(dst, src):
            plan.op("sync", "dma_start", out=dst, in_=src, inc="cload", inc_k=16)
            nconst[0] += 1

        const_dma(vecs[:], vecs_in[:])
        const_dma(ones[:], ones_in[:])
        const_dma(ident[:], ident_in[:])
        const_dma(maskA[:], maskA_in[:])
        const_dma(maskC[:], maskC_in[:])
        const_dma(hot[:], hot_in[:])
        const_dma(cfm[:], c_in[:])
        const_dma(rrow[:, 0:4], rb_in[:])
        const_dma(rrow[:, 4:8], lb_in[:])
        const_dma(bvrow[:, :], bvrow_in[:])
        CONST = ("cload", 16 * nconst[0])

        def tap(name, ap, waits):
            if name in dbg:
                plan.op("sync", "dma_start", out=dbg[name][:], in_=ap,
                        waits=list(waits) + [CONST], inc="tapst", inc_k=16)
            elif name.split("_q")[0] in pace:
                plan.wait("sync", list(waits))

        eps_tok = plan.op("vector", "memset", epsc[:], EPS)
        sil_tok = plan.op("scalar", "activation", silc[:], cfm[:], AF.Silu,
                          waits=[CONST])

        wmod_ids = []
        ev_toks = []
        for mc in range(48):
            j, off = mc // 4, (mc % 4) * 128
            if mc % 4 == 0:
                wmod_ids.append(load_cols(w_mod, j * 512, (j + 1) * 512))
            lid = wmod_ids[j]
            wv = ring.view(lid)
            bank = takeA()
            pairs = [(wv[:, kc, off:off + 128], silc[:, kc:kc + 1]) for kc in range(8)]
            tok = mm_acc(bank.ap[:, 0:1], pairs,
                         waits=[ring.use_tok(lid), sil_tok, bank.free_tok])
            if mc % 4 == 3:
                ring.release(lid, tok)
            et = plan.op("vector", "tensor_copy", modc[:, mc:mc + 1], bank.ap[:, 0:1],
                         waits=[tok])
            bank.free_tok = et
            ev_toks.append(et)

        plan.op("vector", "tensor_scalar_add", dcols[:, DC_CAPL:DC_CAPL + 8],
                modc[:, 8:16], 1.0, waits=[ev_toks[15]])
        plan.op("vector", "tensor_scalar_add", dcols[:, DC_CBPL:DC_CBPL + 8],
                modc[:, 32:40], 1.0, waits=[ev_toks[39]])
        clip_prev = None
        for gi, (src0, dst) in enumerate(((16, DC_GA), (40, DC_GB))):
            sgn, mag = rmath[:, 2 * gi, :], rmath[:, 2 * gi + 1, :]
            t1 = plan.op("vector", "tensor_single_scalar", sgn, modc[:, src0:src0 + 8],
                         0.0, ALU.is_ge, waits=[ev_toks[-1], clip_prev])
            t2 = plan.op("vector", "tensor_scalar", sgn, sgn, 2.0, 1.0,
                         ALU.mult, ALU.subtract, waits=[t1])
            t3a = plan.op("scalar", "activation", mag, modc[:, src0:src0 + 8],
                          AF.Abs, waits=[ev_toks[-1]])
            t3 = plan.op("vector", "tensor_scalar_max", mag, mag, 0.25, waits=[t3a])
            clip_prev = plan.op("vector", "tensor_mul", dcols[:, dst:dst + 8],
                                sgn, mag, waits=[t2, t3])
        gtok = plan.tok("vector")
        plan.op("vector", "tensor_mul", dcols[:, DC_GW:DC_GW + 8],
                dcols[:, DC_GA:DC_GA + 8], vecs[:, V_LNW, :],
                waits=[gtok, CONST])
        plan.op("vector", "tensor_mul", dcols[:, DC_GB2:DC_GB2 + 8],
                dcols[:, DC_GA:DC_GA + 8], vecs[:, V_LNB, :])
        GATES = plan.tok("vector")
        tap("modc", modc[:], [GATES])
        tap("dcols", dcols[:], [GATES])

        ct = plan.op("vector", "tensor_copy", rrbf[:, :], rrow[:, :], waits=[CONST])
        bank = takeA()
        rtok = mm_acc(bank.ap[:, 0:4],
                      [(ones[0:1, :], rrbf[:, 0:4]), (ones[0:1, :], rrbf[:, 4:8])],
                      waits=[ct, bank.free_tok, CONST])
        prev = plan.op("vector", "tensor_copy", rblr[:, 0:4], bank.ap[:, 0:4], waits=[rtok])
        bank.free_tok = prev
        for t8 in range(1, 8):
            prev = plan.op("vector", "tensor_copy", rblr[:, t8 * 4:(t8 + 1) * 4],
                           rblr[:, 0:4], waits=[prev])
        RBL = prev

        cb = plan.op("vector", "tensor_copy", bvrbf[:, :], bvrow[:, :], waits=[CONST])
        for h in range(2):
            bank = takeA()
            tok = mm_acc(bank.ap[:], [(ones[0:1, :], bvrbf[:, h * 512:(h + 1) * 512])],
                         waits=[cb, bank.free_tok])
            ftok = plan.op("vector", "tensor_copy", bvrows[:, h * 512:(h + 1) * 512],
                           bank.ap[:], waits=[tok])
            bank.free_tok = ftok
        BVROWS = plan.tok("vector")

        kslots = work[:, 0:8192].rearrange("p (s e) -> p s e", e=1024)
        vslots = work[:, 8192:16384].rearrange("p (s e) -> p s e", e=1024)
        zq = work[:, 16384:24576].rearrange("p (dc t) -> p dc t", t=1024)
        zst = [work[:, 24576 + r * 384:24576 + (r + 1) * 384].rearrange(
            "p (i t) -> p i t", t=128) for r in range(2)]
        osb = work[:, 25600:29696].rearrange("p (dc t) -> p dc t", t=512)
        kh = halo[:, 0:2048].rearrange("p (s e) -> p s e", e=1024)
        vh = halo[:, 2048:4096].rearrange("p (s e) -> p s e", e=1024)
        uview = work[:, 0:32768].rearrange("p (hf t) -> p hf t", t=1024)

        def chain(base, nfc=8):
            return work[:, base:base + nfc * 1024].rearrange("p (fc t) -> p fc t", t=1024)

        lnwar = [None]

        def ln(width, csl, src_fn, out_fn, pre_toks=()):
            nh = width // 512
            if width == 1024:
                s1b, s2b = takeO(), takeO()
            else:
                s1b, s2b = takeA(), takeA()
            s1, s2 = s1b.ap, s2b.ap
            seg_t = {}
            last1 = last2 = None
            for fc in range(FC):
                slot = fc % 2
                ap, is_f32 = src_fn(fc)
                ap = ap[:, csl]
                wl = list(pre_toks) if fc < 2 else [seg_t[fc - 2]]
                if is_f32:
                    c1 = plan.op("vector", "tensor_copy", xbsq[:, slot, 0:width], ap,
                                 waits=wl)
                    src_bf = xbsq[:, slot, 0:width]
                    c2 = plan.op("vector", "tensor_mul", xbsq[:, 2 + slot, 0:width],
                                 src_bf, src_bf, waits=[c1])
                else:
                    src_bf = ap
                    c2 = plan.op("vector", "tensor_mul", xbsq[:, 2 + slot, 0:width],
                                 ap, ap, waits=wl)
                w0 = [c2] + (s1b.waits() + s2b.waits() if fc == 0 else [])
                for h in range(nh):
                    hs = slice(h * 512, (h + 1) * 512)
                    last1 = mm_acc(s1[:, hs], [(ones[:], src_bf[:, hs])], waits=w0,
                                   flags=[(fc == 0, fc == FC - 1)])
                    w0 = []
                    last2 = mm_acc(s2[:, hs], [(ones[:], xbsq[:, 2 + slot, hs])],
                                   flags=[(fc == 0, fc == FC - 1)])
                seg_t[fc] = last2
            m = mstat[:, 0, 0:width]
            qq = mstat[:, 1, 0:width]
            tp = mstat[:, 2, 0:width]
            a1 = plan.op("scalar", "activation", m, s1[:, 0:width], AF.Copy,
                         scale=RINV, waits=[last1, lnwar[0]])
            a2 = plan.op("scalar", "activation", qq, s2[:, 0:width], AF.Copy,
                         scale=RINV, waits=[last2])
            s1b.release(a1)
            s2b.release(a2)
            v1 = plan.op("vector", "tensor_mul", tp, m, m, waits=[a1])
            v2 = plan.op("vector", "tensor_tensor", tp, qq, tp, ALU.subtract,
                         waits=[v1, a2])
            l1 = plan.op("scalar", "activation", tp, tp, AF.Ln, bias=epsc[:, 0:1],
                         waits=[v2, eps_tok])
            l2 = plan.op("scalar", "activation", tp, tp, AF.Exp, scale=-0.5,
                         waits=[l1])
            mb = plan.op("vector", "tensor_copy", statbf[:, 0, 0:width], m, waits=[a1])
            rb_ = plan.op("vector", "tensor_copy", statbf[:, 1, 0:width], tp, waits=[l2])
            outs = []
            prev_out = None
            for fc in range(FC):
                ap, is_f32 = src_fn(fc)
                ap = ap[:, csl]
                slot = fc % 2
                wl = [mb, rb_]
                if fc >= 2:
                    wl.append(outs[fc - 2])
                if is_f32:
                    tt = plan.op("vector", "tensor_tensor", xbsq[:, slot, 0:width],
                                 ap, m, ALU.subtract, waits=wl)
                else:
                    tt = plan.op("vector", "tensor_tensor", xbsq[:, slot, 0:width],
                                 ap, statbf[:, 0, 0:width], ALU.subtract, waits=wl)
                t2 = plan.op("vector", "tensor_mul", xbsq[:, slot, 0:width],
                             xbsq[:, slot, 0:width], statbf[:, 1, 0:width],
                             waits=[tt])
                prev_out = out_fn(fc, xbsq[:, slot, 0:width], t2, prev_out)
                outs.append(prev_out)
            lnwar[0] = plan.tok("vector")
            return outs

        prev_yhalf = [None, None]
        for q in range(NQ):
            ring.start_quarter(q)
            xin_h = []
            for n in range(2):
                nsl = slice(n * 512, (n + 1) * 512)
                xin_h.append(plan.op(
                    "sync", "dma_start",
                    out=xres[:, :, nsl],
                    in_=x_in[:, q * TQ + n * 512:q * TQ + (n + 1) * 512].rearrange(
                        "(fc p) t -> p fc t", p=128),
                    waits=([prev_yhalf[n]] if prev_yhalf[n] else []),
                    inc=f"xload{n}", inc_k=16))

            def src_x(fc):
                return (xres[:, fc, :], True)

            plan.mark("LN1")
            h1tok = [None, None]
            for chnk in range(2):
                csl = slice(chnk * 512, (chnk + 1) * 512)

                def out_h1(fc, tile, tok, prev, _csl=csl):
                    return plan.op("vector", "tensor_scalar", hbuf[:, fc, _csl],
                                   tile,
                                   dcols[:, DC_CAPL + fc:DC_CAPL + fc + 1],
                                   modc[:, 0 + fc:0 + fc + 1],
                                   ALU.mult, ALU.add, waits=[tok, GATES])

                ln(512, csl, src_x, out_h1, pre_toks=[xin_h[chnk], GATES])
                h1tok[chnk] = plan.tok("vector")
            if q == 0:
                tap("h1", hbuf[:, :, :], [plan.tok("vector")])

            plan.mark("qkv")
            for j in range(6):
                lid = load_cols(w_qkv, j * 512, (j + 1) * 512)
                wv = ring.view(lid)
                if j < 2:
                    for i in range(4):
                        dc = j * 4 + i
                        for ch in range(2):
                            tsl = slice(ch * 512, (ch + 1) * 512)
                            bank = takeA()
                            tok = mm_acc(bank.ap[:],
                                         [(wv[:, kc, i * 128:(i + 1) * 128],
                                           hbuf[:, kc, tsl]) for kc in range(8)],
                                         waits=[ring.use_tok(lid), bank.free_tok,
                                                h1tok[ch]])
                            et = plan.op("scalar", "activation", zq[:, dc, tsl],
                                         bank.ap[:], AF.Identity,
                                         bias=vecs[:, V_BQ, dc:dc + 1], waits=[tok])
                            bank.free_tok = et
                elif j < 4:
                    for i in range(4):
                        dc = (j - 2) * 4 + i
                        for ch in range(2):
                            tsl = slice(ch * 512, (ch + 1) * 512)
                            bank = takeA()
                            tok = mm_acc(bank.ap[:],
                                         [(wv[:, kc, i * 128:(i + 1) * 128],
                                           hbuf[:, kc, tsl]) for kc in range(8)],
                                         waits=[ring.use_tok(lid), bank.free_tok,
                                                h1tok[ch]])
                            dst = kslots[:, ch * 4:(ch + 1) * 4,
                                         dc * 128:(dc + 1) * 128]
                            src = bank.ap[:].rearrange("p (s t) -> p s t", t=128)
                            et = plan.op("scalar", "activation", dst, src, AF.Identity,
                                         bias=vecs[:, V_BK, dc:dc + 1], waits=[tok])
                            bank.free_tok = et
                else:
                    dv = j - 4
                    for ch in range(2):
                        for lt in range(4):
                            ttok = slice(ch * 512 + lt * 128, ch * 512 + (lt + 1) * 128)
                            bank = takeA()
                            tok = mm_acc(bank.ap[:],
                                         [(hbuf[:, kc, ttok], wv[:, kc, :])
                                          for kc in range(8)],
                                         waits=[ring.use_tok(lid), bank.free_tok,
                                                h1tok[ch]])
                            et = plan.op("vector", "tensor_add",
                                         vslots[:, ch * 4 + lt,
                                                dv * 512:(dv + 1) * 512],
                                         bank.ap[:], bvrows[:, dv * 512:(dv + 1) * 512],
                                         waits=[tok, BVROWS])
                            bank.free_tok = et
                ring.release(lid, plan.tok("tensor") if j < 4 else plan.tok("vector"))
            plan.barrier()
            if q == 0:
                tap("zq", work[:, 16384:24576], [plan.tok("vector")])
                tap("kv", work[:, 0:16384], [plan.tok("vector")])

            qkv_done = plan.tok("tensor")
            o_mm_tok = [None, None]

            def emit_st(ch, lt):
                QTl = ch * 4 + lt
                QTg = q * 8 + QTl
                pairs_kt = [kt for kt in range(QTg - 2, QTg + 1) if kt >= 0]
                r = QTl % 2
                st_toks = []
                for pi, ktg in enumerate(pairs_kt):
                    if ktg >= q * 8:
                        kv_ = kslots[:, ktg - q * 8, :].rearrange(
                            "p (dc t) -> p dc t", t=128)
                    else:
                        kv_ = kh[:, ktg - (q * 8 - 2), :].rearrange(
                            "p (dc t) -> p dc t", t=128)
                    bank = takeA()
                    tok = mm_acc(bank.ap[:, 0:128],
                                 [(kv_[:, dc, :],
                                   zq[:, dc, ch * 512 + lt * 128:
                                      ch * 512 + (lt + 1) * 128])
                                  for dc in range(8)],
                                 waits=[bank.free_tok])
                    ewl = [tok, o_mm_tok[r]]
                    if ktg == QTg:
                        et = plan.op("vector", "tensor_mul", zst[r][:, pi, :],
                                     bank.ap[:, 0:128], maskC[:], waits=ewl)
                    elif ktg == QTg - 1:
                        et = plan.op("scalar", "activation", zst[r][:, pi, :],
                                     bank.ap[:, 0:128], AF.Copy, scale=SCALE,
                                     waits=ewl)
                    else:
                        et = plan.op("vector", "tensor_mul", zst[r][:, pi, :],
                                     bank.ap[:, 0:128], maskA[:], waits=ewl)
                    bank.free_tok = et
                    st_toks.append(et)
                return pairs_kt, r, st_toks

            def emit_o(ch, lt, pairs_kt, r, st_toks):
                ob = takeO()
                plan.wait("tensor", ob.waits() + st_toks)
                otok = plan._bump("tensorprog", 1)
                plan.rec("tensor",
                         reads=[vslots[:, :, :], vh[:, :, :], zst[r][:, :, :]],
                         writes=[ob.ap[:]], incs=[("tensorprog", otok[1])],
                         tag="o_mm")

                def go_o(t, _pairs=tuple(pairs_kt), _q=q, _r=r, _ap=ob.ap, _p=plan):
                    ins = None
                    for dc in range(8):
                        for pi, ktg in enumerate(_pairs):
                            if ktg >= _q * 8:
                                vv = vslots[:, ktg - _q * 8, :]
                            else:
                                vv = vh[:, ktg - (_q * 8 - 2), :]
                            ins = t.matmul(_ap[:, dc * 128:(dc + 1) * 128],
                                           vv[:, dc * 128:(dc + 1) * 128],
                                           zst[_r][:, pi, :],
                                           start=(pi == 0),
                                           stop=(pi == len(_pairs) - 1))
                    ins.then_inc(_p.handle("tensorprog"), 1)
                plan.ops["tensor"].append(go_o)
                o_mm_tok[r] = ("tensorprog", otok[1])
                et = plan.op("vector", "tensor_copy",
                             osb[:, :, lt * 128:(lt + 1) * 128],
                             ob.ap[:].rearrange("p (dc t) -> p dc t", t=128),
                             waits=[o_mm_tok[r]])
                ob.release(et)

            def attn_chunk(ch):
                pend = emit_st(ch, 0)
                for lt in range(4):
                    nxt = emit_st(ch, lt + 1) if lt < 3 else None
                    emit_o(ch, lt, *pend)
                    pend = nxt

            def attn_ln(ch):
                tsl = slice(ch * 512, (ch + 1) * 512)

                def src_o(fc):
                    return (osb[:, fc, :], False)

                def out_attn(fc, tile, tok, prev, _tsl=tsl):
                    u = plan.op("vector", "tensor_scalar", gtmp[:, :], tile,
                                dcols[:, DC_GW + fc:DC_GW + fc + 1],
                                dcols[:, DC_GB2 + fc:DC_GB2 + fc + 1],
                                ALU.mult, ALU.add, waits=[tok, prev, GATES])
                    return plan.op("vector", "tensor_tensor", xres[:, fc, _tsl],
                                   xres[:, fc, _tsl], gtmp[:, :], ALU.add,
                                   waits=[u])

                ln(512, slice(0, 512), src_o, out_attn,
                   pre_toks=[plan.tok("vector")])

            h2tok = [None, None]

            def ln2_chunk(ch):
                csl = slice(ch * 512, (ch + 1) * 512)

                def out_h2(fc, tile, tok, prev, _csl=csl):
                    return plan.op("vector", "tensor_scalar", hbuf[:, fc, _csl],
                                   tile,
                                   dcols[:, DC_CBPL + fc:DC_CBPL + fc + 1],
                                   modc[:, 24 + fc:24 + fc + 1],
                                   ALU.mult, ALU.add, waits=[tok, GATES, qkv_done])

                ln(512, csl, src_x, out_h2, pre_toks=[plan.tok("vector"), GATES])
                h2tok[ch] = plan.tok("vector")

            plan.mark("attn0")
            attn_chunk(0)
            if q == 0:
                tap("osb", work[:, 25600:29696], [plan.tok("vector")])
            plan.mark("attnln0")
            attn_ln(0)
            plan.mark("attn1")
            attn_chunk(1)
            plan.mark("ln2_0")
            ln2_chunk(0)
            plan.mark("attnln1")
            attn_ln(1)
            plan.mark("ln2_1")
            ln2_chunk(1)
            tap(f"x1_q{q}", xres[:, :, :], [plan.tok("vector")])
            tap(f"h2_q{q}", hbuf[:, :, :], [plan.tok("vector")])

            ttk = plan.tok("tensor")
            for i in range(2):
                plan.op("vector", "tensor_copy", kh[:, i, :], kslots[:, 6 + i, :],
                        waits=[ttk])
                plan.op("vector", "tensor_copy", vh[:, i, :], vslots[:, 6 + i, :])

            gate_prev = [plan.tok("vector")]

            def make_wb(e):
                ob = takeO()
                plan.wait("tensor", ob.waits() + [w32t, CONST])
                tok = plan._bump("tensorprog", 1)
                plan.rec("tensor", reads=[hot[:, :], w32[:, :]], writes=[ob.ap[:]],
                         incs=[("tensorprog", tok[1])], tag="wb_mm")

                def go(t, _e=e, _ap=ob.ap, _p=plan):
                    ins = None
                    for t8 in range(8):
                        qrow = t8 * 4 + _e
                        ins = t.matmul(_ap[:, t8 * 128:(t8 + 1) * 128],
                                       hot[:, qrow * 128:(qrow + 1) * 128], w32[:],
                                       start=True, stop=True)
                    ins.then_inc(_p.handle("tensorprog"), 1)
                plan.ops["tensor"].append(go)
                et = plan.op("vector", "tensor_copy", wbg[:], ob.ap[:],
                             waits=[("tensorprog", tok[1]), gate_prev[0]])
                ob.release(et)
                return et

            def gate_evac(bank_ap, fc, n, bias_col, wb_tok, mmtok):
                nsl = slice(n * 512, (n + 1) * 512)
                wl = [mmtok, wb_tok, gate_prev[0], GATES]
                if bias_col is not None:
                    g1 = plan.op("vector", "scalar_tensor_tensor", gtmp[:, :],
                                 bank_ap, bias_col, wbg[:, nsl],
                                 ALU.add, ALU.mult, waits=wl)
                else:
                    g1 = plan.op("vector", "tensor_mul", gtmp[:, :],
                                 bank_ap, wbg[:, nsl], waits=wl)
                g2 = plan.op("vector", "scalar_tensor_tensor", xres[:, fc, nsl],
                             gtmp[:, :], dcols[:, DC_GB + fc:DC_GB + fc + 1],
                             xres[:, fc, nsl], ALU.mult, ALU.add, waits=[g1])
                gate_prev[0] = g2
                return g2

            h_rhs = lambda kc, nsl: hbuf[:, kc, nsl]

            def dense_stage(w_ids, rhs_view, out_cb):
                toks = []
                for fc in range(FC):
                    for n in range(2):
                        nsl = slice(n * 512, (n + 1) * 512)
                        lid = w_ids[fc // 4]
                        off = (fc % 4) * 128
                        wv = ring.view(lid)
                        bank = takeA()
                        tok = mm_acc(bank.ap[:],
                                     [(wv[:, kc, off:off + 128], rhs_view(kc, nsl))
                                      for kc in range(8)],
                                     waits=[ring.use_tok(lid), bank.free_tok])
                        toks.append(out_cb(fc, n, bank, tok))
                return toks

            def evac_to(chain_view, bias_row):
                def cb(fc, n, bank, tok):
                    et = plan.op("scalar", "activation",
                                 chain_view[:, fc, n * 512:(n + 1) * 512],
                                 bank.ap[:], AF.Identity,
                                 bias=vecs[:, bias_row, fc:fc + 1], waits=[tok])
                    bank.free_tok = et
                    return et
                return cb

            def big_out(w_ids, rhs_fcview, nkc, out_cb, rhs_rec=None):
                assert len(w_ids) <= 2
                for fc in range(FC):
                    for n in range(2):
                        nsl = slice(n * 512, (n + 1) * 512)
                        bank = takeA()
                        plan.wait("tensor", [bank.free_tok] +
                                  [ring.use_tok(i) for i in w_ids])
                        tok = plan._bump("tensorprog", 1)
                        plan.rec("tensor",
                                 reads=[ring.view(i) for i in w_ids] +
                                       [rhs_rec if rhs_rec is not None
                                        else work[:, :]],
                                 writes=[bank.ap[:]],
                                 incs=[("tensorprog", tok[1])], tag="bigout_mm")

                        def go(t, _fc=fc, _nsl=nsl, _ap=bank.ap, _ids=tuple(w_ids),
                               _rv=rhs_fcview, _nkc=nkc, _p=plan):
                            ins = None
                            per = _nkc // len(_ids)
                            for seg, lid in enumerate(_ids):
                                wv = ring.view(lid)
                                for s in range(per):
                                    kk = seg * per + s
                                    ins = t.matmul(_ap,
                                                   wv[:, s, _fc * 128:(_fc + 1) * 128],
                                                   _rv(kk, _nsl),
                                                   start=(kk == 0), stop=(kk == _nkc - 1))
                            ins.then_inc(_p.handle("tensorprog"), 1)
                        plan.ops["tensor"].append(go)
                        out_cb(fc, n, bank, ("tensorprog", tok[1]))

            def rounds_out(w_dram, rhs_fcview, gate_cb):
                for r in range(2):
                    tb = [takeA() for _ in range(4)] + [takeO(), takeO()]
                    aps = [tb[i].ap for i in range(4)] + \
                          [tb[4].ap[:, 0:512], tb[4].ap[:, 512:1024],
                           tb[5].ap[:, 0:512], tb[5].ap[:, 512:1024]]
                    tiles = [(4 * r + i, n) for i in range(4) for n in range(2)]
                    base = plan.cnt["tensorprog"]
                    seg_ids = []
                    for s in range(4):
                        seg_ids.append(ring.plan_load(
                            w_src(w_dram, s * 1024, (s + 1) * 1024,
                                  r * 512, (r + 1) * 512), 8, 512))
                        ring.release(seg_ids[s], ("tensorprog", base + s + 1))
                    plan.wait("tensor",
                              sum((b.waits() for b in tb), []) +
                              [ring.use_tok(seg_ids[0]), plan.tok("vector"),
                               plan.tok("scalar")])
                    plan._bump("tensorprog", 4)
                    segw = [ring.use_tok(i) for i in seg_ids]
                    for seg in range(4):
                        if seg > 0:
                            plan.log.append(("wait", "tensor", ((segw[seg][0],
                                                                segw[seg][1]),)))
                        plan.rec("tensor",
                                 reads=[ring.view(seg_ids[seg]), work[:, :]],
                                 writes=list(aps),
                                 incs=[("tensorprog", base + seg + 1)],
                                 tag="rounds_mm")

                    def go(t, _tiles=tuple(tiles), _aps=tuple(aps),
                           _ids=tuple(seg_ids), _rv=rhs_fcview,
                           _segw=tuple(segw), _p=plan, _r=r):
                        for seg in range(4):
                            if seg > 0:
                                t.wait_ge(_p.handle(_segw[seg][0]), _segw[seg][1])
                            wv = ring.view(_ids[seg])
                            ins = None
                            for s in range(8):
                                kk = seg * 8 + s
                                for ti, (outf, n) in enumerate(_tiles):
                                    ins = t.matmul(
                                        _aps[ti],
                                        wv[:, s, (outf - 4 * _r) * 128:
                                           (outf - 4 * _r + 1) * 128],
                                        _rv(kk, slice(n * 512, (n + 1) * 512)),
                                        start=(kk == 0), stop=(kk == 31))
                            ins.then_inc(_p.handle("tensorprog"), 1)
                    plan.ops["tensor"].append(go)
                    done = ("tensorprog", base + 4)
                    gs = []
                    for ti, (outf, n) in enumerate(tiles):
                        gs.append(gate_cb(aps[ti], outf, n, done))
                    for ti in range(4):
                        tb[ti].release(gs[ti])
                    tb[4].release(gs[5])
                    tb[5].release(gs[7])


            u_rhs = lambda kk, nsl: uview[:, kk, nsl]

            u_toks = [None] * 3
            rwq = [None]

            def emit_router():
                rt_id = ring.plan_load(w_src(w_router, 0, D, 0, 4), 8, 4)
                rtv = ring.view(rt_id)
                bank = takeA()
                plan.wait("tensor", [bank.free_tok, ring.use_tok(rt_id), h2tok[1]])
                lt_tok = plan._bump("tensorprog", 1)
                plan.rec("tensor", reads=[hbuf[:, :, :], rtv[:, :, :]],
                         writes=[bank.ap[:, 0:32]],
                         incs=[("tensorprog", lt_tok[1])], tag="router_mm")

                def go_rt(t, _ap=bank.ap, _rtv=rtv, _p=plan):
                    ins = None
                    for t8 in range(8):
                        for kc in range(8):
                            ins = t.matmul(_ap[:, t8 * 4:(t8 + 1) * 4],
                                           hbuf[:, kc, t8 * 128:(t8 + 1) * 128],
                                           _rtv[:, kc, 0:4],
                                           start=(kc == 0), stop=(kc == 7))
                    ins.then_inc(_p.handle("tensorprog"), 1)
                plan.ops["tensor"].append(go_rt)
                ring.release(rt_id, ("tensorprog", lt_tok[1]))
                lg = plan.op("vector", "tensor_add", wtm[:], bank.ap[:, 0:32], rblr[:],
                             waits=[("tensorprog", lt_tok[1]), RBL])
                bank.free_tok = lg
                l_ = lambda e: wtm[:, e:32:4]
                ra = lambda i: rmath[:, i, :]
                ta = plan.op("vector", "tensor_tensor", ra(0), l_(0), l_(1), ALU.max,
                             waits=[lg])
                tb = plan.op("vector", "tensor_tensor", ra(1), l_(0), l_(1), ALU.min)
                tc_ = plan.op("vector", "tensor_tensor", ra(2), l_(2), l_(3), ALU.max)
                td = plan.op("vector", "tensor_tensor", ra(3), l_(2), l_(3), ALU.min)
                t4 = plan.op("vector", "tensor_tensor", ra(4), ra(0), ra(2), ALU.max,
                             waits=[ta, tc_])
                t5 = plan.op("vector", "tensor_tensor", ra(5), ra(0), ra(2), ALU.min)
                t6 = plan.op("vector", "tensor_tensor", ra(6), ra(1), ra(3), ALU.max,
                             waits=[tb, td])
                t7 = plan.op("vector", "tensor_tensor", ra(7), ra(5), ra(6), ALU.max,
                             waits=[t5, t6])
                t8_ = plan.op("vector", "tensor_tensor", ra(8), ra(7), ra(4), ALU.subtract,
                              waits=[t4, t7])
                e2_ = plan.op("scalar", "activation", ra(8), ra(8), AF.Exp, waits=[t8_])
                t9 = plan.op("vector", "tensor_scalar_add", ra(8), ra(8), 1.0, waits=[e2_])
                t10 = plan.op("vector", "reciprocal", ra(9), ra(8), waits=[t9])
                wq_toks = rwq[0] = []
                for e in range(4):
                    de = plan.op("vector", "tensor_tensor", ra(10), l_(e), ra(4),
                                 ALU.subtract, waits=[t4, t10] + wq_toks[-1:])
                    ee = plan.op("scalar", "activation", ra(10), ra(10), AF.Exp, waits=[de])
                    se = plan.op("vector", "tensor_tensor", ra(11), l_(e), ra(7),
                                 ALU.is_ge, waits=[t7])
                    m1_ = plan.op("vector", "tensor_tensor", ra(10), ra(10), ra(11),
                                  ALU.mult, waits=[ee, se])
                    wq_toks.append(plan.op("vector", "tensor_tensor", wq[:, e:32:4],
                                           ra(10), ra(9), ALU.mult, waits=[m1_]))
                if q == 0:
                    tap("wtm", wtm[:], [plan.tok("vector")])
                    tap("wq", wq[:], [plan.tok("vector")])

            plan.mark("e0u")
            for hf in range(32):
                if hf % 4 == 0:
                    j = hf // 4
                    lid1 = load_cols(sw_w1, j * 512, (j + 1) * 512)
                    lid3 = load_cols(sw_w3, j * 512, (j + 1) * 512)
                off = (hf % 4) * 128
                w1v, w3v = ring.view(lid1), ring.view(lid3)
                for n in range(2):
                    nsl = slice(n * 512, (n + 1) * 512)
                    b1, b3 = takeA(), takeA()
                    t1 = mm_acc(b1.ap[:],
                                [(w1v[:, kc, off:off + 128], hbuf[:, kc, nsl])
                                 for kc in range(8)],
                                waits=[ring.use_tok(lid1), b1.free_tok, h2tok[n]])
                    t3 = mm_acc(b3.ap[:],
                                [(w3v[:, kc, off:off + 128], hbuf[:, kc, nsl])
                                 for kc in range(8)],
                                waits=[ring.use_tok(lid3), b3.free_tok])
                    si = (2 * hf + n) % 3
                    ss = stmp[:, si, :]
                    st = plan.op("scalar", "activation", ss, b1.ap[:], AF.Silu,
                                 waits=[t1, u_toks[si]])
                    b1.free_tok = st
                    ut = plan.op("vector", "tensor_mul", uview[:, hf, nsl],
                                 ss, b3.ap[:], waits=[st, t3])
                    b3.free_tok = ut
                    u_toks[si] = ut
                if hf % 4 == 3:
                    ring.release(lid1, plan.tok("tensor"))
                    ring.release(lid3, plan.tok("tensor"))
                if hf == 3:
                    emit_router()
            wq_toks = rwq[0]

            bank = takeA()
            trt = plan.op("tensor", "transpose", bank.ap[0:32, 0:128], wq[:], ident[:],
                          waits=wq_toks + [bank.free_tok, CONST], inc="tensorprog")
            w32t = plan.op("vector", "tensor_copy", w32[:], bank.ap[0:32, 0:128],
                           waits=[trt])
            bank.free_tok = w32t
            wb_tok = make_wb(0)
            plan.barrier()
            if q == 0:
                tap("u0", work[:, 0:32768], [plan.tok("vector")])

            def gate_cb_e0(ap, outf, n, tok, _wb=wb_tok):
                return gate_evac(ap, outf, n, None, _wb, tok)

            plan.mark("e0out")
            rounds_out(sw_w2, u_rhs, gate_cb_e0)
            plan.barrier()

            plan.mark("e1")
            wb_tok = make_wb(1)
            c0, c1v, c2v = chain(0), chain(8192), chain(16384)
            gz = work[:, 0:16384].rearrange("p (fc t) -> p fc t", t=1024)
            eq_ids = [load_cols(me_eq, j * 512, (j + 1) * 512) for j in range(2)]
            dense_stage(eq_ids, h_rhs, evac_to(c2v, V_EQB))
            for i in eq_ids:
                ring.release(i, plan.tok("tensor"))
            plan.barrier()
            wv_ids = [load_cols(me_wv, j * 512, (j + 1) * 512) for j in range(2)]
            c2_rhs = lambda kc, nsl: c2v[:, kc, nsl]
            dense_stage(wv_ids, c2_rhs, evac_to(c1v, V_MEBV))
            for i in wv_ids:
                ring.release(i, plan.tok("tensor"))
            plan.barrier()
            wo_ids = [load_cols(me_wo, j * 512, (j + 1) * 512) for j in range(2)]
            c1_rhs = lambda kc, nsl: c1v[:, kc, nsl]
            dense_stage(wo_ids, c1_rhs, evac_to(c2v, V_MEBO))
            for i in wo_ids:
                ring.release(i, plan.tok("tensor"))
            plan.barrier()
            c1_ids = [load_cols(me_c1, j * 512, (j + 1) * 512) for j in range(4)]
            for fc in range(16):
                for n in range(2):
                    nsl = slice(n * 512, (n + 1) * 512)
                    lid = c1_ids[fc // 4]
                    off = (fc % 4) * 128
                    wv = ring.view(lid)
                    bank = takeA()
                    tok = mm_acc(bank.ap[:],
                                 [(wv[:, kc, off:off + 128], c2v[:, kc, nsl])
                                  for kc in range(8)],
                                 waits=[ring.use_tok(lid), bank.free_tok])
                    et = plan.op("scalar", "activation", gz[:, fc, nsl], bank.ap[:],
                                 AF.Gelu,
                                 bias=vecs[:, V_C1B0 + fc // 8, fc % 8:fc % 8 + 1],
                                 waits=[tok])
                    bank.free_tok = et
            for i in c1_ids:
                ring.release(i, plan.tok("tensor"))
            plan.barrier()
            c2_ids = [load_rows(me_c2, j * 1024, (j + 1) * 1024) for j in range(2)]
            gz_rhs = lambda kk, nsl: gz[:, kk, nsl]

            def cb_e1(fc, n, bank, tok, _wb=wb_tok):
                g = gate_evac(bank.ap[:], fc, n, vecs[:, V_C2B, fc:fc + 1], _wb, tok)
                bank.free_tok = g
                return g

            big_out(c2_ids, gz_rhs, 16, cb_e1, rhs_rec=work[:, 0:16384])
            for i in c2_ids:
                ring.release(i, plan.tok("vector"))
            plan.barrier()

            plan.mark("e2")
            wb_tok = make_wb(2)
            syn_ids = [load_cols(ce_syn, j * 512, (j + 1) * 512) for j in range(2)]
            dense_stage(syn_ids, h_rhs, evac_to(c0, V_SYNB))
            for i in syn_ids:
                ring.release(i, plan.tok("tensor"))
            plan.barrier()
            cewv_ids = [load_cols(ce_wv, j * 512, (j + 1) * 512) for j in range(2)]
            c0_rhs = lambda kc, nsl: c0[:, kc, nsl]
            dense_stage(cewv_ids, c0_rhs, evac_to(c1v, V_CEBV))
            for i in cewv_ids:
                ring.release(i, plan.tok("tensor"))
            plan.barrier()
            cewo_ids = [load_cols(ce_wo, j * 512, (j + 1) * 512) for j in range(2)]

            def cb_ssum(fc, n, bank, tok):
                nsl = slice(n * 512, (n + 1) * 512)
                et = plan.op("vector", "scalar_tensor_tensor", c2v[:, fc, nsl],
                             bank.ap[:], vecs[:, V_CEBO, fc:fc + 1], c0[:, fc, nsl],
                             ALU.add, ALU.add, waits=[tok])
                bank.free_tok = et
                return et

            dense_stage(cewo_ids, c1_rhs, cb_ssum)
            for i in cewo_ids:
                ring.release(i, plan.tok("tensor"))
            plan.barrier()

            def src_ssum(fc):
                return (c2v[:, fc, :], False)
            plan.mark("n1")
            h2a_tok = [None, None]
            for chnk in range(2):
                csl = slice(chnk * 512, (chnk + 1) * 512)
                ln(512, csl, src_ssum,
                   lambda fc, tile, tok, prev, _csl=csl:
                       plan.op("vector", "tensor_scalar", c0[:, fc, _csl], tile,
                               vecs[:, V_N1G, fc:fc + 1], vecs[:, V_N1B, fc:fc + 1],
                               ALU.mult, ALU.add, waits=[tok]),
                   pre_toks=[plan.tok("vector")])
                h2a_tok[chnk] = plan.tok("vector")
            gz2 = work[:, 8192:24576].rearrange("p (fc t) -> p fc t", t=1024)
            f1_ids = [load_cols(ce_f1, j * 512, (j + 1) * 512) for j in range(4)]
            for n in range(2):
                nsl = slice(n * 512, (n + 1) * 512)
                for fc in range(16):
                    lid = f1_ids[fc // 4]
                    off = (fc % 4) * 128
                    wv = ring.view(lid)
                    bank = takeA()
                    tok = mm_acc(bank.ap[:],
                                 [(wv[:, kc, off:off + 128], c0[:, kc, nsl])
                                  for kc in range(8)],
                                 waits=[ring.use_tok(lid), bank.free_tok,
                                        h2a_tok[n]])
                    et = plan.op("scalar", "activation", gz2[:, fc, nsl], bank.ap[:],
                                 AF.Relu,
                                 bias=vecs[:, V_F1B0 + fc // 8, fc % 8:fc % 8 + 1],
                                 waits=[tok])
                    bank.free_tok = et
            for i in f1_ids:
                ring.release(i, plan.tok("tensor"))
            plan.barrier()
            c3 = work[:, 24576:32768].rearrange("p (fc t) -> p fc t", t=1024)
            f2_ids = [load_rows(ce_f2, j * 1024, (j + 1) * 1024) for j in range(2)]
            gz2_rhs = lambda kk, nsl: gz2[:, kk, nsl]

            def cb_s2(fc, n, bank, tok):
                nsl = slice(n * 512, (n + 1) * 512)
                et = plan.op("vector", "scalar_tensor_tensor", c3[:, fc, nsl],
                             bank.ap[:], vecs[:, V_F2B, fc:fc + 1], c0[:, fc, nsl],
                             ALU.add, ALU.add, waits=[tok])
                bank.free_tok = et
                return et

            big_out(f2_ids, gz2_rhs, 16, cb_s2, rhs_rec=work[:, 8192:24576])
            for i in f2_ids:
                ring.release(i, plan.tok("vector"))
            plan.barrier()
            h2b = work[:, 8192:16384].rearrange("p (fc t) -> p fc t", t=1024)

            def src_s2(fc):
                return (c3[:, fc, :], False)

            plan.mark("n2")
            h2b_tok = [None, None]
            for chnk in range(2):
                csl = slice(chnk * 512, (chnk + 1) * 512)
                ln(512, csl, src_s2,
                   lambda fc, tile, tok, prev, _csl=csl:
                       plan.op("vector", "tensor_scalar", h2b[:, fc, _csl], tile,
                               vecs[:, V_N2G, fc:fc + 1], vecs[:, V_N2B, fc:fc + 1],
                               ALU.mult, ALU.add, waits=[tok]),
                   pre_toks=[plan.tok("vector")])
                h2b_tok[chnk] = plan.tok("vector")
            gen_ids = [load_cols(ce_gen, j * 512, (j + 1) * 512) for j in range(2)]
            for n in range(2):
                nsl = slice(n * 512, (n + 1) * 512)
                for fc in range(FC):
                    lid = gen_ids[fc // 4]
                    off = (fc % 4) * 128
                    wv = ring.view(lid)
                    bank = takeA()
                    tok = mm_acc(bank.ap[:],
                                 [(wv[:, kc, off:off + 128], h2b[:, kc, nsl])
                                  for kc in range(8)],
                                 waits=[ring.use_tok(lid), bank.free_tok,
                                        h2b_tok[n]])
                    g = gate_evac(bank.ap[:], fc, n, vecs[:, V_GENB, fc:fc + 1],
                                  wb_tok, tok)
                    bank.free_tok = g
            for i in gen_ids:
                ring.release(i, plan.tok("vector"))
            plan.barrier()

            plan.mark("e3")
            wb_tok = make_wb(3)
            for hf in range(32):
                if hf % 4 == 0:
                    lid = load_cols(ml_w1, (hf // 4) * 512, (hf // 4 + 1) * 512)
                off = (hf % 4) * 128
                wv = ring.view(lid)
                for n in range(2):
                    nsl = slice(n * 512, (n + 1) * 512)
                    bank = takeA()
                    tok = mm_acc(bank.ap[:],
                                 [(wv[:, kc, off:off + 128], hbuf[:, kc, nsl])
                                  for kc in range(8)],
                                 waits=[ring.use_tok(lid), bank.free_tok])
                    et = plan.op("scalar", "activation", uview[:, hf, nsl], bank.ap[:],
                                 AF.Gelu,
                                 bias=vecs[:, V_MLB1 + hf // 8, hf % 8:hf % 8 + 1],
                                 waits=[tok])
                    bank.free_tok = et
                if hf % 4 == 3:
                    ring.release(lid, plan.tok("scalar"))
            plan.barrier()

            def gate_cb_e3(ap, outf, n, tok, _wb=wb_tok):
                return gate_evac(ap, outf, n, vecs[:, V_MLB2, outf:outf + 1],
                                 _wb, tok)

            plan.mark("e3out")
            rounds_out(ml_w2, u_rhs, gate_cb_e3)
            xdone = plan.tok("vector")

            yt = plan.op("sync", "dma_start",
                         out=y_out[:, q * TQ:(q + 1) * TQ].rearrange(
                             "(fc p) t -> p fc t", p=128),
                         in_=xres[:, :, :],
                         waits=[xdone], inc="ystore", inc_k=16)
            prev_yhalf = [yt, yt]
            plan.barrier()

        tap("fin_modc", modc[:], [plan.tok("vector")])
        tap("fin_dcols", dcols[:], [plan.tok("vector")])
        tap("fin_rblr", rblr[:], [plan.tok("vector")])
        tap("fin_bvrows", bvrows[:], [plan.tok("vector")])
        tap("fin_maskA", maskA[:], [plan.tok("vector")])
        tap("fin_ones", ones[:], [plan.tok("vector")])
        tap("fin_vecs", vecs[:], [plan.tok("vector")])
        tap("fin_silc", silc[:], [plan.tok("vector")])
        tap("fin_epsc", epsc[:], [plan.tok("vector")])
        plan.wait("sync", [("ystore", plan.cnt["ystore"])] +
                  ([("tapst", plan.cnt["tapst"])] if "tapst" in plan.cnt else []))

        for n in sorted(plan.cnt.keys()):
            plan.handles[n] = es.enter_context(nc.semaphore(n))
        block = es.enter_context(nc.Block())
        plan.emit(block)

    LAST_MARKS[:] = plan.marks
    LAST_LOG[:] = plan.log
    return nc, din


_CACHE = {}


def _host_inputs(inputs):
    x = np.asarray(inputs["L0_attn_x"])
    c = np.asarray(inputs["L0_c"])
    vec_rows = [
        inputs["L0_attn_ln_w"], inputs["L0_attn_ln_b"],
        inputs["L0_ffn_me_eq_b"], inputs["L0_ffn_me_bv"], inputs["L0_ffn_me_bo"],
        inputs["L0_ffn_me_c2b"],
        inputs["L0_ffn_ce_syn_b"], inputs["L0_ffn_ce_bv"], inputs["L0_ffn_ce_bo"],
        inputs["L0_ffn_ce_n1g"], inputs["L0_ffn_ce_n1b"], inputs["L0_ffn_ce_f2b"],
        inputs["L0_ffn_ce_n2g"], inputs["L0_ffn_ce_n2b"], inputs["L0_ffn_ce_gen_b"],
        inputs["L0_ffn_ml_b2"],
        inputs["L0_attn_b_qkv"][0:1024], inputs["L0_attn_b_qkv"][1024:2048],
        inputs["L0_attn_b_qkv"][2048:3072],
        inputs["L0_ffn_me_c1b"][0:1024], inputs["L0_ffn_me_c1b"][1024:2048],
        inputs["L0_ffn_ce_f1b"][0:1024], inputs["L0_ffn_ce_f1b"][1024:2048],
        inputs["L0_ffn_ml_b1"][0:1024], inputs["L0_ffn_ml_b1"][1024:2048],
        inputs["L0_ffn_ml_b1"][2048:3072], inputs["L0_ffn_ml_b1"][3072:4096],
    ]
    vecs = np.stack([np.asarray(r, dtype=np.float32) for r in vec_rows])
    vecs_fm = np.ascontiguousarray(vecs.reshape(27, 8, 128).transpose(2, 0, 1))
    onesb = np.ones((128, 128), dtype=ml_dtypes.bfloat16)
    ident = np.eye(128, dtype=np.float32)
    p = np.arange(128)[:, None]
    qc = np.arange(128)[None, :]
    maskA = ((qc < p).astype(np.float32)) * np.float32(SCALE)
    maskC = ((qc >= p).astype(np.float32)) * np.float32(SCALE)
    hotrows = np.zeros((32, 4096), dtype=ml_dtypes.bfloat16)
    for qrow in range(32):
        hotrows[qrow, qrow * 128:(qrow + 1) * 128] = 1
    shared = dict(
        w_qkv=np.asarray(inputs["L0_attn_W_qkv"]),
        w_mod=np.asarray(inputs["L0_wmod"]),
        w_router=np.asarray(inputs["L0_ffn_router_w"]),
        rb=np.asarray(inputs["L0_ffn_router_b"]).reshape(1, 4),
        lb=np.asarray(inputs["L0_ffn_load_balancer"]).reshape(1, 4),
        sw_w1=np.asarray(inputs["L0_ffn_sw_w1"]),
        sw_w3=np.asarray(inputs["L0_ffn_sw_w3"]),
        sw_w2=np.asarray(inputs["L0_ffn_sw_w2"]),
        me_eq=np.asarray(inputs["L0_ffn_me_eq_w"]),
        me_wv=np.asarray(inputs["L0_ffn_me_wv"]),
        me_wo=np.asarray(inputs["L0_ffn_me_wo"]),
        me_c1=np.asarray(inputs["L0_ffn_me_c1w"]),
        me_c2=np.asarray(inputs["L0_ffn_me_c2w"]),
        ce_syn=np.asarray(inputs["L0_ffn_ce_syn_w"]),
        ce_wv=np.asarray(inputs["L0_ffn_ce_wv"]),
        ce_wo=np.asarray(inputs["L0_ffn_ce_wo"]),
        ce_f1=np.asarray(inputs["L0_ffn_ce_f1w"]),
        ce_f2=np.asarray(inputs["L0_ffn_ce_f2w"]),
        ce_gen=np.asarray(inputs["L0_ffn_ce_gen_w"]),
        ml_w1=np.asarray(inputs["L0_ffn_ml_w1"]),
        ml_w2=np.asarray(inputs["L0_ffn_ml_w2"]),
        vecs_fm=vecs_fm,
        bvrow=np.ascontiguousarray(
            np.asarray(inputs["L0_attn_b_qkv"][2048:3072])).reshape(1, 1024),
        ones_bf=onesb, ident=ident, maskA=maskA, maskC=maskC, hotrows=hotrows,
    )
    in_maps = []
    for b in range(B):
        m = dict(shared)
        m["x_fm"] = np.ascontiguousarray(np.asarray(x[b]).T)
        m["c_fm"] = np.ascontiguousarray(np.asarray(c[b]).reshape(8, 128).T)
        in_maps.append(m)
    return in_maps


def kernel(**inputs):
    if "nc" not in _CACHE:
        _CACHE["nc"] = build_nc()
    nc, _ = _CACHE["nc"]
    in_maps = _host_inputs(inputs)
    res = run_bass_kernel_spmd(nc, in_maps, core_ids=list(range(B)))
    out = np.stack([np.asarray(res.results[b]["y_fm"]).T for b in range(B)])
    return np.ascontiguousarray(out.astype(np.float32))


# version 13
# speedup vs baseline: 1.0562x; 1.0091x over previous
import contextlib
import numpy as np
import ml_dtypes
import concourse.bass as bass
import concourse.mybir as mybir
from concourse.bass_utils import run_bass_kernel_spmd

F32 = mybir.dt.float32
BF16 = mybir.dt.bfloat16
AF = mybir.ActivationFunctionType
ALU = mybir.AluOpType

B, T, D = 8, 4096, 1024
NQ = 4
TQ = T // NQ
FC = D // 128
EPS = 1e-5
SCALE = 1.0 / 512.0
RINV = 1.0 / 1024.0

V_LNW, V_LNB = 0, 1
V_EQB, V_MEBV, V_MEBO, V_C2B = 2, 3, 4, 5
(V_SYNB, V_CEBV, V_CEBO, V_N1G, V_N1B, V_F2B, V_N2G, V_N2B, V_GENB,
 V_MLB2) = 6, 7, 8, 9, 10, 11, 12, 13, 14, 15
V_BQ, V_BK, V_BV = 16, 17, 18
V_C1B0 = 19
V_F1B0 = 21
V_MLB1 = 23

DC_CAPL, DC_CBPL, DC_GA, DC_GB, DC_GW, DC_GB2 = 0, 8, 16, 24, 32, 40

ENGS = ("tensor", "vector", "scalar", "gpsimd", "sync")
LAST_MARKS = []
LAST_LOG = []


class Plan:

    def __init__(self, nc):
        self.nc = nc
        self.ops = {e: [] for e in ENGS}
        self.cnt = {}
        self.seen = {e: {} for e in ENGS}
        self.handles = {}
        self.marks = []
        self.mm_count = [0]
        self.log = []

    def mark(self, label):
        def go(e, _l=label, _p=self):
            _p.marks.append((_p.mm_count[0], _l))
        self.ops["tensor"].append(go)

    def sem(self, name):
        if name not in self.cnt:
            self.cnt[name] = 0
        return name

    def handle(self, name):
        return self.handles[name]

    def _bump(self, name, k):
        self.sem(name)
        self.cnt[name] += k
        return (name, self.cnt[name])

    def tok(self, eng):
        return (eng + "prog", self.cnt.get(eng + "prog", 0))

    def wait(self, eng, toks):
        best = {}
        for t in toks:
            if t is None:
                continue
            n, v = t
            if v > 0 and v > best.get(n, 0):
                best[n] = v
        todo = []
        for n, v in sorted(best.items()):
            if self.seen[eng].get(n, 0) < v:
                self.sem(n)
                todo.append((n, v))
                self.seen[eng][n] = v
        if todo:
            self.log.append(("wait", eng, tuple(todo)))

            def go(e, _todo=tuple(todo), _p=self):
                for n, v in _todo:
                    e.wait_ge(_p.handle(n), v)
            self.ops[eng].append(go)

    def op(self, eng, fn_name, *args, waits=(), inc=None, inc_k=1, **kw):
        self.wait(eng, list(waits))
        sem_name = inc if inc is not None else eng + "prog"
        tok = self._bump(sem_name, inc_k)
        aps = [a for a in args if hasattr(a, "ap") and hasattr(a, "tensor")]
        wr, rd = [], []
        if "out" in kw:
            wr.append(kw["out"])
        if "in_" in kw:
            rd.append(kw["in_"])
        if aps:
            wr.append(aps[0])
            rd.extend(aps[1:])
        for k2 in ("bias", "scale"):
            if k2 in kw and hasattr(kw[k2], "ap"):
                rd.append(kw[k2])
        self.log.append(("op", eng, tuple(rd), tuple(wr), ((sem_name, tok[1]),),
                         fn_name))

        def go(e, _fn=fn_name, _args=args, _kw=kw, _sem=sem_name, _k=inc_k, _p=self):
            ins = getattr(e, _fn)(*_args, **_kw)
            ins.then_inc(_p.handle(_sem), _k)
        self.ops[eng].append(go)
        return tok

    def rec(self, eng, reads=(), writes=(), incs=(), tag=""):
        self.log.append(("op", eng, tuple(reads), tuple(writes), tuple(incs), tag))

    def barrier(self, engs=("tensor", "vector", "scalar")):
        cur = {e: (e + "prog", self.cnt.get(e + "prog", 0)) for e in engs}
        for e in engs:
            self.wait(e, [cur[o] for o in engs if o != e])

    def emit(self, block):
        class _CountingTensor:
            def __init__(s, eng, p):
                s._e, s._p = eng, p

            def matmul(s, *a, **k):
                s._p.mm_count[0] += 1
                return s._e.matmul(*a, **k)

            def __getattr__(s, n):
                return getattr(s._e, n)

        for eng in ENGS:
            lst = self.ops[eng]
            if not lst:
                continue

            def run(engine, _lst=lst, _eng=eng):
                if _eng == "tensor":
                    engine = _CountingTensor(engine, self)
                for f in _lst:
                    f(engine)
            getattr(block, eng)(run)


class Bank:
    def __init__(self, ap, name):
        self.ap = ap
        self.name = name
        self.free_tok = None

    def waits(self):
        return [self.free_tok]

    def release(self, tok):
        self.free_tok = tok


def build_nc(debug=(), pace=()):
    nc = bass.Bass("TRN2", target_bir_lowering=False)
    plan = Plan(nc)
    din = {}

    def dt_in(name, shape, dtype=F32):
        din[name] = nc.dram_tensor(name, shape, dtype, kind="ExternalInput").ap()
        return din[name]

    x_in = dt_in("x_fm", [D, T])
    c_in = dt_in("c_fm", [128, FC])
    w_qkv = dt_in("w_qkv", [D, 3 * D])
    w_mod = dt_in("w_mod", [D, 6 * D])
    w_router = dt_in("w_router", [D, 4])
    rb_in = dt_in("rb", [1, 4])
    lb_in = dt_in("lb", [1, 4])
    sw_w1 = dt_in("sw_w1", [D, 4096])
    sw_w3 = dt_in("sw_w3", [D, 4096])
    sw_w2 = dt_in("sw_w2", [4096, D])
    me_eq = dt_in("me_eq", [D, D])
    me_wv = dt_in("me_wv", [D, D])
    me_wo = dt_in("me_wo", [D, D])
    me_c1 = dt_in("me_c1", [D, 2048])
    me_c2 = dt_in("me_c2", [2048, D])
    ce_syn = dt_in("ce_syn", [D, D])
    ce_wv = dt_in("ce_wv", [D, D])
    ce_wo = dt_in("ce_wo", [D, D])
    ce_f1 = dt_in("ce_f1", [D, 2048])
    ce_f2 = dt_in("ce_f2", [2048, D])
    ce_gen = dt_in("ce_gen", [D, D])
    ml_w1 = dt_in("ml_w1", [D, 4096])
    ml_w2 = dt_in("ml_w2", [4096, D])
    vecs_in = dt_in("vecs_fm", [128, 27, 8])
    bvrow_in = dt_in("bvrow", [1, D])
    ones_in = dt_in("ones_bf", [128, 128], BF16)
    ident_in = dt_in("ident", [128, 128])
    maskA_in = dt_in("maskA", [128, 128])
    maskC_in = dt_in("maskC", [128, 128])
    hot_in = dt_in("hotrows", [32, 4096], BF16)

    y_out = nc.dram_tensor("y_fm", [D, T], F32, kind="ExternalOutput").ap()
    wcache = nc.dram_tensor("wcache", [128, 327680], BF16, kind="Internal").ap()
    dbg = {}
    for k, shape, dtp in debug:
        dbg[k] = nc.dram_tensor("dbg_" + k, list(shape), dtp, kind="ExternalOutput").ap()

    es = contextlib.ExitStack()

    def sbt(name, shape, dtype=F32):
        return es.enter_context(nc.sbuf_tensor(name, shape, dtype))

    with es:
        xres = sbt("xres", [128, FC, TQ], F32)
        hbuf = sbt("hbuf", [128, FC, TQ], BF16)
        work = sbt("work", [128, 32768], BF16)
        wring = sbt("wring", [128, 20480], BF16)
        mstat = sbt("mstat", [128, 3, 1024], F32)
        statbf = sbt("statbf", [128, 2, 1024], BF16)
        xbsq = sbt("xbsq", [128, 4, 1024], BF16)
        wbg = sbt("wbg", [128, 1024], F32)
        gtmp = sbt("gtmp", [128, 512], F32)
        halo = sbt("halo", [128, 4096], BF16)
        stmp = sbt("stmp", [128, 3, 512], BF16)
        vecs = sbt("vecs_s", [128, 27, 8], F32)
        modc = sbt("modc", [128, 48], F32)
        dcols = sbt("dcols", [128, 64], F32)
        ones = sbt("ones_s", [128, 128], BF16)
        ident = sbt("ident_s", [128, 128], F32)
        maskA = sbt("maskA_s", [128, 128], F32)
        maskC = sbt("maskC_s", [128, 128], F32)
        hot = sbt("hot_s", [32, 4096], BF16)
        w32 = sbt("w32", [32, 128], BF16)
        wtm = sbt("wtm", [128, 32], F32)
        wq = sbt("wq", [128, 32], F32)
        rmath = sbt("rmath", [128, 12, 8], F32)
        rblr = sbt("rblr", [128, 32], F32)
        bvrows = sbt("bvrows", [128, D], BF16)
        epsc = sbt("epsc", [128, 1], F32)
        cfm = sbt("cfm", [128, FC], F32)
        silc = sbt("silc", [128, FC], BF16)
        rrow = mstat[0:1, 1, 0:8]
        rrbf = statbf[0:1, 1, 0:8]
        bvrow = mstat[0:1, 0, 0:1024]
        bvrbf = statbf[0:1, 0, 0:1024]

        pA = [es.enter_context(nc.psum_tensor(f"pA{i}", [128, 512], F32)) for i in range(4)]
        pO = [es.enter_context(nc.psum_tensor(f"pO{i}", [128, 1024], F32)) for i in range(2)]
        bankA = [Bank(pA[i][:], f"A{i}") for i in range(4)] + \
                [Bank(pO[i][:, j * 512:(j + 1) * 512], f"O{i}h{j}")
                 for i in range(2) for j in range(2)]
        pairs_ = [(pO[i][:], bankA[4 + 2 * i], bankA[5 + 2 * i]) for i in range(2)]

        class Pair:
            def __init__(self, ap, b0, b1):
                self.ap = ap
                self.b0, self.b1 = b0, b1

            @property
            def free_tok(self):
                return None

            def waits(self):
                return [self.b0.free_tok, self.b1.free_tok]

            def release(self, tok):
                self.b0.free_tok = tok
                self.b1.free_tok = tok

        rrA = [0]

        def takeA():
            b = bankA[rrA[0] % 8]
            rrA[0] += 1
            return b

        rrO = [0]

        def takeO():
            i = rrO[0] % 2
            rrO[0] += 1
            return Pair(*pairs_[i])

        def mm_acc(bank_ap, pairs, waits=(), flags=None):
            plan.wait("tensor", list(waits))
            tok = plan._bump("tensorprog", 1)
            plan.rec("tensor", reads=[a for pr in pairs for a in pr],
                     writes=[bank_ap], incs=[("tensorprog", tok[1])], tag="mm")
            n = len(pairs)
            if flags is None:
                flags = [(i == 0, i == n - 1) for i in range(n)]

            def go(t, _ap=bank_ap, _pairs=tuple(pairs), _flags=tuple(flags), _p=plan):
                ins = None
                for (lh, rh), (st, sp) in zip(_pairs, _flags):
                    ins = t.matmul(_ap, lh, rh, start=st, stop=sp)
                ins.then_inc(_p.handle("tensorprog"), 1)
            plan.ops["tensor"].append(go)
            return tok

        RING_UNITS, UNIT = 10, 2048

        class WRing:
            def __init__(self):
                self.head = 0
                self.loads = []
                self.ld_tok = []
                self.unit_free = [[] for _ in range(RING_UNITS)]
                self.cache_mode = None
                self.q_idx = 0
                self.fill_offs = []
                self.arena_top = 0

            def start_quarter(self, q):
                self.cache_mode = "fill" if q == 0 else "use"
                self.q_idx = 0

            def plan_load(self, src_ap, kc, inner):
                elems = kc * inner
                n_units = (elems + UNIT - 1) // UNIT
                if (self.head % RING_UNITS) + n_units > RING_UNITS:
                    while self.head % RING_UNITS != 0:
                        self.head += 1
                off = self.head % RING_UNITS
                idx = len(self.loads)
                wbk_tok = None
                if self.cache_mode == "fill":
                    aoff = self.arena_top
                    self.arena_top += elems
                    assert self.arena_top <= 327680, "wcache overflow"
                    self.fill_offs.append([aoff, kc, inner, None])
                    self.q_idx += 1
                elif self.cache_mode == "use":
                    aoff, kc0, in0, wtok = self.fill_offs[self.q_idx]
                    assert (kc0, in0) == (kc, inner), "load sequence mismatch"
                    src_ap = wcache[:, aoff:aoff + elems].rearrange(
                        "p (kc n) -> p kc n", kc=kc)
                    plan.wait("gpsimd", [wtok])
                    self.q_idx += 1
                self.loads.append((off, n_units, src_ap, kc, inner))
                self.head += n_units
                waits = [t for u in range(off, off + n_units)
                         for t in self.unit_free[u]]
                plan.wait("gpsimd", waits)
                sem = f"wl{idx % 16}"
                tok = plan._bump(sem, 16)
                plan.rec("gpsimd", reads=[],
                         writes=[wring[:, off * UNIT:off * UNIT + kc * inner]],
                         incs=[(sem, tok[1])], tag="wload")

                def go(g, _off=off, _src=src_ap, _kc=kc, _in=inner, _p=plan,
                       _sem=sem):
                    dst = wring[:, _off * UNIT:_off * UNIT + _kc * _in].rearrange(
                        "p (kc n) -> p kc n", kc=_kc)
                    g.dma_start(out=dst, in_=_src).then_inc(_p.handle(_sem), 16)
                plan.ops["gpsimd"].append(go)
                self.ld_tok.append((sem, tok[1]))
                if self.cache_mode == "fill":
                    aoff = self.fill_offs[-1][0]
                    wsem = f"wb{idx % 16}"
                    wtok = plan.op("sync", "dma_start",
                                   out=wcache[:, aoff:aoff + elems],
                                   in_=wring[:, off * UNIT:off * UNIT + elems],
                                   waits=[(sem, tok[1])], inc=wsem, inc_k=16)
                    self.fill_offs[-1][3] = wtok
                    for u in range(off, off + n_units):
                        self.unit_free[u] = list(self.unit_free[u]) + [wtok]
                return idx

            def view(self, idx):
                off, n, _src, kc, inner = self.loads[idx]
                return wring[:, off * UNIT:off * UNIT + kc * inner].rearrange(
                    "p (kc n) -> p kc n", kc=kc)

            def use_tok(self, idx):
                return self.ld_tok[idx]

            def release(self, idx, tok):
                off, n, _src, _kc, _in = self.loads[idx]
                for u in range(off, off + n):
                    keep = [t for t in self.unit_free[u]
                            if t and t[0].startswith("wb")]
                    self.unit_free[u] = keep + [tok]

        ring = WRing()
        plan.sem("wload")

        def w_src(w_ap, r0, r1, c0, c1):
            return w_ap[r0:r1, c0:c1].rearrange("(kc p) n -> p kc n", p=128)

        def load_cols(w_ap, c0, c1):
            return ring.plan_load(w_src(w_ap, 0, D, c0, c1), 8, c1 - c0)

        def load_rows(w_ap, r0, r1):
            return ring.plan_load(w_src(w_ap, r0, r1, 0, D), (r1 - r0) // 128, D)

        nconst = [0]

        def const_dma(dst, src):
            plan.op("sync", "dma_start", out=dst, in_=src, inc="cload", inc_k=16)
            nconst[0] += 1

        const_dma(vecs[:], vecs_in[:])
        const_dma(ones[:], ones_in[:])
        const_dma(ident[:], ident_in[:])
        const_dma(maskA[:], maskA_in[:])
        const_dma(maskC[:], maskC_in[:])
        const_dma(hot[:], hot_in[:])
        const_dma(cfm[:], c_in[:])
        const_dma(rrow[:, 0:4], rb_in[:])
        const_dma(rrow[:, 4:8], lb_in[:])
        const_dma(bvrow[:, :], bvrow_in[:])
        CONST = ("cload", 16 * nconst[0])

        def tap(name, ap, waits):
            if name in dbg:
                plan.op("sync", "dma_start", out=dbg[name][:], in_=ap,
                        waits=list(waits) + [CONST], inc="tapst", inc_k=16)
            elif name.split("_q")[0] in pace:
                plan.wait("sync", list(waits))

        eps_tok = plan.op("vector", "memset", epsc[:], EPS)
        sil_tok = plan.op("scalar", "activation", silc[:], cfm[:], AF.Silu,
                          waits=[CONST])

        wmod_ids = []
        ev_toks = []
        for mc in range(48):
            j, off = mc // 4, (mc % 4) * 128
            if mc % 4 == 0:
                wmod_ids.append(load_cols(w_mod, j * 512, (j + 1) * 512))
            lid = wmod_ids[j]
            wv = ring.view(lid)
            bank = takeA()
            pairs = [(wv[:, kc, off:off + 128], silc[:, kc:kc + 1]) for kc in range(8)]
            tok = mm_acc(bank.ap[:, 0:1], pairs,
                         waits=[ring.use_tok(lid), sil_tok, bank.free_tok])
            if mc % 4 == 3:
                ring.release(lid, tok)
            et = plan.op("vector", "tensor_copy", modc[:, mc:mc + 1], bank.ap[:, 0:1],
                         waits=[tok])
            bank.free_tok = et
            ev_toks.append(et)

        plan.op("vector", "tensor_scalar_add", dcols[:, DC_CAPL:DC_CAPL + 8],
                modc[:, 8:16], 1.0, waits=[ev_toks[15]])
        plan.op("vector", "tensor_scalar_add", dcols[:, DC_CBPL:DC_CBPL + 8],
                modc[:, 32:40], 1.0, waits=[ev_toks[39]])
        clip_prev = None
        for gi, (src0, dst) in enumerate(((16, DC_GA), (40, DC_GB))):
            sgn, mag = rmath[:, 2 * gi, :], rmath[:, 2 * gi + 1, :]
            t1 = plan.op("vector", "tensor_single_scalar", sgn, modc[:, src0:src0 + 8],
                         0.0, ALU.is_ge, waits=[ev_toks[-1], clip_prev])
            t2 = plan.op("vector", "tensor_scalar", sgn, sgn, 2.0, 1.0,
                         ALU.mult, ALU.subtract, waits=[t1])
            t3a = plan.op("scalar", "activation", mag, modc[:, src0:src0 + 8],
                          AF.Abs, waits=[ev_toks[-1]])
            t3 = plan.op("vector", "tensor_scalar_max", mag, mag, 0.25, waits=[t3a])
            clip_prev = plan.op("vector", "tensor_mul", dcols[:, dst:dst + 8],
                                sgn, mag, waits=[t2, t3])
        gtok = plan.tok("vector")
        plan.op("vector", "tensor_mul", dcols[:, DC_GW:DC_GW + 8],
                dcols[:, DC_GA:DC_GA + 8], vecs[:, V_LNW, :],
                waits=[gtok, CONST])
        plan.op("vector", "tensor_mul", dcols[:, DC_GB2:DC_GB2 + 8],
                dcols[:, DC_GA:DC_GA + 8], vecs[:, V_LNB, :])
        GATES = plan.tok("vector")
        tap("modc", modc[:], [GATES])
        tap("dcols", dcols[:], [GATES])

        ct = plan.op("vector", "tensor_copy", rrbf[:, :], rrow[:, :], waits=[CONST])
        bank = takeA()
        rtok = mm_acc(bank.ap[:, 0:4],
                      [(ones[0:1, :], rrbf[:, 0:4]), (ones[0:1, :], rrbf[:, 4:8])],
                      waits=[ct, bank.free_tok, CONST])
        prev = plan.op("vector", "tensor_copy", rblr[:, 0:4], bank.ap[:, 0:4], waits=[rtok])
        bank.free_tok = prev
        for t8 in range(1, 8):
            prev = plan.op("vector", "tensor_copy", rblr[:, t8 * 4:(t8 + 1) * 4],
                           rblr[:, 0:4], waits=[prev])
        RBL = prev

        cb = plan.op("vector", "tensor_copy", bvrbf[:, :], bvrow[:, :], waits=[CONST])
        for h in range(2):
            bank = takeA()
            tok = mm_acc(bank.ap[:], [(ones[0:1, :], bvrbf[:, h * 512:(h + 1) * 512])],
                         waits=[cb, bank.free_tok])
            ftok = plan.op("vector", "tensor_copy", bvrows[:, h * 512:(h + 1) * 512],
                           bank.ap[:], waits=[tok])
            bank.free_tok = ftok
        BVROWS = plan.tok("vector")

        kslots = work[:, 0:8192].rearrange("p (s e) -> p s e", e=1024)
        vslots = work[:, 8192:16384].rearrange("p (s e) -> p s e", e=1024)
        zq = work[:, 16384:24576].rearrange("p (dc t) -> p dc t", t=1024)
        zst = [work[:, 24576 + r * 384:24576 + (r + 1) * 384].rearrange(
            "p (i t) -> p i t", t=128) for r in range(2)]
        osb = work[:, 25600:29696].rearrange("p (dc t) -> p dc t", t=512)
        kh = halo[:, 0:2048].rearrange("p (s e) -> p s e", e=1024)
        vh = halo[:, 2048:4096].rearrange("p (s e) -> p s e", e=1024)
        uview = work[:, 0:32768].rearrange("p (hf t) -> p hf t", t=1024)

        def chain(base, nfc=8):
            return work[:, base:base + nfc * 1024].rearrange("p (fc t) -> p fc t", t=1024)

        lnwar = [None]

        def ln(width, csl, src_fn, out_fn, pre_toks=()):
            nh = width // 512
            if width == 1024:
                s1b, s2b = takeO(), takeO()
            else:
                s1b, s2b = takeA(), takeA()
            s1, s2 = s1b.ap, s2b.ap
            seg_t = {}
            last1 = last2 = None
            for fc in range(FC):
                slot = fc % 2
                ap, is_f32 = src_fn(fc)
                ap = ap[:, csl]
                wl = list(pre_toks) if fc < 2 else [seg_t[fc - 2]]
                if is_f32:
                    c1 = plan.op("vector", "tensor_copy", xbsq[:, slot, 0:width], ap,
                                 waits=wl)
                    src_bf = xbsq[:, slot, 0:width]
                    c2 = plan.op("vector", "tensor_mul", xbsq[:, 2 + slot, 0:width],
                                 src_bf, src_bf, waits=[c1])
                else:
                    src_bf = ap
                    c2 = plan.op("vector", "tensor_mul", xbsq[:, 2 + slot, 0:width],
                                 ap, ap, waits=wl)
                w0 = [c2] + (s1b.waits() + s2b.waits() if fc == 0 else [])
                for h in range(nh):
                    hs = slice(h * 512, (h + 1) * 512)
                    last1 = mm_acc(s1[:, hs], [(ones[:], src_bf[:, hs])], waits=w0,
                                   flags=[(fc == 0, fc == FC - 1)])
                    w0 = []
                    last2 = mm_acc(s2[:, hs], [(ones[:], xbsq[:, 2 + slot, hs])],
                                   flags=[(fc == 0, fc == FC - 1)])
                seg_t[fc] = last2
            m = mstat[:, 0, 0:width]
            qq = mstat[:, 1, 0:width]
            tp = mstat[:, 2, 0:width]
            a1 = plan.op("scalar", "activation", m, s1[:, 0:width], AF.Copy,
                         scale=RINV, waits=[last1, lnwar[0]])
            a2 = plan.op("scalar", "activation", qq, s2[:, 0:width], AF.Copy,
                         scale=RINV, waits=[last2])
            s1b.release(a1)
            s2b.release(a2)
            v1 = plan.op("vector", "tensor_mul", tp, m, m, waits=[a1])
            v2 = plan.op("vector", "tensor_tensor", tp, qq, tp, ALU.subtract,
                         waits=[v1, a2])
            l1 = plan.op("scalar", "activation", tp, tp, AF.Ln, bias=epsc[:, 0:1],
                         waits=[v2, eps_tok])
            l2 = plan.op("scalar", "activation", tp, tp, AF.Exp, scale=-0.5,
                         waits=[l1])
            mb = plan.op("vector", "tensor_copy", statbf[:, 0, 0:width], m, waits=[a1])
            rb_ = plan.op("vector", "tensor_copy", statbf[:, 1, 0:width], tp, waits=[l2])
            outs = []
            prev_out = None
            for fc in range(FC):
                ap, is_f32 = src_fn(fc)
                ap = ap[:, csl]
                slot = fc % 2
                wl = [mb, rb_]
                if fc >= 2:
                    wl.append(outs[fc - 2])
                if is_f32:
                    tt = plan.op("vector", "tensor_tensor", xbsq[:, slot, 0:width],
                                 ap, m, ALU.subtract, waits=wl)
                else:
                    tt = plan.op("vector", "tensor_tensor", xbsq[:, slot, 0:width],
                                 ap, statbf[:, 0, 0:width], ALU.subtract, waits=wl)
                t2 = plan.op("vector", "tensor_mul", xbsq[:, slot, 0:width],
                             xbsq[:, slot, 0:width], statbf[:, 1, 0:width],
                             waits=[tt])
                prev_out = out_fn(fc, xbsq[:, slot, 0:width], t2, prev_out)
                outs.append(prev_out)
            lnwar[0] = plan.tok("vector")
            return outs

        prev_yhalf = [None, None]
        for q in range(NQ):
            ring.start_quarter(q)
            xin_h = []
            for n in range(2):
                nsl = slice(n * 512, (n + 1) * 512)
                xin_h.append(plan.op(
                    "sync", "dma_start",
                    out=xres[:, :, nsl],
                    in_=x_in[:, q * TQ + n * 512:q * TQ + (n + 1) * 512].rearrange(
                        "(fc p) t -> p fc t", p=128),
                    waits=([prev_yhalf[n]] if prev_yhalf[n] else []),
                    inc=f"xload{n}", inc_k=16))

            def src_x(fc):
                return (xres[:, fc, :], True)

            plan.mark("LN1")
            h1tok = [None, None]
            for chnk in range(2):
                csl = slice(chnk * 512, (chnk + 1) * 512)

                def out_h1(fc, tile, tok, prev, _csl=csl):
                    return plan.op("vector", "tensor_scalar", hbuf[:, fc, _csl],
                                   tile,
                                   dcols[:, DC_CAPL + fc:DC_CAPL + fc + 1],
                                   modc[:, 0 + fc:0 + fc + 1],
                                   ALU.mult, ALU.add, waits=[tok, GATES])

                ln(512, csl, src_x, out_h1, pre_toks=[xin_h[chnk], GATES])
                h1tok[chnk] = plan.tok("vector")
            if q == 0:
                tap("h1", hbuf[:, :, :], [plan.tok("vector")])

            plan.mark("qkv")
            for j in range(6):
                lid = load_cols(w_qkv, j * 512, (j + 1) * 512)
                wv = ring.view(lid)
                if j < 2:
                    for i in range(4):
                        dc = j * 4 + i
                        for ch in range(2):
                            tsl = slice(ch * 512, (ch + 1) * 512)
                            bank = takeA()
                            tok = mm_acc(bank.ap[:],
                                         [(wv[:, kc, i * 128:(i + 1) * 128],
                                           hbuf[:, kc, tsl]) for kc in range(8)],
                                         waits=[ring.use_tok(lid), bank.free_tok,
                                                h1tok[ch]])
                            et = plan.op("scalar", "activation", zq[:, dc, tsl],
                                         bank.ap[:], AF.Identity,
                                         bias=vecs[:, V_BQ, dc:dc + 1], waits=[tok])
                            bank.free_tok = et
                elif j < 4:
                    for i in range(4):
                        dc = (j - 2) * 4 + i
                        for ch in range(2):
                            tsl = slice(ch * 512, (ch + 1) * 512)
                            bank = takeA()
                            tok = mm_acc(bank.ap[:],
                                         [(wv[:, kc, i * 128:(i + 1) * 128],
                                           hbuf[:, kc, tsl]) for kc in range(8)],
                                         waits=[ring.use_tok(lid), bank.free_tok,
                                                h1tok[ch]])
                            dst = kslots[:, ch * 4:(ch + 1) * 4,
                                         dc * 128:(dc + 1) * 128]
                            src = bank.ap[:].rearrange("p (s t) -> p s t", t=128)
                            et = plan.op("scalar", "activation", dst, src, AF.Identity,
                                         bias=vecs[:, V_BK, dc:dc + 1], waits=[tok])
                            bank.free_tok = et
                else:
                    dv = j - 4
                    for ch in range(2):
                        for lt in range(4):
                            ttok = slice(ch * 512 + lt * 128, ch * 512 + (lt + 1) * 128)
                            bank = takeA()
                            tok = mm_acc(bank.ap[:],
                                         [(hbuf[:, kc, ttok], wv[:, kc, :])
                                          for kc in range(8)],
                                         waits=[ring.use_tok(lid), bank.free_tok,
                                                h1tok[ch]])
                            et = plan.op("vector", "tensor_add",
                                         vslots[:, ch * 4 + lt,
                                                dv * 512:(dv + 1) * 512],
                                         bank.ap[:], bvrows[:, dv * 512:(dv + 1) * 512],
                                         waits=[tok, BVROWS])
                            bank.free_tok = et
                ring.release(lid, plan.tok("tensor") if j < 4 else plan.tok("vector"))
            plan.barrier()
            if q == 0:
                tap("zq", work[:, 16384:24576], [plan.tok("vector")])
                tap("kv", work[:, 0:16384], [plan.tok("vector")])

            qkv_done = plan.tok("tensor")
            o_mm_tok = [None, None]

            def emit_st(ch, lt):
                QTl = ch * 4 + lt
                QTg = q * 8 + QTl
                pairs_kt = [kt for kt in range(QTg - 2, QTg + 1) if kt >= 0]
                r = QTl % 2
                st_toks = []
                for pi, ktg in enumerate(pairs_kt):
                    if ktg >= q * 8:
                        kv_ = kslots[:, ktg - q * 8, :].rearrange(
                            "p (dc t) -> p dc t", t=128)
                    else:
                        kv_ = kh[:, ktg - (q * 8 - 2), :].rearrange(
                            "p (dc t) -> p dc t", t=128)
                    bank = takeA()
                    tok = mm_acc(bank.ap[:, 0:128],
                                 [(kv_[:, dc, :],
                                   zq[:, dc, ch * 512 + lt * 128:
                                      ch * 512 + (lt + 1) * 128])
                                  for dc in range(8)],
                                 waits=[bank.free_tok])
                    ewl = [tok, o_mm_tok[r]]
                    if ktg == QTg:
                        et = plan.op("vector", "tensor_mul", zst[r][:, pi, :],
                                     bank.ap[:, 0:128], maskC[:], waits=ewl)
                    elif ktg == QTg - 1:
                        et = plan.op("scalar", "activation", zst[r][:, pi, :],
                                     bank.ap[:, 0:128], AF.Copy, scale=SCALE,
                                     waits=ewl)
                    else:
                        et = plan.op("vector", "tensor_mul", zst[r][:, pi, :],
                                     bank.ap[:, 0:128], maskA[:], waits=ewl)
                    bank.free_tok = et
                    st_toks.append(et)
                return pairs_kt, r, st_toks

            def emit_o(ch, lt, pairs_kt, r, st_toks):
                ob = takeO()
                plan.wait("tensor", ob.waits() + st_toks)
                otok = plan._bump("tensorprog", 1)
                plan.rec("tensor",
                         reads=[vslots[:, :, :], vh[:, :, :], zst[r][:, :, :]],
                         writes=[ob.ap[:]], incs=[("tensorprog", otok[1])],
                         tag="o_mm")

                def go_o(t, _pairs=tuple(pairs_kt), _q=q, _r=r, _ap=ob.ap, _p=plan):
                    ins = None
                    for dc in range(8):
                        for pi, ktg in enumerate(_pairs):
                            if ktg >= _q * 8:
                                vv = vslots[:, ktg - _q * 8, :]
                            else:
                                vv = vh[:, ktg - (_q * 8 - 2), :]
                            ins = t.matmul(_ap[:, dc * 128:(dc + 1) * 128],
                                           vv[:, dc * 128:(dc + 1) * 128],
                                           zst[_r][:, pi, :],
                                           start=(pi == 0),
                                           stop=(pi == len(_pairs) - 1))
                    ins.then_inc(_p.handle("tensorprog"), 1)
                plan.ops["tensor"].append(go_o)
                o_mm_tok[r] = ("tensorprog", otok[1])
                et = plan.op("vector", "tensor_copy",
                             osb[:, :, lt * 128:(lt + 1) * 128],
                             ob.ap[:].rearrange("p (dc t) -> p dc t", t=128),
                             waits=[o_mm_tok[r]])
                ob.release(et)

            def attn_chunk(ch):
                pend = emit_st(ch, 0)
                for lt in range(4):
                    nxt = emit_st(ch, lt + 1) if lt < 3 else None
                    emit_o(ch, lt, *pend)
                    pend = nxt

            def attn_ln(ch):
                tsl = slice(ch * 512, (ch + 1) * 512)

                def src_o(fc):
                    return (osb[:, fc, :], False)

                def out_attn(fc, tile, tok, prev, _tsl=tsl):
                    u = plan.op("vector", "tensor_scalar", gtmp[:, :], tile,
                                dcols[:, DC_GW + fc:DC_GW + fc + 1],
                                dcols[:, DC_GB2 + fc:DC_GB2 + fc + 1],
                                ALU.mult, ALU.add, waits=[tok, prev, GATES])
                    return plan.op("vector", "tensor_tensor", xres[:, fc, _tsl],
                                   xres[:, fc, _tsl], gtmp[:, :], ALU.add,
                                   waits=[u])

                ln(512, slice(0, 512), src_o, out_attn,
                   pre_toks=[plan.tok("vector")])

            h2tok = [None, None]

            def ln2_chunk(ch):
                csl = slice(ch * 512, (ch + 1) * 512)

                def out_h2(fc, tile, tok, prev, _csl=csl):
                    return plan.op("vector", "tensor_scalar", hbuf[:, fc, _csl],
                                   tile,
                                   dcols[:, DC_CBPL + fc:DC_CBPL + fc + 1],
                                   modc[:, 24 + fc:24 + fc + 1],
                                   ALU.mult, ALU.add, waits=[tok, GATES, qkv_done])

                ln(512, csl, src_x, out_h2, pre_toks=[plan.tok("vector"), GATES])
                h2tok[ch] = plan.tok("vector")

            plan.mark("attn0")
            attn_chunk(0)
            if q == 0:
                tap("osb", work[:, 25600:29696], [plan.tok("vector")])
            plan.mark("attnln0")
            attn_ln(0)
            plan.mark("attn1")
            attn_chunk(1)
            plan.mark("ln2_0")
            ln2_chunk(0)
            plan.mark("attnln1")
            attn_ln(1)
            plan.mark("ln2_1")
            ln2_chunk(1)
            tap(f"x1_q{q}", xres[:, :, :], [plan.tok("vector")])
            tap(f"h2_q{q}", hbuf[:, :, :], [plan.tok("vector")])

            ttk = plan.tok("tensor")
            for i in range(2):
                plan.op("vector", "tensor_copy", kh[:, i, :], kslots[:, 6 + i, :],
                        waits=[ttk])
                plan.op("vector", "tensor_copy", vh[:, i, :], vslots[:, 6 + i, :])

            gate_prev = [plan.tok("vector")]

            def make_wb(e):
                ob = takeO()
                plan.wait("tensor", ob.waits() + [w32t, CONST])
                tok = plan._bump("tensorprog", 1)
                plan.rec("tensor", reads=[hot[:, :], w32[:, :]], writes=[ob.ap[:]],
                         incs=[("tensorprog", tok[1])], tag="wb_mm")

                def go(t, _e=e, _ap=ob.ap, _p=plan):
                    ins = None
                    for t8 in range(8):
                        qrow = t8 * 4 + _e
                        ins = t.matmul(_ap[:, t8 * 128:(t8 + 1) * 128],
                                       hot[:, qrow * 128:(qrow + 1) * 128], w32[:],
                                       start=True, stop=True)
                    ins.then_inc(_p.handle("tensorprog"), 1)
                plan.ops["tensor"].append(go)
                et = plan.op("vector", "tensor_copy", wbg[:], ob.ap[:],
                             waits=[("tensorprog", tok[1]), gate_prev[0]])
                ob.release(et)
                return et

            def gate_evac(bank_ap, fc, n, bias_col, wb_tok, mmtok):
                nsl = slice(n * 512, (n + 1) * 512)
                wl = [mmtok, wb_tok, gate_prev[0], GATES]
                if bias_col is not None:
                    g1 = plan.op("vector", "scalar_tensor_tensor", gtmp[:, :],
                                 bank_ap, bias_col, wbg[:, nsl],
                                 ALU.add, ALU.mult, waits=wl)
                else:
                    g1 = plan.op("vector", "tensor_mul", gtmp[:, :],
                                 bank_ap, wbg[:, nsl], waits=wl)
                g2 = plan.op("vector", "scalar_tensor_tensor", xres[:, fc, nsl],
                             gtmp[:, :], dcols[:, DC_GB + fc:DC_GB + fc + 1],
                             xres[:, fc, nsl], ALU.mult, ALU.add, waits=[g1])
                gate_prev[0] = g2
                return g2

            h_rhs = lambda kc, nsl: hbuf[:, kc, nsl]

            def dense_stage(w_ids, rhs_view, out_cb):
                toks = []
                for fc in range(FC):
                    for n in range(2):
                        nsl = slice(n * 512, (n + 1) * 512)
                        lid = w_ids[fc // 4]
                        off = (fc % 4) * 128
                        wv = ring.view(lid)
                        bank = takeA()
                        tok = mm_acc(bank.ap[:],
                                     [(wv[:, kc, off:off + 128], rhs_view(kc, nsl))
                                      for kc in range(8)],
                                     waits=[ring.use_tok(lid), bank.free_tok])
                        toks.append(out_cb(fc, n, bank, tok))
                return toks

            def evac_to(chain_view, bias_row):
                def cb(fc, n, bank, tok):
                    et = plan.op("scalar", "activation",
                                 chain_view[:, fc, n * 512:(n + 1) * 512],
                                 bank.ap[:], AF.Identity,
                                 bias=vecs[:, bias_row, fc:fc + 1], waits=[tok])
                    bank.free_tok = et
                    return et
                return cb

            def big_out(w_ids, rhs_fcview, nkc, out_cb, rhs_rec=None):
                assert len(w_ids) <= 2
                for fc in range(FC):
                    for n in range(2):
                        nsl = slice(n * 512, (n + 1) * 512)
                        bank = takeA()
                        plan.wait("tensor", [bank.free_tok] +
                                  [ring.use_tok(i) for i in w_ids])
                        tok = plan._bump("tensorprog", 1)
                        plan.rec("tensor",
                                 reads=[ring.view(i) for i in w_ids] +
                                       [rhs_rec if rhs_rec is not None
                                        else work[:, :]],
                                 writes=[bank.ap[:]],
                                 incs=[("tensorprog", tok[1])], tag="bigout_mm")

                        def go(t, _fc=fc, _nsl=nsl, _ap=bank.ap, _ids=tuple(w_ids),
                               _rv=rhs_fcview, _nkc=nkc, _p=plan):
                            ins = None
                            per = _nkc // len(_ids)
                            for seg, lid in enumerate(_ids):
                                wv = ring.view(lid)
                                for s in range(per):
                                    kk = seg * per + s
                                    ins = t.matmul(_ap,
                                                   wv[:, s, _fc * 128:(_fc + 1) * 128],
                                                   _rv(kk, _nsl),
                                                   start=(kk == 0), stop=(kk == _nkc - 1))
                            ins.then_inc(_p.handle("tensorprog"), 1)
                        plan.ops["tensor"].append(go)
                        out_cb(fc, n, bank, ("tensorprog", tok[1]))

            def rounds_out(w_dram, rhs_fcview, gate_cb):
                for r in range(2):
                    tb = bankA[0:4] + [takeO(), takeO()]
                    aps = [tb[i].ap for i in range(4)] + \
                          [tb[4].ap[:, 0:512], tb[4].ap[:, 512:1024],
                           tb[5].ap[:, 0:512], tb[5].ap[:, 512:1024]]
                    tiles = [(4 * r + i, n) for i in range(4) for n in range(2)]
                    base = plan.cnt["tensorprog"]
                    seg_ids = []
                    for s in range(4):
                        seg_ids.append(ring.plan_load(
                            w_src(w_dram, s * 1024, (s + 1) * 1024,
                                  r * 512, (r + 1) * 512), 8, 512))
                        ring.release(seg_ids[s], ("tensorprog", base + s + 1))
                    plan.wait("tensor",
                              sum((b.waits() for b in tb), []) +
                              [ring.use_tok(seg_ids[0]), plan.tok("vector"),
                               plan.tok("scalar")])
                    plan._bump("tensorprog", 4)
                    segw = [ring.use_tok(i) for i in seg_ids]
                    for seg in range(4):
                        if seg > 0:
                            plan.log.append(("wait", "tensor", ((segw[seg][0],
                                                                segw[seg][1]),)))
                        plan.rec("tensor",
                                 reads=[ring.view(seg_ids[seg]), work[:, :]],
                                 writes=list(aps),
                                 incs=[("tensorprog", base + seg + 1)],
                                 tag="rounds_mm")

                    def go(t, _tiles=tuple(tiles), _aps=tuple(aps),
                           _ids=tuple(seg_ids), _rv=rhs_fcview,
                           _segw=tuple(segw), _p=plan, _r=r):
                        for seg in range(4):
                            if seg > 0:
                                t.wait_ge(_p.handle(_segw[seg][0]), _segw[seg][1])
                            wv = ring.view(_ids[seg])
                            ins = None
                            for s in range(8):
                                kk = seg * 8 + s
                                for ti, (outf, n) in enumerate(_tiles):
                                    ins = t.matmul(
                                        _aps[ti],
                                        wv[:, s, (outf - 4 * _r) * 128:
                                           (outf - 4 * _r + 1) * 128],
                                        _rv(kk, slice(n * 512, (n + 1) * 512)),
                                        start=(kk == 0), stop=(kk == 31))
                            ins.then_inc(_p.handle("tensorprog"), 1)
                    plan.ops["tensor"].append(go)
                    done = ("tensorprog", base + 4)
                    gs = []
                    for ti, (outf, n) in enumerate(tiles):
                        gs.append(gate_cb(aps[ti], outf, n, done))
                    for ti in range(4):
                        tb[ti].release(gs[ti])
                    tb[4].release(gs[5])
                    tb[5].release(gs[7])


            u_rhs = lambda kk, nsl: uview[:, kk, nsl]

            u_toks = [None] * 3
            rwq = [None]

            def emit_router():
                rt_id = ring.plan_load(w_src(w_router, 0, D, 0, 4), 8, 4)
                rtv = ring.view(rt_id)
                bank = takeA()
                plan.wait("tensor", [bank.free_tok, ring.use_tok(rt_id), h2tok[1]])
                lt_tok = plan._bump("tensorprog", 1)
                plan.rec("tensor", reads=[hbuf[:, :, :], rtv[:, :, :]],
                         writes=[bank.ap[:, 0:32]],
                         incs=[("tensorprog", lt_tok[1])], tag="router_mm")

                def go_rt(t, _ap=bank.ap, _rtv=rtv, _p=plan):
                    ins = None
                    for t8 in range(8):
                        for kc in range(8):
                            ins = t.matmul(_ap[:, t8 * 4:(t8 + 1) * 4],
                                           hbuf[:, kc, t8 * 128:(t8 + 1) * 128],
                                           _rtv[:, kc, 0:4],
                                           start=(kc == 0), stop=(kc == 7))
                    ins.then_inc(_p.handle("tensorprog"), 1)
                plan.ops["tensor"].append(go_rt)
                ring.release(rt_id, ("tensorprog", lt_tok[1]))
                lg = plan.op("vector", "tensor_add", wtm[:], bank.ap[:, 0:32], rblr[:],
                             waits=[("tensorprog", lt_tok[1]), RBL])
                bank.free_tok = lg
                l_ = lambda e: wtm[:, e:32:4]
                ra = lambda i: rmath[:, i, :]
                ta = plan.op("vector", "tensor_tensor", ra(0), l_(0), l_(1), ALU.max,
                             waits=[lg])
                tb = plan.op("vector", "tensor_tensor", ra(1), l_(0), l_(1), ALU.min)
                tc_ = plan.op("vector", "tensor_tensor", ra(2), l_(2), l_(3), ALU.max)
                td = plan.op("vector", "tensor_tensor", ra(3), l_(2), l_(3), ALU.min)
                t4 = plan.op("vector", "tensor_tensor", ra(4), ra(0), ra(2), ALU.max,
                             waits=[ta, tc_])
                t5 = plan.op("vector", "tensor_tensor", ra(5), ra(0), ra(2), ALU.min)
                t6 = plan.op("vector", "tensor_tensor", ra(6), ra(1), ra(3), ALU.max,
                             waits=[tb, td])
                t7 = plan.op("vector", "tensor_tensor", ra(7), ra(5), ra(6), ALU.max,
                             waits=[t5, t6])
                t8_ = plan.op("vector", "tensor_tensor", ra(8), ra(7), ra(4), ALU.subtract,
                              waits=[t4, t7])
                e2_ = plan.op("scalar", "activation", ra(8), ra(8), AF.Exp, waits=[t8_])
                t9 = plan.op("vector", "tensor_scalar_add", ra(8), ra(8), 1.0, waits=[e2_])
                t10 = plan.op("vector", "reciprocal", ra(9), ra(8), waits=[t9])
                wq_toks = rwq[0] = []
                for e in range(4):
                    de = plan.op("vector", "tensor_tensor", ra(10), l_(e), ra(4),
                                 ALU.subtract, waits=[t4, t10] + wq_toks[-1:])
                    ee = plan.op("scalar", "activation", ra(10), ra(10), AF.Exp, waits=[de])
                    se = plan.op("vector", "tensor_tensor", ra(11), l_(e), ra(7),
                                 ALU.is_ge, waits=[t7])
                    m1_ = plan.op("vector", "tensor_tensor", ra(10), ra(10), ra(11),
                                  ALU.mult, waits=[ee, se])
                    wq_toks.append(plan.op("vector", "tensor_tensor", wq[:, e:32:4],
                                           ra(10), ra(9), ALU.mult, waits=[m1_]))
                if q == 0:
                    tap("wtm", wtm[:], [plan.tok("vector")])
                    tap("wq", wq[:], [plan.tok("vector")])

            plan.mark("e0u")
            for hf in range(32):
                if hf % 4 == 0:
                    j = hf // 4
                    lid1 = load_cols(sw_w1, j * 512, (j + 1) * 512)
                    lid3 = load_cols(sw_w3, j * 512, (j + 1) * 512)
                off = (hf % 4) * 128
                w1v, w3v = ring.view(lid1), ring.view(lid3)
                for n in range(2):
                    nsl = slice(n * 512, (n + 1) * 512)
                    b1, b3 = takeA(), takeA()
                    t1 = mm_acc(b1.ap[:],
                                [(w1v[:, kc, off:off + 128], hbuf[:, kc, nsl])
                                 for kc in range(8)],
                                waits=[ring.use_tok(lid1), b1.free_tok, h2tok[n]])
                    t3 = mm_acc(b3.ap[:],
                                [(w3v[:, kc, off:off + 128], hbuf[:, kc, nsl])
                                 for kc in range(8)],
                                waits=[ring.use_tok(lid3), b3.free_tok])
                    si = (2 * hf + n) % 3
                    ss = stmp[:, si, :]
                    st = plan.op("scalar", "activation", ss, b1.ap[:], AF.Silu,
                                 waits=[t1, u_toks[si]])
                    b1.free_tok = st
                    ut = plan.op("vector", "tensor_mul", uview[:, hf, nsl],
                                 ss, b3.ap[:], waits=[st, t3])
                    b3.free_tok = ut
                    u_toks[si] = ut
                if hf % 4 == 3:
                    ring.release(lid1, plan.tok("tensor"))
                    ring.release(lid3, plan.tok("tensor"))
                if hf == 3:
                    emit_router()
            wq_toks = rwq[0]

            bank = takeA()
            trt = plan.op("tensor", "transpose", bank.ap[0:32, 0:128], wq[:], ident[:],
                          waits=wq_toks + [bank.free_tok, CONST], inc="tensorprog")
            w32t = plan.op("vector", "tensor_copy", w32[:], bank.ap[0:32, 0:128],
                           waits=[trt])
            bank.free_tok = w32t
            wb_tok = make_wb(0)
            plan.barrier()
            if q == 0:
                tap("u0", work[:, 0:32768], [plan.tok("vector")])

            def gate_cb_e0(ap, outf, n, tok, _wb=wb_tok):
                return gate_evac(ap, outf, n, None, _wb, tok)

            plan.mark("e0out")
            rounds_out(sw_w2, u_rhs, gate_cb_e0)
            plan.barrier()

            plan.mark("e1")
            wb_tok = make_wb(1)
            c0, c1v, c2v = chain(0), chain(8192), chain(16384)
            gz = work[:, 0:16384].rearrange("p (fc t) -> p fc t", t=1024)
            eq_ids = [load_cols(me_eq, j * 512, (j + 1) * 512) for j in range(2)]
            dense_stage(eq_ids, h_rhs, evac_to(c2v, V_EQB))
            for i in eq_ids:
                ring.release(i, plan.tok("tensor"))
            plan.barrier()
            wv_ids = [load_cols(me_wv, j * 512, (j + 1) * 512) for j in range(2)]
            c2_rhs = lambda kc, nsl: c2v[:, kc, nsl]
            dense_stage(wv_ids, c2_rhs, evac_to(c1v, V_MEBV))
            for i in wv_ids:
                ring.release(i, plan.tok("tensor"))
            plan.barrier()
            wo_ids = [load_cols(me_wo, j * 512, (j + 1) * 512) for j in range(2)]
            c1_rhs = lambda kc, nsl: c1v[:, kc, nsl]
            dense_stage(wo_ids, c1_rhs, evac_to(c2v, V_MEBO))
            for i in wo_ids:
                ring.release(i, plan.tok("tensor"))
            plan.barrier()
            c1_ids = [load_cols(me_c1, j * 512, (j + 1) * 512) for j in range(4)]
            for fc in range(16):
                for n in range(2):
                    nsl = slice(n * 512, (n + 1) * 512)
                    lid = c1_ids[fc // 4]
                    off = (fc % 4) * 128
                    wv = ring.view(lid)
                    bank = takeA()
                    tok = mm_acc(bank.ap[:],
                                 [(wv[:, kc, off:off + 128], c2v[:, kc, nsl])
                                  for kc in range(8)],
                                 waits=[ring.use_tok(lid), bank.free_tok])
                    et = plan.op("scalar", "activation", gz[:, fc, nsl], bank.ap[:],
                                 AF.Gelu,
                                 bias=vecs[:, V_C1B0 + fc // 8, fc % 8:fc % 8 + 1],
                                 waits=[tok])
                    bank.free_tok = et
            for i in c1_ids:
                ring.release(i, plan.tok("tensor"))
            plan.barrier()
            c2_ids = [load_rows(me_c2, j * 1024, (j + 1) * 1024) for j in range(2)]
            gz_rhs = lambda kk, nsl: gz[:, kk, nsl]

            def cb_e1(fc, n, bank, tok, _wb=wb_tok):
                g = gate_evac(bank.ap[:], fc, n, vecs[:, V_C2B, fc:fc + 1], _wb, tok)
                bank.free_tok = g
                return g

            big_out(c2_ids, gz_rhs, 16, cb_e1, rhs_rec=work[:, 0:16384])
            for i in c2_ids:
                ring.release(i, plan.tok("vector"))
            plan.barrier()

            plan.mark("e2")
            wb_tok = make_wb(2)
            syn_ids = [load_cols(ce_syn, j * 512, (j + 1) * 512) for j in range(2)]
            dense_stage(syn_ids, h_rhs, evac_to(c0, V_SYNB))
            for i in syn_ids:
                ring.release(i, plan.tok("tensor"))
            plan.barrier()
            cewv_ids = [load_cols(ce_wv, j * 512, (j + 1) * 512) for j in range(2)]
            c0_rhs = lambda kc, nsl: c0[:, kc, nsl]
            dense_stage(cewv_ids, c0_rhs, evac_to(c1v, V_CEBV))
            for i in cewv_ids:
                ring.release(i, plan.tok("tensor"))
            plan.barrier()
            cewo_ids = [load_cols(ce_wo, j * 512, (j + 1) * 512) for j in range(2)]

            def cb_ssum(fc, n, bank, tok):
                nsl = slice(n * 512, (n + 1) * 512)
                et = plan.op("vector", "scalar_tensor_tensor", c2v[:, fc, nsl],
                             bank.ap[:], vecs[:, V_CEBO, fc:fc + 1], c0[:, fc, nsl],
                             ALU.add, ALU.add, waits=[tok])
                bank.free_tok = et
                return et

            dense_stage(cewo_ids, c1_rhs, cb_ssum)
            for i in cewo_ids:
                ring.release(i, plan.tok("tensor"))
            plan.barrier()

            def src_ssum(fc):
                return (c2v[:, fc, :], False)
            plan.mark("n1")
            h2a_tok = [None, None]
            for chnk in range(2):
                csl = slice(chnk * 512, (chnk + 1) * 512)
                ln(512, csl, src_ssum,
                   lambda fc, tile, tok, prev, _csl=csl:
                       plan.op("vector", "tensor_scalar", c0[:, fc, _csl], tile,
                               vecs[:, V_N1G, fc:fc + 1], vecs[:, V_N1B, fc:fc + 1],
                               ALU.mult, ALU.add, waits=[tok]),
                   pre_toks=[plan.tok("vector")])
                h2a_tok[chnk] = plan.tok("vector")
            gz2 = work[:, 8192:24576].rearrange("p (fc t) -> p fc t", t=1024)
            f1_ids = [load_cols(ce_f1, j * 512, (j + 1) * 512) for j in range(4)]
            for n in range(2):
                nsl = slice(n * 512, (n + 1) * 512)
                for fc in range(16):
                    lid = f1_ids[fc // 4]
                    off = (fc % 4) * 128
                    wv = ring.view(lid)
                    bank = takeA()
                    tok = mm_acc(bank.ap[:],
                                 [(wv[:, kc, off:off + 128], c0[:, kc, nsl])
                                  for kc in range(8)],
                                 waits=[ring.use_tok(lid), bank.free_tok,
                                        h2a_tok[n]])
                    et = plan.op("scalar", "activation", gz2[:, fc, nsl], bank.ap[:],
                                 AF.Relu,
                                 bias=vecs[:, V_F1B0 + fc // 8, fc % 8:fc % 8 + 1],
                                 waits=[tok])
                    bank.free_tok = et
            for i in f1_ids:
                ring.release(i, plan.tok("tensor"))
            plan.barrier()
            c3 = work[:, 24576:32768].rearrange("p (fc t) -> p fc t", t=1024)
            f2_ids = [load_rows(ce_f2, j * 1024, (j + 1) * 1024) for j in range(2)]
            gz2_rhs = lambda kk, nsl: gz2[:, kk, nsl]

            def cb_s2(fc, n, bank, tok):
                nsl = slice(n * 512, (n + 1) * 512)
                et = plan.op("vector", "scalar_tensor_tensor", c3[:, fc, nsl],
                             bank.ap[:], vecs[:, V_F2B, fc:fc + 1], c0[:, fc, nsl],
                             ALU.add, ALU.add, waits=[tok])
                bank.free_tok = et
                return et

            big_out(f2_ids, gz2_rhs, 16, cb_s2, rhs_rec=work[:, 8192:24576])
            for i in f2_ids:
                ring.release(i, plan.tok("vector"))
            plan.barrier()
            h2b = work[:, 8192:16384].rearrange("p (fc t) -> p fc t", t=1024)

            def src_s2(fc):
                return (c3[:, fc, :], False)

            plan.mark("n2")
            h2b_tok = [None, None]
            for chnk in range(2):
                csl = slice(chnk * 512, (chnk + 1) * 512)
                ln(512, csl, src_s2,
                   lambda fc, tile, tok, prev, _csl=csl:
                       plan.op("vector", "tensor_scalar", h2b[:, fc, _csl], tile,
                               vecs[:, V_N2G, fc:fc + 1], vecs[:, V_N2B, fc:fc + 1],
                               ALU.mult, ALU.add, waits=[tok]),
                   pre_toks=[plan.tok("vector")])
                h2b_tok[chnk] = plan.tok("vector")
            gen_ids = [load_cols(ce_gen, j * 512, (j + 1) * 512) for j in range(2)]
            for n in range(2):
                nsl = slice(n * 512, (n + 1) * 512)
                for fc in range(FC):
                    lid = gen_ids[fc // 4]
                    off = (fc % 4) * 128
                    wv = ring.view(lid)
                    bank = takeA()
                    tok = mm_acc(bank.ap[:],
                                 [(wv[:, kc, off:off + 128], h2b[:, kc, nsl])
                                  for kc in range(8)],
                                 waits=[ring.use_tok(lid), bank.free_tok,
                                        h2b_tok[n]])
                    g = gate_evac(bank.ap[:], fc, n, vecs[:, V_GENB, fc:fc + 1],
                                  wb_tok, tok)
                    bank.free_tok = g
            for i in gen_ids:
                ring.release(i, plan.tok("vector"))
            plan.barrier()

            plan.mark("e3")
            wb_tok = make_wb(3)
            for hf in range(32):
                if hf % 4 == 0:
                    lid = load_cols(ml_w1, (hf // 4) * 512, (hf // 4 + 1) * 512)
                off = (hf % 4) * 128
                wv = ring.view(lid)
                for n in range(2):
                    nsl = slice(n * 512, (n + 1) * 512)
                    bank = takeA()
                    tok = mm_acc(bank.ap[:],
                                 [(wv[:, kc, off:off + 128], hbuf[:, kc, nsl])
                                  for kc in range(8)],
                                 waits=[ring.use_tok(lid), bank.free_tok])
                    et = plan.op("scalar", "activation", uview[:, hf, nsl], bank.ap[:],
                                 AF.Gelu,
                                 bias=vecs[:, V_MLB1 + hf // 8, hf % 8:hf % 8 + 1],
                                 waits=[tok])
                    bank.free_tok = et
                if hf % 4 == 3:
                    ring.release(lid, plan.tok("scalar"))
            plan.barrier()

            def gate_cb_e3(ap, outf, n, tok, _wb=wb_tok):
                return gate_evac(ap, outf, n, vecs[:, V_MLB2, outf:outf + 1],
                                 _wb, tok)

            plan.mark("e3out")
            rounds_out(ml_w2, u_rhs, gate_cb_e3)
            xdone = plan.tok("vector")

            yt = plan.op("sync", "dma_start",
                         out=y_out[:, q * TQ:(q + 1) * TQ].rearrange(
                             "(fc p) t -> p fc t", p=128),
                         in_=xres[:, :, :],
                         waits=[xdone], inc="ystore", inc_k=16)
            prev_yhalf = [yt, yt]
            plan.barrier()

        tap("fin_modc", modc[:], [plan.tok("vector")])
        tap("fin_dcols", dcols[:], [plan.tok("vector")])
        tap("fin_rblr", rblr[:], [plan.tok("vector")])
        tap("fin_bvrows", bvrows[:], [plan.tok("vector")])
        tap("fin_maskA", maskA[:], [plan.tok("vector")])
        tap("fin_ones", ones[:], [plan.tok("vector")])
        tap("fin_vecs", vecs[:], [plan.tok("vector")])
        tap("fin_silc", silc[:], [plan.tok("vector")])
        tap("fin_epsc", epsc[:], [plan.tok("vector")])
        plan.wait("sync", [("ystore", plan.cnt["ystore"])] +
                  ([("tapst", plan.cnt["tapst"])] if "tapst" in plan.cnt else []))

        for n in sorted(plan.cnt.keys()):
            plan.handles[n] = es.enter_context(nc.semaphore(n))
        block = es.enter_context(nc.Block())
        plan.emit(block)

    LAST_MARKS[:] = plan.marks
    LAST_LOG[:] = plan.log
    return nc, din


_CACHE = {}


def _host_inputs(inputs):
    x = np.asarray(inputs["L0_attn_x"])
    c = np.asarray(inputs["L0_c"])
    vec_rows = [
        inputs["L0_attn_ln_w"], inputs["L0_attn_ln_b"],
        inputs["L0_ffn_me_eq_b"], inputs["L0_ffn_me_bv"], inputs["L0_ffn_me_bo"],
        inputs["L0_ffn_me_c2b"],
        inputs["L0_ffn_ce_syn_b"], inputs["L0_ffn_ce_bv"], inputs["L0_ffn_ce_bo"],
        inputs["L0_ffn_ce_n1g"], inputs["L0_ffn_ce_n1b"], inputs["L0_ffn_ce_f2b"],
        inputs["L0_ffn_ce_n2g"], inputs["L0_ffn_ce_n2b"], inputs["L0_ffn_ce_gen_b"],
        inputs["L0_ffn_ml_b2"],
        inputs["L0_attn_b_qkv"][0:1024], inputs["L0_attn_b_qkv"][1024:2048],
        inputs["L0_attn_b_qkv"][2048:3072],
        inputs["L0_ffn_me_c1b"][0:1024], inputs["L0_ffn_me_c1b"][1024:2048],
        inputs["L0_ffn_ce_f1b"][0:1024], inputs["L0_ffn_ce_f1b"][1024:2048],
        inputs["L0_ffn_ml_b1"][0:1024], inputs["L0_ffn_ml_b1"][1024:2048],
        inputs["L0_ffn_ml_b1"][2048:3072], inputs["L0_ffn_ml_b1"][3072:4096],
    ]
    vecs = np.stack([np.asarray(r, dtype=np.float32) for r in vec_rows])
    vecs_fm = np.ascontiguousarray(vecs.reshape(27, 8, 128).transpose(2, 0, 1))
    onesb = np.ones((128, 128), dtype=ml_dtypes.bfloat16)
    ident = np.eye(128, dtype=np.float32)
    p = np.arange(128)[:, None]
    qc = np.arange(128)[None, :]
    maskA = ((qc < p).astype(np.float32)) * np.float32(SCALE)
    maskC = ((qc >= p).astype(np.float32)) * np.float32(SCALE)
    hotrows = np.zeros((32, 4096), dtype=ml_dtypes.bfloat16)
    for qrow in range(32):
        hotrows[qrow, qrow * 128:(qrow + 1) * 128] = 1
    shared = dict(
        w_qkv=np.asarray(inputs["L0_attn_W_qkv"]),
        w_mod=np.asarray(inputs["L0_wmod"]),
        w_router=np.asarray(inputs["L0_ffn_router_w"]),
        rb=np.asarray(inputs["L0_ffn_router_b"]).reshape(1, 4),
        lb=np.asarray(inputs["L0_ffn_load_balancer"]).reshape(1, 4),
        sw_w1=np.asarray(inputs["L0_ffn_sw_w1"]),
        sw_w3=np.asarray(inputs["L0_ffn_sw_w3"]),
        sw_w2=np.asarray(inputs["L0_ffn_sw_w2"]),
        me_eq=np.asarray(inputs["L0_ffn_me_eq_w"]),
        me_wv=np.asarray(inputs["L0_ffn_me_wv"]),
        me_wo=np.asarray(inputs["L0_ffn_me_wo"]),
        me_c1=np.asarray(inputs["L0_ffn_me_c1w"]),
        me_c2=np.asarray(inputs["L0_ffn_me_c2w"]),
        ce_syn=np.asarray(inputs["L0_ffn_ce_syn_w"]),
        ce_wv=np.asarray(inputs["L0_ffn_ce_wv"]),
        ce_wo=np.asarray(inputs["L0_ffn_ce_wo"]),
        ce_f1=np.asarray(inputs["L0_ffn_ce_f1w"]),
        ce_f2=np.asarray(inputs["L0_ffn_ce_f2w"]),
        ce_gen=np.asarray(inputs["L0_ffn_ce_gen_w"]),
        ml_w1=np.asarray(inputs["L0_ffn_ml_w1"]),
        ml_w2=np.asarray(inputs["L0_ffn_ml_w2"]),
        vecs_fm=vecs_fm,
        bvrow=np.ascontiguousarray(
            np.asarray(inputs["L0_attn_b_qkv"][2048:3072])).reshape(1, 1024),
        ones_bf=onesb, ident=ident, maskA=maskA, maskC=maskC, hotrows=hotrows,
    )
    in_maps = []
    for b in range(B):
        m = dict(shared)
        m["x_fm"] = np.ascontiguousarray(np.asarray(x[b]).T)
        m["c_fm"] = np.ascontiguousarray(np.asarray(c[b]).reshape(8, 128).T)
        in_maps.append(m)
    return in_maps


def kernel(**inputs):
    if "nc" not in _CACHE:
        _CACHE["nc"] = build_nc()
    nc, _ = _CACHE["nc"]
    in_maps = _host_inputs(inputs)
    res = run_bass_kernel_spmd(nc, in_maps, core_ids=list(range(B)))
    out = np.stack([np.asarray(res.results[b]["y_fm"]).T for b in range(B)])
    return np.ascontiguousarray(out.astype(np.float32))
